# version 15
# baseline (speedup 1.0000x reference)
import contextlib
import numpy as np
import concourse.bass as bass
import concourse.mybir as mybir

F32 = mybir.dt.float32; BF16 = mybir.dt.bfloat16
I32 = mybir.dt.int32; U16 = mybir.dt.uint16; F16 = mybir.dt.float16
AOT = mybir.AluOpType; ACT = mybir.ActivationFunctionType

DM = 2048; HC = 16
T = 1024; NCH = 8; CL = 128
NH = 16; NKV = 4; HD = 128
IF = 5632; NIB = 44
GUSL = 1408; DWSL = 256
TP = 512
GUP = [4, 4, 3]
EPS_LN = 1e-5; EPS_ATT = 1e-6; EPS_RMS = 1e-6
CONSTS = [EPS_LN, EPS_ATT, -0.5, -1.0, 1.0 / 2048.0, 0.5, 2.0, 1.0]

FFN_ORDER = []
for _k in range(3):
    _lo = sum(GUP[:_k])
    for _r in range(4):
        for _j in range(GUP[_k]):
            FFN_ORDER.append((_k, _r, _lo + _j))

PROJ_SLICES = ([("wkT", c, 0) for c in (0, 256)] + [("wvT", c, 1) for c in (0, 256)]
               + [("wqT", c, 2) for c in range(0, 2048, 256)])


class Ctr:
    def __init__(self):
        self.v = 0


class E:
    def __init__(self, eng, sem, dma=False):
        self.e = eng; self.sem = sem; self.ctr = Ctr()
        self.last = None; self.dma = dma

    def w(self, sem, val):
        if val > 0:
            self.e.wait_ge(sem, val)

    def op(self, fn, *args, inc=None, ser=True, **kw):
        if ser and self.last is not None:
            self.e.wait_ge(*self.last)
        inst = fn(*args, **kw)
        step = 16 if self.dma else 1
        if inc is not None:
            sem, ctr = inc
            ctr.v += step
            inst.then_inc(sem, step)
            self.last = (sem, ctr.v)
        else:
            self.ctr.v += step
            inst.then_inc(self.sem, step)
            self.last = (self.sem, self.ctr.v)
        return inst


def build(sched, probe=None):
    marks = {}

    def mk(name, c):
        marks[name] = c.v

    def sv(name):
        return sched.get(name, 0)

    nc = bass.Bass("TRN2", target_bir_lowering=False)
    dram = lambda n, s, d, k="ExternalInput", **kw: nc.dram_tensor(n, s, d, kind=k, **kw).ap()

    xT = dram("xT", [DM, T], F32)
    w_in = {"wqT": dram("wqT", [DM, DM], F32), "wkT": dram("wkT", [DM, 512], F32),
            "wvT": dram("wvT", [DM, 512], F32)}
    woT = dram("woT", [DM, DM], F32)
    gup = dram("gup", [GUSL, 256], I32)
    dwp = dram("dwp", [DWSL, 704], I32)
    gus = dram("gus", [GUSL, 16], F16)
    dws = dram("dws", [DWSL, 44], F16)
    cosb = dram("cosb", [NCH, CL, HD], BF16)
    sinb = dram("sinb", [NCH, CL, HD], BF16)
    identb = dram("identb", [128, 128], BF16)
    tril4b = dram("tril4b", [128, 128], BF16)
    pmaskb = dram("pmaskb", [128, 8], F32)
    constsb = dram("constsb", [128, 8], F32)

    y = dram("y", [DM, T], F32, k="ExternalOutput")
    prb = dram("prb", [128, 16384], F32, k="ExternalOutput") if probe else None

    RG = [[0, 1, 2, 3, 4, 5, 6, 7]]
    ccs_i = dram("ccs_i", [128, 516], F32, k="Internal")
    ccs_o = dram("ccs_o", [128 * 8, 516], F32, k="Internal", addr_space="Shared")
    ccg_i = [dram(f"ccg_i{k}", [DM, GUP[k] * 128], BF16, k="Internal") for k in range(3)]
    ccg_o = [dram(f"ccg_o{k}", [DM * 8, GUP[k] * 128], BF16, k="Internal", addr_space="Shared")
             for k in range(3)]
    ccd_i = dram("ccd_i", [IF, DWSL], BF16, k="Internal")
    ccd_o = dram("ccd_o", [IF * 8, DWSL], BF16, k="Internal", addr_space="Shared")

    es = contextlib.ExitStack()
    with es:
        sb = lambda n, s, d: es.enter_context(nc.sbuf_tensor(n, s, d))
        psum = lambda n, s, d: es.enter_context(nc.psum_tensor(n, s, d))
        sem = lambda n: es.enter_context(nc.semaphore(n))

        x_res = sb("x_res", [128, HC * T], F32)
        lnho = sb("lnho", [128, HC * T], BF16)
        RA = sb("RA", [128, 16384], BF16)
        RB = sb("RB", [128, 12288], BF16)
        RC = sb("RC", [128, 8192], BF16)
        dqw = sb("dqw", [128, 1408], U16)
        dt1 = sb("dt1", [128, 512], U16)
        dt2 = sb("dt2", [128, 512], U16)
        dt3 = sb("dt3", [128, 512], U16)
        dt4 = sb("dt4", [128, 512], U16)
        dvals = sb("dvals", [128, 2816], U16)
        dstg = sb("dstg", [128, 2816], BF16)
        dscal = sb("dscal", [128, 44], F16)
        gsc = sb("gsc", [128, 16], F16)
        dscalf = sb("dscalf", [128, 44], F32)
        gscf = sb("gscf", [128, 16], F32)
        dbuf = sb("dbuf", [128, 2048], BF16)
        dbuf_u16 = None
        rtb = sb("rtb", [128, 4096], BF16)
        sq_t = sb("sq_t", [128, 1024], BF16)
        s_loc = None
        s_eff = sb("s_eff", [128, 1024], BF16)
        z_eff = sb("z_eff", [128, 16], BF16)
        z_loc = sb("z_loc", [128, NCH * 8], BF16)
        ident = sb("ident", [128, 128], BF16)
        trilm = sb("trilm", [128, 128], BF16)
        cos_s = sb("cos_s", [128, NCH * HD], BF16)
        sin_s = sb("sin_s", [128, NCH * HD], BF16)
        pmk = sb("pmk", [128, 8], F32)
        ones_c = sb("ones_c", [128, 1], BF16)
        ones_r = sb("ones_r", [1, 128], BF16)
        consts = sb("consts", [128, 8], F32)
        for _i, _val in enumerate(CONSTS):
            nc.const_aps.register(F32, float(_val), consts[:, _i:_i + 1])

        q_tm = RA
        kv_tm = RB
        s_loc = RB
        SLOC0 = 8192
        wbuf = RC
        dwb = RC

        def ffnw_ap(i):
            s = i % 3
            if s < 2:
                return RC[:, 4096 * s:4096 * (s + 1)]
            return RB[:, 8192:12288]

        dbuf_u16 = dbuf[:].bitcast(U16)

        def hact(gI):
            if gI < 32:
                return RA[:, TP * gI:TP * (gI + 1)]
            return RB[:, TP * (gI - 32):TP * (gI - 31)]

        stin = dstg[:].bitcast(F32)
        lnscr = stin[0:1, 0:512]
        dvbf = dvals[:].bitcast(BF16)
        a4m = dvbf[:, 0:1024]
        qf_fm = dvbf[:, 1024:2048]
        kf_fm = dvbf[:, 2048:2304]
        rdbc = dvbf[:, 2304:2816]
        rt = lambda i, n=1024: rtb[:, 1024 * i:1024 * i + n]
        s_glob = rtb[:].bitcast(F32)[:, 0:512]
        z_glob = rtb[:].bitcast(F32)[:, 512:520]
        s_cum = rtb[:].bitcast(F32)[:, 520:1032]
        z_cum = rtb[:].bitcast(F32)[:, 1032:1040]
        stats = rtb[:].bitcast(F32)
        mu_f = stats[0:1, 0:1024]
        var_f = stats[0:1, 1024:2048]
        rs_s = rtb[0:1, 2048:3072]
        mu_s = rtb[0:1, 3072:4096]
        dent = sq_t[:].bitcast(F32)[0:1, 0:512]
        rden = dstg[0:1, 2048:2560]

        pA = psum("pA", [128, 512], F32)
        pB = psum("pB", [128, 512], F32)
        pS = psum("pS", [128, 512], F32)
        pT2 = psum("pT2", [128, 512], F32)
        pZ = psum("pZ", [128, 512], F32)
        pC = psum("pC", [128, 512], F32)
        pE = psum("pE", [128, 512], F32)
        pT = psum("pT", [128, 512], F32)

        sd = sem("sd"); pk0 = sem("pk0"); pk1 = sem("pk1")
        wb0 = sem("wb0"); wb1 = sem("wb1")
        wos = [sem(f"wo{j}") for j in range(8)]
        gd = sem("gd"); st_0 = sem("st_0"); st_1 = sem("st_1")
        gf0 = sem("gf0"); gf1 = sem("gf1"); gf2 = sem("gf2")
        df0 = sem("df0"); df1 = sem("df1")
        ccsem = sem("ccsem"); vsem = sem("vsem"); ssem = sem("ssem"); tsem = sem("tsem")
        v2 = sem("v2"); s2 = sem("s2"); t2 = sem("t2")
        outs = sem("outs"); css = sem("css"); cst = sem("cst"); prd = sem("prd"); gp2 = sem("gp2")

        OUT = Ctr(); CSS = Ctr()
        block = es.enter_context(nc.Block())

        probe_src = {
            "xln1": (lnho, 16384), "qtm": (q_tm, 16384), "kvtm": (kv_tm, 8192),
            "headout": (lnho, 16384), "xmid": (x_res, 16384), "xln2": (lnho, 16384),
            "hact": (RA, 16384), "yb": (lnho, 16384), "sglob": (s_glob, 512),
            "stot": (s_cum, 512), "sloc": (RB, 12288), "deqg": (dstg, 2816),
        }

        @block.sync
        def _(sync):
            D = E(sync, sd, dma=True)
            D.op(sync.dma_start, out=x_res[:].rearrange("p (c t) -> p c t", c=HC),
                 in_=xT.rearrange("(c p) t -> p c t", p=128), ser=False)
            D.op(sync.dma_start, out=ident[:], in_=identb, ser=False)
            D.op(sync.dma_start, out=trilm[:], in_=tril4b, ser=False)
            D.op(sync.dma_start, out=cos_s[:].rearrange("p (c d) -> p c d", c=NCH),
                 in_=cosb.rearrange("c p d -> p c d"), ser=False)
            D.op(sync.dma_start, out=sin_s[:].rearrange("p (c d) -> p c d", c=NCH),
                 in_=sinb.rearrange("c p d -> p c d"), ser=False)
            D.op(sync.dma_start, out=pmk[:], in_=pmaskb, ser=False)
            D.op(sync.dma_start, out=consts[:], in_=constsb, ser=False)
            mk("sd.consts", D.ctr)
            P = [E(sync, pk0, dma=True), E(sync, pk1, dma=True)]
            S = E(sync, css, dma=True)

            def pk_load(t):
                PP = P[t % 2]
                PP.w(v2, sv(f"v.deqfree.{t}"))

                if t < 11:
                    PP.op(sync.dma_start, out=dqw[:, 0:512],
                          in_=gup[128 * t:128 * (t + 1), :].bitcast(U16), ser=False)
                    PP.op(sync.dma_start, out=gsc[:],
                          in_=gus[128 * t:128 * (t + 1), :], ser=False)
                else:
                    tt = t - 11
                    PP.op(sync.dma_start, out=dqw[:, 0:1408],
                          in_=dwp[128 * tt:128 * (tt + 1), :].bitcast(U16), ser=False)
                    PP.op(sync.dma_start, out=dscal[:],
                          in_=dws[128 * tt:128 * (tt + 1), :], ser=False)
                mk(f"sd.pk.{t}", PP.ctr)

            def cc_store(t):
                S.w(v2, sv(f"v.deqstg.{t}"))
                if t < 11:
                    k = min(t // 4, 2)
                    lc = (t - [0, 4, 8][k]) * 128
                    S.op(sync.dma_start,
                         out=ccg_i[k][:, lc:lc + 128].rearrange("(c p) o -> p c o", p=128),
                         in_=dstg[:, 0:2048].rearrange("p (c o) -> p c o", c=HC),
                         inc=(css, CSS), ser=False)
                    mk(f"sd.ccgst.{t}", CSS)
                else:
                    tt = t - 11
                    ot, hf = tt // 2, tt % 2
                    S.op(sync.dma_start,
                         out=ccd_i[2816 * hf:2816 * (hf + 1), 128 * ot:128 * (ot + 1)]
                             .rearrange("(c p) o -> p c o", p=128),
                         in_=dstg[:, 0:2816].rearrange("p (c o) -> p c o", c=22),
                         inc=(css, CSS), ser=False)
                    mk(f"sd.ccdst.{tt}", CSS)

            ST = E(sync, cst, dma=True)
            pk_load(0)
            pk_load(1)
            for t in range(2, 11):
                cc_store(t - 2)
                if t == 3:
                    ST.w(v2, sv("v.stot"))
                    ST.op(sync.dma_start, out=ccs_i[:, 0:512], in_=s_cum[:], ser=False)
                    ST.op(sync.dma_start, out=ccs_i[:, 512:516], in_=z_cum[:, 0:4], ser=False)
                    mk("sd.ccsst", ST.ctr)
                pk_load(t)
            cc_store(9)
            cc_store(10)
            TT = [E(sync, st_0, dma=True), E(sync, st_1, dma=True)]
            sync.wait_ge(css, sv("sd.ccgst.10"))
            sync.wait_ge(ccsem, 1)
            for r in range(8):
                Q = TT[r % 2]
                Q.w(v2, sv(f"v.stfree.{r}"))
                Q.op(sync.dma_start, out=stin[:, 516 * (r % 2):516 * (r % 2) + 516],
                     in_=ccs_o[128 * r:128 * (r + 1), :], ser=False)
                mk(f"sd.stin.{r}", Q.ctr)
            pk_load(11)
            G = [E(sync, gf0, dma=True), E(sync, gf1, dma=True), E(sync, gf2, dma=True)]
            F = [E(sync, df0, dma=True), E(sync, df1, dma=True)]
            for ps in range(2):
                for bi, (k, r, lb) in enumerate(FFN_ORDER):
                    i = ps * NIB + bi
                    GG = G[i % 3]
                    sync.wait_ge(ccsem, 2 + k)
                    if i == 0:
                        GG.w(t2, sv("t.wbfree.13"))
                    if i == 1:
                        GG.w(t2, sv("t.wo.15.1"))
                    if i == 2:
                        GG.w(t2, sv("t.rbc.31"))
                    if ps == 1 and bi in (1, 2):
                        GG.w(t2, sv("t.dn.15"))
                    GG.w(t2, sv(f"t.gffree.{i}"))
                    lc = (lb - [0, 4, 8][k]) * 128
                    fw = ffnw_ap(i)
                    GG.op(sync.dma_start,
                          out=fw[:, 0:DM].rearrange("p (c o) -> p c o", c=HC),
                          in_=ccg_o[k][DM * r:DM * (r + 1), lc:lc + 128]
                              .rearrange("(c p) o -> p c o", p=128), ser=False)
                    GG.op(sync.dma_start,
                          out=fw[:, DM:2 * DM].rearrange("p (c o) -> p c o", c=HC),
                          in_=ccg_o[k][DM * (4 + r):DM * (5 + r), lc:lc + 128]
                              .rearrange("(c p) o -> p c o", p=128), ser=False)
                    mk(f"sd.gf.{i}", GG.ctr)
                    if ps == 0:
                        if bi == 12:
                            pk_load(12)
                        if bi in (12, 19, 26, 33):
                            cc_store(11 + (12, 19, 26, 33).index(bi))
                sync.wait_ge(ccsem, 5)
                for t in range(HC):
                    i = ps * HC + t
                    r, lc = t // 2, (t % 2) * 128
                    for hx in range(2):
                        hidx = 2 * i + hx
                        FF = F[hidx % 2]
                        if t == 0:
                            FF.w(t2, sv(f"t.gu.{ps * NIB + NIB - 1}"))
                        FF.w(t2, sv(f"t.dnh.{hidx - 2}") if hidx >= 2 else 0)
                        so = (hidx % 2) * 2816
                        FF.op(sync.dma_start,
                              out=dwb[:, so:so + 2816].rearrange("p (c o) -> p c o", c=22),
                              in_=ccd_o[IF * r + 2816 * hx:IF * r + 2816 * (hx + 1),
                                        lc:lc + 128]
                                  .rearrange("(c p) o -> p c o", p=128), ser=False)
                        mk(f"sd.df.{hidx}", FF.ctr)
            O = E(sync, outs, dma=True)
            yv = y.rearrange("(c p) t -> p c t", p=128)
            for ps_ in range(2):
                for hc_ in range(HC):
                    O.w(v2, sv(f"v.res.{ps_}.{hc_}"))
                    O.op(sync.dma_start, out=yv[:, hc_, TP * ps_:TP * (ps_ + 1)],
                         in_=x_res[:, T * hc_ + TP * ps_:T * hc_ + TP * (ps_ + 1)],
                         inc=(outs, OUT), ser=False)
            sync.wait_ge(outs, OUT.v)

        @block.gpsimd
        def _(g):
            GD = E(g, gd, dma=True)
            GD.op(g.dma_start, out=lnho[:].rearrange("p (c t) -> p c t", c=HC),
                  in_=xT.rearrange("(c p) t -> p c t", p=128), ser=False)
            mk("gd.xb", GD.ctr)
            W = [E(g, wb0, dma=True), E(g, wb1, dma=True)]
            GP = E(g, gp2)
            g.wait_ge(sd, sv("sd.consts"))
            dprev = [0, []]

            def wb_load(sl):
                wname, c0, kind = PROJ_SLICES[sl]
                WW = W[sl % 2]
                WW.w(t2, sv(f"t.wbfree.{sl}"))
                so = (sl % 2) * HC * 256
                WW.op(g.dma_start, out=wbuf[:, so:so + HC * 256].rearrange("p (c o) -> p c o", c=HC),
                      in_=w_in[wname][:, c0:c0 + 256].rearrange("(c p) o -> p c o", p=128),
                      ser=False)
                mk(f"gd.wb.{sl}", WW.ctr)

            for sl in range(2):
                wb_load(sl)
            wb_load(2)
            wb_load(3)
            wb_load(4)
            wb_load(5)
            g.wait_ge(cst, sv("sd.ccsst"))
            g.collective_compute("AllGather", AOT.bypass, ins=[ccs_i], outs=[ccs_o],
                                 replica_groups=RG).then_inc(ccsem, 1)
            wb_load(6)
            wb_load(7)
            wb_load(8)
            wb_load(9)
            g.wait_ge(css, sv("sd.ccgst.3"))
            g.collective_compute("AllGather", AOT.bypass, ins=[ccg_i[0]], outs=[ccg_o[0]],
                                 replica_groups=RG).then_inc(ccsem, 1)
            wb_load(10)
            wb_load(11)
            g.wait_ge(css, sv("sd.ccgst.7"))
            g.collective_compute("AllGather", AOT.bypass, ins=[ccg_i[1]], outs=[ccg_o[1]],
                                 replica_groups=RG).then_inc(ccsem, 1)
            g.wait_ge(css, sv("sd.ccgst.10"))
            g.collective_compute("AllGather", AOT.bypass, ins=[ccg_i[2]], outs=[ccg_o[2]],
                                 replica_groups=RG).then_inc(ccsem, 1)
            V = [E(g, s_, dma=True) for s_ in wos]
            p1slot = [RC[:, 4096:6144], RC[:, 6144:8192], RA[:, 0:2048], RA[:, 2048:4096]]
            p1wait = ["t.wbfree.13", "t.wbfree.13", "t.scanT.0.3", "t.scanT.1.3"]
            for pw in range(2):
                for t in range(HC):
                    VV = V[4 * pw + t % 4]
                    if pw == 0:
                        VV.w(t2, sv(p1wait[t]) if t < 4 else sv(f"t.wo.{t - 4}.0"))
                        dst = p1slot[t % 4]
                    else:
                        VV.w(t2, sv(f"t.rbc.{8 * t + 7}") if t < 4 else sv(f"t.wo.{t - 4}.1"))
                        dst = RB[:, 2048 * (t % 4):2048 * (t % 4 + 1)]
                    VV.op(g.dma_start, out=dst.rearrange("p (c o) -> p c o", c=HC),
                          in_=woT[:, 128 * t:128 * (t + 1)].rearrange("(c p) o -> p c o", p=128),
                          ser=False)
                    mk(f"gd.wo{pw}.{t}", VV.ctr)
            g.wait_ge(css, sv("sd.ccdst.3"))
            g.collective_compute("AllGather", AOT.bypass, ins=[ccd_i], outs=[ccd_o],
                                 replica_groups=RG).then_inc(ccsem, 1)

        @block.scalar
        def _(s):
            S = E(s, ssem)
            S2 = Ctr()

            def sop(fn, *a, m=None, **kw):
                S.op(fn, *a, inc=(s2, S2) if m else None, **kw)
                if m:
                    mk(m, S2)

            def sprobe(names):
                if probe in names:
                    s.wait_ge(v2, sv("v.probe"))
                    psrc, ncols = probe_src[probe]
                    if psrc.dtype == F32:
                        s.dma_start(out=prb[:, 0:ncols],
                                    in_=psrc[:, 0:ncols]).then_inc(prd, 16)
                    else:
                        s.dma_start(out=prb[:].bitcast(BF16)[:, 0:ncols],
                                    in_=psrc[:, 0:ncols]).then_inc(prd, 16)
                    s.wait_ge(prd, 16)

            s.wait_ge(sd, sv("sd.consts"))
            s.wait_ge(gd, sv("gd.xb"))
            for hc in range(HC):
                if hc:
                    S.w(t2, sv(f"t.sqc0.{hc - 1}"))
                sop(s.activation, sq_t[:, 0:1024], lnho[:, T * hc:T * (hc + 1)], ACT.Square,
                    m=f"s.sq0.{hc}")
            S.w(v2, sv("v.var1"))
            sop(s.activation, dent, var_f[:, 0:512], ACT.Ln, bias=EPS_LN)
            sop(s.activation, rs_s[:, 0:512], dent, ACT.Exp, scale=-0.5)
            sop(s.activation, dent, var_f[:, 512:1024], ACT.Ln, bias=EPS_LN)
            sop(s.activation, rs_s[:, 512:1024], dent, ACT.Exp, scale=-0.5, m="s.rstd1")
            sprobe(("xln1",))
            def pev(sl):
                wname, c0, kind = PROJ_SLICES[sl]
                for ch in range(NCH):
                    S.w(t2, sv(f"t.proj.{sl}.{ch}"))
                    src = (pA, pB, pC, pE)[ch % 4]
                    if kind == 0:
                        dst = kv_tm[:, 1024 * ch + c0:1024 * ch + c0 + 256]
                    elif kind == 1:
                        dst = kv_tm[:, 1024 * ch + 512 + c0:1024 * ch + 512 + c0 + 256]
                    else:
                        dst = q_tm[:, DM * ch + c0:DM * ch + c0 + 256]
                    sop(s.copy, dst, src[:, 0:256], m=f"s.pev.{sl}.{ch}")

            for sl in range(4):
                pev(sl)
            for ch in range(NCH):
                S.w(v2, sv(f"v.ropek.{ch}"))
                sop(s.activation, rt(1, 512), rt(0, 512), ACT.Exp, m=f"s.expk.{ch}")
            sprobe(("kvtm", "deqg"))
            for sl in range(4, 12):
                pev(sl)
            for ch in range(2):
                for hf in range(2):
                    S.w(v2, sv(f"v.ropeq.{ch}.{hf}"))
                    sop(s.activation, rt(3), rt(2), ACT.Exp, m=f"s.expq.{ch}.{hf}")
            sprobe(("qtm", "stot", "sglob"))
            for ch in range(NCH):
                for kv in range(NKV):
                    S.w(t2, sv(f"t.den.{ch}.{kv}"))
                    sop(s.activation, dent, pE[0:1, :], ACT.Ln, bias=EPS_ATT)
                    sop(s.activation, rden, dent, ACT.Exp, scale=-1.0, m=f"s.rden.{ch}.{kv}")
                    if ch + 2 < NCH and kv in (1, 2):
                        S.w(v2, sv(f"v.ropeq.{ch + 2}.{kv - 1}"))
                        sop(s.activation, rt(3), rt(2), ACT.Exp,
                            m=f"s.expq.{ch + 2}.{kv - 1}")
            sprobe(("headout", "sloc", "xmid"))
            for hc in range(HC):
                S.w(v2, sv(f"v.ln2cast.{hc}"))
                if hc:
                    S.w(t2, sv(f"t.sqc2.{hc - 1}"))
                sop(s.activation, sq_t[:, 0:1024], lnho[:, T * hc:T * (hc + 1)], ACT.Square,
                    m=f"s.sq2.{hc}")
            S.w(v2, sv("v.var2"))
            sop(s.activation, dent, var_f[:, 0:512], ACT.Ln, bias=EPS_LN)
            sop(s.activation, rs_s[:, 0:512], dent, ACT.Exp, scale=-0.5)
            sop(s.activation, dent, var_f[:, 512:1024], ACT.Ln, bias=EPS_LN)
            sop(s.activation, rs_s[:, 512:1024], dent, ACT.Exp, scale=-0.5, m="s.rstd2")
            sprobe(("xln2",))
            for ps in range(2):
                for bi in range(NIB):
                    i = ps * NIB + bi
                    S.w(t2, sv(f"t.gu.{i}"))
                    if i:
                        S.w(v2, sv(f"v.hact.{i - 1}"))
                    gb = (pA, pC)[bi % 2]
                    sop(s.activation, sq_t[:, 0:TP], gb[:], ACT.Silu, m=f"s.silu.{i}")
                if ps == 0:
                    sprobe(("hact",))
                for t in range(HC):
                    i2 = ps * HC + t
                    S.w(t2, sv(f"t.dn.{i2}"))
                    if i2:
                        S.w(t2, sv(f"t.rmsstat.{i2 - 1}"))
                    db = (pA, pB, pC, pE)[t % 4]
                    sop(s.copy, lnho[:, T * t + TP * ps:T * t + TP * (ps + 1)], db[:])
                    sop(s.activation, sq_t[:, 0:TP], db[:], ACT.Square, m=f"s.dnev.{i2}")
                S.w(t2, sv(f"t.rmss.{ps}"))
                sop(s.activation, dent, pZ[0:1, :], ACT.Ln, scale=1.0 / DM, bias=EPS_RMS)
                sop(s.activation, rden, dent, ACT.Exp, scale=-0.5, m=f"s.rms.{ps}")
                S.w(t2, sv(f"t.rmsbc.{ps}"))
                if ps == 1:
                    sprobe(("yb",))

        @block.vector
        def _(v):
            V = E(v, vsem)
            V2 = Ctr()

            def vop(fn, *a, m=None, **kw):
                V.op(fn, *a, inc=(v2, V2) if m else None, **kw)
                if m:
                    mk(m, V2)

            def pmark(name):
                if V.last is not None:
                    v.wait_ge(*V.last)
                v.sem_inc(v2, 1)
                V2.v += 1
                mk(name, V2)
                V.last = (v2, V2.v)
                if name == "v.probe":
                    v.wait_ge(prd, 16)

            vop(v.memset, ones_c[:], 1.0)
            vop(v.memset, ones_r[:], 1.0, m="v.ones")

            v.wait_ge(t2, sv("t.ln1stats"))
            vop(v.tensor_scalar, mu_f[:, 0:512], pC[0:1, :], 1.0 / DM, None, AOT.mult)
            vop(v.tensor_scalar, mu_f[:, 512:1024], pE[0:1, :], 1.0 / DM, None, AOT.mult)
            vop(v.tensor_scalar, var_f[:, 0:512], pZ[0:1, :], 1.0 / DM, None, AOT.mult)
            vop(v.tensor_scalar, var_f[:, 512:1024], pS[0:1, :], 1.0 / DM, None, AOT.mult)
            vop(v.tensor_tensor, z_tot[:].bitcast(F32) if False else lnscr,
                mu_f[:, 0:512], mu_f[:, 0:512], AOT.mult)
            vop(v.tensor_tensor, var_f[:, 0:512], var_f[:, 0:512], lnscr,
                AOT.subtract)
            vop(v.tensor_tensor, lnscr, mu_f[:, 512:1024], mu_f[:, 512:1024],
                AOT.mult)
            vop(v.tensor_tensor, var_f[:, 512:1024], var_f[:, 512:1024], lnscr,
                AOT.subtract, m="v.var1")
            V.w(s2, sv("s.rstd1"))
            vop(v.tensor_tensor, mu_s[:], mu_f[:], rs_s[:], AOT.mult, m="v.murstd1")
            v.wait_ge(t2, sv("t.ln1bc"))
            v.wait_ge(sd, sv("sd.consts"))
            for hc in range(HC):
                for half in range(2):
                    rb = (pC, pE)[half]; mb = (pZ, pS)[half]
                    c0 = T * hc + 512 * half
                    vop(v.tensor_tensor, dstg[:, 0:512], x_res[:, c0:c0 + 512], rb[:], AOT.mult)
                    vop(v.tensor_tensor, lnho[:, c0:c0 + 512], dstg[:, 0:512], mb[:],
                        AOT.subtract)
            pmark("v.ln1done")
            if probe == "xln1":
                pmark("v.probe")

            vqbatch = [0]
            def deq_piece(nu16, src_off, scal_ap, ngrp, grp0, dv_off, dvb=None):
                if dvb is None:
                    dvb = dvals
                for j in range(4):
                    w = dqw[:, src_off:src_off + nu16]
                    vop(v.tensor_scalar, dt1[:, 0:nu16], w, 4 * j, 7,
                        AOT.logical_shift_right, AOT.bitwise_and)
                    if 12 - 4 * j >= 0:
                        vop(v.tensor_scalar, dt2[:, 0:nu16], w, 12 - 4 * j, 0x8000,
                            AOT.logical_shift_left, AOT.bitwise_and)
                    else:
                        vop(v.tensor_scalar, dt2[:, 0:nu16], w, 4 * j - 12, 0x8000,
                            AOT.logical_shift_right, AOT.bitwise_and)
                    vop(v.tensor_scalar, dt3[:, 0:nu16], dt1[:, 0:nu16], 64.0, 16128.0,
                        AOT.mult, AOT.add)
                    vop(v.tensor_scalar, dt4[:, 0:nu16], dt3[:, 0:nu16], 2.0, -16256.0,
                        AOT.mult, AOT.add)
                    vop(v.tensor_tensor, dt3[:, 0:nu16], dt3[:, 0:nu16], dt4[:, 0:nu16],
                        AOT.min)
                    vop(v.tensor_single_scalar, dt4[:, 0:nu16], dt1[:, 0:nu16], 1.0, AOT.min)
                    vop(v.tensor_tensor, dt3[:, 0:nu16], dt3[:, 0:nu16], dt4[:, 0:nu16],
                        AOT.mult)
                    vop(v.tensor_tensor, dvb[:, dv_off + j:dv_off + 4 * nu16:4],
                        dt3[:, 0:nu16], dt2[:, 0:nu16], AOT.add)
                dv = dvb[:, dv_off:dv_off + 4 * nu16].bitcast(BF16)
                for gi in range(ngrp):
                    vop(v.tensor_scalar, dv[:, 128 * gi:128 * (gi + 1)],
                        dv[:, 128 * gi:128 * (gi + 1)],
                        scal_ap[:, grp0 + gi:grp0 + gi + 1], None, AOT.mult)

            def deq_gate_tile(t):
                v.wait_ge((pk0, pk1)[t % 2], sv(f"sd.pk.{t}"))
                if t >= 1:
                    v.wait_ge(t2, sv(f"t.deqT.{t - 1}.1"))
                vop(v.tensor_copy, gscf[:], gsc[:])
                deq_piece(512, 0, gscf, 16, 0, 0)
                pmark(f"v.deqfree.{t + 1}")
                pmark(f"v.deqval.{t}")
                if t >= 1:
                    v.wait_ge(css, sv(f"sd.ccgst.{t - 1}"))
                for hcb in range(2):
                    v.wait_ge(t2, sv(f"t.deqT.{t}.{hcb}"))
                    bank = (pT2, pT)[vqbatch[0] % 2][:].bitcast(BF16)
                    vqbatch[0] += 1
                    vop(v.tensor_copy, dstg[:, 1024 * hcb:1024 * (hcb + 1)], bank[:],
                        m=f"v.deqTev.{t}.{hcb}")
                pmark(f"v.deqstg.{t}")

            def deq_down_map(tt, hf):
                t = 11 + tt
                dd = 2 * tt + hf
                if hf == 0:
                    v.wait_ge((pk0, pk1)[t % 2], sv(f"sd.pk.{t}"))
                if dd >= 1:
                    v.wait_ge(t2, sv(f"t.deqTd.{dd - 1}.2"))
                else:
                    v.wait_ge(t2, sv("t.deqT.10.1"))
                if hf == 0:
                    vop(v.tensor_copy, dscalf[:], dscal[:])
                for sub in range(2):
                    deq_piece(352, 704 * hf + 352 * sub, dscalf, 11, 22 * hf + 11 * sub,
                              1408 * sub)
                if hf == 1:
                    pmark(f"v.deqfree.{t + 1}")
                pmark(f"v.deqvald.{dd}")

            def deq_down_evac_b(dd, hcb):
                if hcb == 0:
                    v.wait_ge(css, sv(f"sd.ccdst.{dd - 1}") if dd >= 1 else sv("sd.ccgst.10"))
                nb = (8, 8, 6)[hcb]
                v.wait_ge(t2, sv(f"t.deqTd.{dd}.{hcb}"))
                bank = (pT2, pT)[vqbatch[0] % 2][:].bitcast(BF16)
                vqbatch[0] += 1
                vop(v.tensor_copy, dstg[:, 1024 * hcb:1024 * hcb + 128 * nb],
                    bank[:, 0:128 * nb], m=f"v.deqTevd.{dd}.{hcb}")
                if hcb == 2:
                    pmark(f"v.deqstg.{11 + dd}")

            deq_gate_tile(0)
            deq_gate_tile(1)

            def rope(dst_ap, n, ch, alt=False):
                hv = dst_ap.rearrange("p (h d) -> p h d", h=n)
                lo = hv[:, :, 0:64]; hi = hv[:, :, 64:128]
                cl = cos_s[:, 128 * ch:128 * ch + 64].rearrange("p d -> p () d").broadcast_to([128, n, 64])
                cn = cos_s[:, 128 * ch + 64:128 * (ch + 1)].rearrange("p d -> p () d").broadcast_to([128, n, 64])
                sl_ = sin_s[:, 128 * ch:128 * ch + 64].rearrange("p d -> p () d").broadcast_to([128, n, 64])
                sn = sin_s[:, 128 * ch + 64:128 * (ch + 1)].rearrange("p d -> p () d").broadcast_to([128, n, 64])
                m = n * 64
                b1 = dbuf[:, 0:m] if alt else rt(0, m)
                b2 = dbuf[:, 1024:1024 + m] if alt else rt(1, m)
                w1 = b1.rearrange("p (h d) -> p h d", h=n)
                w2 = b2.rearrange("p (h d) -> p h d", h=n)
                w3 = rt(2, m).rearrange("p (h d) -> p h d", h=n)
                w4 = rt(3, m).rearrange("p (h d) -> p h d", h=n)
                vop(v.tensor_tensor, w1, lo, cl, AOT.mult)
                vop(v.tensor_tensor, w2, hi, sl_, AOT.mult)
                vop(v.tensor_tensor, w3, hi, cn, AOT.mult)
                vop(v.tensor_tensor, w4, lo, sn, AOT.mult)
                vop(v.tensor_tensor, lo, w1, w2, AOT.subtract)
                vop(v.tensor_tensor, hi, w3, w4, AOT.add)

            for ch in range(NCH):
                V.w(s2, sv(f"s.pev.3.{ch}"))
                kap = kv_tm[:, 1024 * ch:1024 * ch + 512]
                rope(kap, 4, ch)
                vop(v.tensor_single_scalar, rt(0, 512), kap, 0.0, AOT.min, m=f"v.ropek.{ch}")
                V.w(s2, sv(f"s.expk.{ch}"))
                vop(v.scalar_tensor_tensor, kap, kap, 0.0, rt(1, 512), AOT.max, AOT.add,
                    m=f"v.eluk.{ch}")
            if probe == "kvtm":
                pmark("v.probe")

            vop(v.memset, s_cum[:], 0.0)
            vop(v.memset, z_cum[:], 0.0)
            for ch in range(NCH):
                v.wait_ge(t2, sv(f"t.stacc.{ch}"))
                vop(v.tensor_tensor, s_cum[:], s_cum[:], pS[:], AOT.add)
                vop(v.tensor_copy, s_loc[:, SLOC0 + 512 * ch:SLOC0 + 512 * (ch + 1)], s_cum[:])
                vop(v.tensor_tensor, z_cum[:, 0:4], z_cum[:, 0:4], pZ[:, 0:4], AOT.add)
                vop(v.tensor_copy, z_loc[:, 8 * ch:8 * ch + 4], z_cum[:, 0:4],
                    m=f"v.stev.{ch}")
            pmark("v.stot")

            for t in range(2, 11):
                deq_gate_tile(t)
            if probe == "deqg":
                pmark("v.probe")

            def q_rope(ch, alt):
                V.w(s2, sv(f"s.pev.11.{ch}"))
                rope(q_tm[:, DM * ch:DM * (ch + 1)], 16, ch, alt=alt)

            def q_elu(ch, hf):
                qh = q_tm[:, DM * ch + 1024 * hf:DM * ch + 1024 * (hf + 1)]
                vop(v.tensor_single_scalar, rt(2), qh, 0.0, AOT.min,
                    m=f"v.ropeq.{ch}.{hf}")
                V.w(s2, sv(f"s.expq.{ch}.{hf}"))
                vop(v.scalar_tensor_tensor, qh, qh, 0.0, rt(3), AOT.max, AOT.add,
                    m=f"v.eluq.{ch}.{hf}")

            v.wait_ge(cst, 32)
            for ch in range(2):
                q_rope(ch, False)
                q_elu(ch, 0)
                q_elu(ch, 1)
            if probe == "qtm":
                pmark("v.probe")

            vop(v.memset, s_glob[:], 0.0)
            vop(v.memset, z_glob[:], 0.0)
            for r in range(8):
                v.wait_ge((st_0, st_1)[r % 2], sv(f"sd.stin.{r}"))
                so = 516 * (r % 2)
                vop(v.scalar_tensor_tensor, s_glob[:], stin[:, so:so + 512],
                    pmk[:, r:r + 1], s_glob[:], AOT.mult, AOT.add)
                vop(v.scalar_tensor_tensor, z_glob[:, 0:4], stin[:, so + 512:so + 516],
                    pmk[:, r:r + 1], z_glob[:, 0:4], AOT.mult, AOT.add,
                    m=f"v.stfree.{r + 2}")
            pmark("v.sglob")
            if probe == "sglob":
                pmark("v.probe")
            if probe == "stot":
                pmark("v.probe")

            def v_seff(ch):
                se = s_eff[:, 512 * (ch % 2):512 * (ch % 2 + 1)]
                ze = z_eff[:, 8 * (ch % 2):8 * (ch % 2) + 4]
                if ch == 0:
                    vop(v.tensor_scalar, se, s_glob[:], 1.0, None, AOT.mult)
                    vop(v.tensor_scalar, ze, z_glob[:, 0:4], 1.0, None, AOT.mult,
                        m=f"v.seff.{ch}")
                else:
                    vop(v.tensor_tensor, se, s_glob[:],
                        s_loc[:, SLOC0 + 512 * (ch - 1):SLOC0 + 512 * ch], AOT.add)
                    vop(v.tensor_tensor, ze, z_glob[:, 0:4],
                        z_loc[:, 8 * (ch - 1):8 * (ch - 1) + 4], AOT.add, m=f"v.seff.{ch}")

            def v_qfkf_a4m(idx):
                ch, kv = idx // NKV, idx % NKV
                par = idx % 2
                if kv == 0:
                    v_seff(ch)
                pTx = (pT, pT2)[par][:].bitcast(BF16)
                v.wait_ge(t2, sv(f"t.scanT.{ch}.{kv}"))
                for j in range(4):
                    vop(v.tensor_copy,
                        qf_fm[:, 512 * par + 128 * j:512 * par + 128 * (j + 1)],
                        pTx[:, 128 * j:128 * (j + 1)])
                vop(v.tensor_copy, kf_fm[:, 128 * par:128 * (par + 1)], pTx[:, 512:640],
                    m=f"v.qfkf.{ch}.{kv}")
                v.wait_ge(t2, sv(f"t.a4.{ch}.{kv}"))
                vop(v.tensor_tensor,
                    a4m[:, 512 * par:512 * (par + 1)].rearrange("p (h t) -> p h t", h=4),
                    pA[:].rearrange("p (h t) -> p h t", h=4),
                    trilm[:].rearrange("p t -> p () t").broadcast_to([128, 4, 128]),
                    AOT.mult, m=f"v.a4m.{ch}.{kv}")

            def v_odiv(idx):
                ch, kv = idx // NKV, idx % NKV
                v.wait_ge(t2, sv(f"t.rbc.{idx}"))
                vop(v.tensor_copy, rdbc, pC[:])
                vop(v.tensor_tensor,
                    lnho[:].rearrange("p (h t) -> p h t", h=HC)[:, 4 * kv:4 * (kv + 1),
                                                                128 * ch:128 * (ch + 1)],
                    pB[:].rearrange("p (h t) -> p h t", h=4),
                    rdbc.rearrange("p (h t) -> p h t", h=4),
                    AOT.mult, m=f"v.odiv.{ch}.{kv}")

            def woev_h0(ot):
                v.wait_ge(t2, sv(f"t.wo.{ot}.0"))
                src = (pS, pZ)[ot % 2]
                c0 = T * ot
                vop(v.tensor_tensor, x_res[:, c0:c0 + 512], x_res[:, c0:c0 + 512],
                    src[:], AOT.add, m=f"v.woevh0.{ot}")

            v_qfkf_a4m(0)
            for idx in range(NCH * NKV):
                if idx + 1 < NCH * NKV:
                    v_qfkf_a4m(idx + 1)
                v_odiv(idx)
                if idx % 4 == 0 and 2 + idx // 4 < NCH:
                    q_rope(2 + idx // 4, True)
                if idx % 4 == 1 and 2 + idx // 4 < NCH:
                    q_elu(2 + idx // 4, 0)
                if idx % 4 == 2 and 2 + idx // 4 < NCH:
                    q_elu(2 + idx // 4, 1)
                if idx >= 17:
                    woev_h0(idx - 17)
            woev_h0(15)
            pmark("v.scandone")
            if probe == "headout":
                pmark("v.probe")
            if probe == "sloc":
                pmark("v.probe")

            for ot in range(HC):
                v.wait_ge(t2, sv(f"t.wo.{ot}.1"))
                src = (pA, pB)[ot % 2]
                c0 = T * ot + 512
                vop(v.tensor_tensor, x_res[:, c0:c0 + 512], x_res[:, c0:c0 + 512],
                    src[:], AOT.add, m=f"v.woev.{ot}.1")
            if probe == "xmid":
                pmark("v.probe")

            for hc in range(HC):
                vop(v.tensor_copy, lnho[:, T * hc:T * (hc + 1)], x_res[:, T * hc:T * (hc + 1)],
                    m=f"v.ln2cast.{hc}")
            v.wait_ge(t2, sv("t.ln2stats"))
            v.wait_ge(css, sv("sd.ccgst.10"))
            vop(v.tensor_scalar, mu_f[:, 0:512], pC[0:1, :], 1.0 / DM, None, AOT.mult)
            vop(v.tensor_scalar, mu_f[:, 512:1024], pE[0:1, :], 1.0 / DM, None, AOT.mult)
            vop(v.tensor_scalar, var_f[:, 0:512], pZ[0:1, :], 1.0 / DM, None, AOT.mult)
            vop(v.tensor_scalar, var_f[:, 512:1024], pS[0:1, :], 1.0 / DM, None, AOT.mult)
            vop(v.tensor_tensor, lnscr, mu_f[:, 0:512], mu_f[:, 0:512], AOT.mult)
            vop(v.tensor_tensor, var_f[:, 0:512], var_f[:, 0:512], lnscr,
                AOT.subtract)
            vop(v.tensor_tensor, lnscr, mu_f[:, 512:1024], mu_f[:, 512:1024],
                AOT.mult)
            vop(v.tensor_tensor, var_f[:, 512:1024], var_f[:, 512:1024], lnscr,
                AOT.subtract, m="v.var2")
            V.w(s2, sv("s.rstd2"))
            vop(v.tensor_tensor, mu_s[:], mu_f[:], rs_s[:], AOT.mult, m="v.murstd2")
            v.wait_ge(t2, sv("t.ln2bc"))
            for half in range(2):
                for hc in range(HC):
                    rb = (pC, pZ)[half]; mb = (pE, pS)[half]
                    c0 = T * hc + 512 * half
                    vop(v.tensor_tensor, dstg[:, 0:512], x_res[:, c0:c0 + 512], rb[:], AOT.mult)
                    vop(v.tensor_tensor, lnho[:, c0:c0 + 512], dstg[:, 0:512], mb[:],
                        AOT.subtract)
                if half == 0:
                    pmark("v.ln2done0")
            pmark("v.ln2done")
            if probe == "xln2":
                pmark("v.probe")

            for ps in range(2):
                for bi, (k, r, lb) in enumerate(FFN_ORDER):
                    i = ps * NIB + bi
                    gI = 11 * r + lb
                    V.w(s2, sv(f"s.silu.{i}"))
                    ub = (pB, pE)[bi % 2]
                    vop(v.tensor_tensor, hact(gI), sq_t[:, 0:TP], ub[:], AOT.mult,
                        m=f"v.hact.{i}")
                    if ps == 0:
                        if i == 2:
                            deq_down_map(0, 0)
                        if i in (7, 14, 21, 28):
                            dd = (7, 14, 21, 28).index(i)
                            deq_down_evac_b(dd, 0)
                            deq_down_evac_b(dd, 1)
                        if i in (9, 16, 23, 30):
                            dd = (9, 16, 23, 30).index(i)
                            deq_down_evac_b(dd, 2)
                            if dd < 3:
                                deq_down_map((dd + 1) // 2, (dd + 1) % 2)
                if probe == "hact" and ps == 0:
                    pmark("v.probe")
                v.wait_ge(t2, sv(f"t.rmsbc.{ps}"))
                for hc in range(HC):
                    c0 = T * hc + TP * ps
                    vop(v.tensor_tensor, dstg[:, 0:512], lnho[:, c0:c0 + 512], pS[:], AOT.mult)
                    vop(v.tensor_tensor, x_res[:, c0:c0 + 512], x_res[:, c0:c0 + 512],
                        dstg[:, 0:512], AOT.add, m=f"v.res.{ps}.{hc}")
                if probe == "yb" and ps == 1:
                    pmark("v.probe")
            pmark("v.final")

        @block.tensor
        def _(t):
            TE = E(t, tsem)
            T2 = Ctr()

            def top(fn, *a, m=None, **kw):
                TE.op(fn, *a, inc=(t2, T2) if m else None, ser=False, **kw)
                if m:
                    mk(m, T2)

            t.wait_ge(gd, sv("gd.xb"))
            t.wait_ge(sd, sv("sd.consts"))
            t.wait_ge(v2, sv("v.ones"))
            for hc in range(HC):
                t.wait_ge(s2, sv(f"s.sq0.{hc}"))
                st, sp = (hc == 0), (hc == HC - 1)
                top(t.matmul, pC[0:1, :], ones_c[:], lnho[:, T * hc:T * hc + 512],
                    start=st, stop=sp)
                top(t.matmul, pE[0:1, :], ones_c[:], lnho[:, T * hc + 512:T * (hc + 1)],
                    start=st, stop=sp)
                top(t.matmul, pZ[0:1, :], ones_c[:], sq_t[:, 0:512], start=st, stop=sp)
                top(t.matmul, pS[0:1, :], ones_c[:], sq_t[:, 512:1024], start=st, stop=sp,
                    m=f"t.sqc0.{hc}")
            mk("t.ln1stats", T2)
            t.wait_ge(v2, sv("v.murstd1"))
            top(t.matmul, pC[:], ones_r[:], rs_s[:, 0:512], start=True, stop=True)
            top(t.matmul, pE[:], ones_r[:], rs_s[:, 512:1024], start=True, stop=True)
            top(t.matmul, pZ[:], ones_r[:], mu_s[:, 0:512], start=True, stop=True)
            top(t.matmul, pS[:], ones_r[:], mu_s[:, 512:1024], start=True, stop=True,
                m="t.ln1bc")

            dqbatch = [0]
            dqevs = []

            def deq_T(ti):
                t.wait_ge(v2, sv(f"v.deqval.{ti}"))
                tsrc = dvals
                for hcb in range(2):
                    bank = (pT2, pT)[dqbatch[0] % 2][:].bitcast(BF16)
                    if dqbatch[0] >= 2:
                        t.wait_ge(v2, sv(dqevs[dqbatch[0] - 2]))
                    dqbatch[0] += 1
                    for qq in range(8):
                        hcx = 8 * hcb + qq
                        top(t.transpose, bank[:, 128 * qq:128 * (qq + 1)],
                            tsrc[:, 128 * hcx:128 * (hcx + 1)].bitcast(BF16), ident[:],
                            m=f"t.deqT.{ti}.{hcb}" if qq == 7 else None)
                    dqevs.append(f"v.deqTev.{ti}.{hcb}")

            def deq_Td_b(dd, hcb):
                if hcb == 0:
                    t.wait_ge(v2, sv(f"v.deqvald.{dd}"))
                nb = (8, 8, 6)[hcb]
                bank = (pT2, pT)[dqbatch[0] % 2][:].bitcast(BF16)
                if dqbatch[0] >= 2:
                    t.wait_ge(v2, sv(dqevs[dqbatch[0] - 2]))
                dqbatch[0] += 1
                for qq in range(nb):
                    icx = 8 * hcb + qq
                    top(t.transpose, bank[:, 128 * qq:128 * (qq + 1)],
                        dvals[:, 128 * icx:128 * (icx + 1)].bitcast(BF16), ident[:],
                        m=f"t.deqTd.{dd}.{hcb}" if qq == nb - 1 else None)
                dqevs.append(f"v.deqTevd.{dd}.{hcb}")

            def proj_slice(sl):
                t.wait_ge((wb0, wb1)[sl % 2], sv(f"gd.wb.{sl}"))
                t.wait_ge(v2, sv("v.ln1done"))
                so = (sl % 2) * HC * 256
                for ch in range(NCH):
                    idx = sl * NCH + ch
                    if idx >= 4:
                        pr = idx - 4
                        t.wait_ge(s2, sv(f"s.pev.{pr // NCH}.{pr % NCH}"))
                    dstp = (pA, pB, pC, pE)[ch % 4]
                    for hc in range(HC):
                        top(t.matmul, dstp[:, 0:256],
                            lnho[:, T * hc + 128 * ch:T * hc + 128 * (ch + 1)],
                            wbuf[:, so + 256 * hc:so + 256 * (hc + 1)],
                            start=(hc == 0), stop=(hc == HC - 1),
                            m=f"t.proj.{sl}.{ch}" if hc == HC - 1 else None)
                mk(f"t.wbfree.{sl + 2}", T2)

            for sl in range(4):
                proj_slice(sl)
            deq_T(0)
            deq_T(1)
            for ch in range(NCH):
                t.wait_ge(v2, sv(f"v.eluk.{ch}"))
                if ch > 0:
                    t.wait_ge(v2, sv(f"v.stev.{ch - 1}"))
                for kv in range(NKV):
                    st = (kv == 0)
                    sp = (kv == NKV - 1)
                    top(t.matmul, pS[:, 128 * kv:128 * (kv + 1)],
                        kv_tm[:, 1024 * ch + 128 * kv:1024 * ch + 128 * (kv + 1)],
                        kv_tm[:, 1024 * ch + 512 + 128 * kv:1024 * ch + 512 + 128 * (kv + 1)],
                        start=st, stop=(kv == NKV - 1))
                    top(t.matmul, pZ[:, kv:kv + 1],
                        kv_tm[:, 1024 * ch + 128 * kv:1024 * ch + 128 * (kv + 1)],
                        ones_c[:], start=st, stop=sp,
                        m=f"t.stacc.{ch}" if kv == NKV - 1 else None)
            for sl in range(4, 12):
                proj_slice(sl)
                if sl - 2 <= 10:
                    deq_T(sl - 2)
            deq_T(10)

            def scan_T_A4(idx):
                ch, kv = idx // NKV, idx % NKV
                par = idx % 2
                t.wait_ge(v2, sv(f"v.eluq.{ch}.{kv // 2}"))
                if idx >= 2:
                    t.wait_ge(v2, sv(f"v.qfkf.{(idx - 2) // NKV}.{(idx - 2) % NKV}"))
                pTx = (pT, pT2)[par][:].bitcast(BF16)
                for j in range(4):
                    h = 4 * kv + j
                    top(t.transpose, pTx[:, 128 * j:128 * (j + 1)],
                        q_tm[:, DM * ch + 128 * h:DM * ch + 128 * (h + 1)], ident[:])
                top(t.transpose, pTx[:, 512:640],
                    kv_tm[:, 1024 * ch + 128 * kv:1024 * ch + 128 * (kv + 1)], ident[:],
                    m=f"t.scanT.{ch}.{kv}")
                t.wait_ge(v2, sv(f"v.qfkf.{ch}.{kv}"))
                if idx >= 1:
                    t.wait_ge(v2, sv(f"v.a4m.{(idx - 1) // NKV}.{(idx - 1) % NKV}"))
                top(t.matmul, pA[:], kf_fm[:, 128 * par:128 * (par + 1)],
                    qf_fm[:, 512 * par:512 * (par + 1)], start=True, stop=True,
                    m=f"t.a4.{ch}.{kv}")

            def scan_rest(idx):
                ch, kv = idx // NKV, idx % NKV
                par = idx % 2
                t.wait_ge(v2, sv(f"v.a4m.{ch}.{kv}"))
                t.wait_ge(v2, sv(f"v.seff.{ch}"))
                if idx >= 1:
                    t.wait_ge(s2, sv(f"s.rden.{(idx - 1) // NKV}.{(idx - 1) % NKV}"))
                top(t.matmul, pE[0:1, :], ones_c[:], a4m[:, 512 * par:512 * (par + 1)],
                    start=True, stop=False)
                top(t.matmul, pE[0:1, :], z_eff[:, 8 * (ch % 2) + kv:8 * (ch % 2) + kv + 1],
                    qf_fm[:, 512 * par:512 * (par + 1)], start=False, stop=True,
                    m=f"t.den.{ch}.{kv}")
                if idx >= 1:
                    t.wait_ge(v2, sv(f"v.odiv.{(idx - 1) // NKV}.{(idx - 1) % NKV}"))
                top(t.matmul, pB[:],
                    kv_tm[:, 1024 * ch + 512 + 128 * kv:1024 * ch + 512 + 128 * (kv + 1)],
                    a4m[:, 512 * par:512 * (par + 1)], start=True, stop=False)
                top(t.matmul, pB[:], s_eff[:, 512 * (ch % 2) + 128 * kv:512 * (ch % 2) + 128 * (kv + 1)],
                    qf_fm[:, 512 * par:512 * (par + 1)], start=False, stop=True)
                t.wait_ge(s2, sv(f"s.rden.{ch}.{kv}"))
                if idx >= 1:
                    t.wait_ge(v2, sv(f"v.odiv.{(idx - 1) // NKV}.{(idx - 1) % NKV}"))
                top(t.matmul, pC[:], ones_r[:], rden, start=True, stop=True,
                    m=f"t.rbc.{idx}")

            wo_p1slot = [RC[:, 4096:6144], RC[:, 6144:8192], RA[:, 0:2048], RA[:, 2048:4096]]

            def wo_half(ot, half):
                t.wait_ge(wos[4 * half + ot % 4], sv(f"gd.wo{half}.{ot}"))
                if half == 0:
                    if ot == 0:
                        t.wait_ge(v2, sv("v.odiv.3.3"))
                    if ot >= 2:
                        t.wait_ge(v2, sv(f"v.woevh0.{ot - 2}"))
                    dstp = (pS, pZ)[ot % 2]
                    wsl = wo_p1slot[ot % 4]
                else:
                    if ot == 0:
                        t.wait_ge(v2, sv(f"v.odiv.{NCH - 1}.{NKV - 1}"))
                    if ot >= 2:
                        t.wait_ge(v2, sv(f"v.woev.{ot - 2}.1"))
                    dstp = (pA, pB)[ot % 2]
                    wsl = RB[:, 2048 * (ot % 4):2048 * (ot % 4 + 1)]
                for hp in range(HC):
                    top(t.matmul, dstp[:],
                        wsl[:, 128 * hp:128 * (hp + 1)],
                        lnho[:, T * hp + 512 * half:T * hp + 512 * (half + 1)],
                        start=(hp == 0), stop=(hp == HC - 1),
                        m=f"t.wo.{ot}.{half}" if hp == HC - 1 else None)

            scan_T_A4(0)
            for idx in range(NCH * NKV):
                if idx + 1 < NCH * NKV:
                    scan_T_A4(idx + 1)
                scan_rest(idx)
                if idx >= 16:
                    wo_half(idx - 16, 0)

            for ot in range(HC):
                wo_half(ot, 1)
            for hc in range(HC):
                t.wait_ge(s2, sv(f"s.sq2.{hc}"))
                st, sp = (hc == 0), (hc == HC - 1)
                top(t.matmul, pC[0:1, :], ones_c[:], lnho[:, T * hc:T * hc + 512],
                    start=st, stop=sp)
                top(t.matmul, pE[0:1, :], ones_c[:], lnho[:, T * hc + 512:T * (hc + 1)],
                    start=st, stop=sp)
                top(t.matmul, pZ[0:1, :], ones_c[:], sq_t[:, 0:512], start=st, stop=sp)
                top(t.matmul, pS[0:1, :], ones_c[:], sq_t[:, 512:1024], start=st, stop=sp,
                    m=f"t.sqc2.{hc}")
            mk("t.ln2stats", T2)
            t.wait_ge(v2, sv("v.murstd2"))
            top(t.matmul, pC[:], ones_r[:], rs_s[:, 0:512], start=True, stop=True)
            top(t.matmul, pZ[:], ones_r[:], rs_s[:, 512:1024], start=True, stop=True)
            top(t.matmul, pE[:], ones_r[:], mu_s[:, 0:512], start=True, stop=True)
            top(t.matmul, pS[:], ones_r[:], mu_s[:, 512:1024], start=True, stop=True,
                m="t.ln2bc")

            for ps in range(2):
                t.wait_ge(v2, sv("v.ln2done0" if ps == 0 else "v.ln2done"))
                for bi, (k, r, lb) in enumerate(FFN_ORDER):
                    i = ps * NIB + bi
                    t.wait_ge((gf0, gf1, gf2)[i % 3], sv(f"sd.gf.{i}"))
                    if i >= 2:
                        t.wait_ge(v2, sv(f"v.hact.{i - 2}"))
                    gb = (pA, pC)[bi % 2]; ub = (pB, pE)[bi % 2]
                    fw = ffnw_ap(i)
                    for hc in range(HC):
                        st, sp = (hc == 0), (hc == HC - 1)
                        top(t.matmul, gb[:], fw[:, 128 * hc:128 * (hc + 1)],
                            lnho[:, T * hc + TP * ps:T * hc + TP * (ps + 1)],
                            start=st, stop=sp)
                        top(t.matmul, ub[:],
                            fw[:, DM + 128 * hc:DM + 128 * (hc + 1)],
                            lnho[:, T * hc + TP * ps:T * hc + TP * (ps + 1)],
                            start=st, stop=sp, m=f"t.gu.{i}" if sp else None)
                    mk(f"t.gffree.{i + 3}", T2)
                    if ps == 0:
                        if bi in (6, 13, 20, 27):
                            dd = (6, 13, 20, 27).index(bi)
                            deq_Td_b(dd, 0)
                            deq_Td_b(dd, 1)
                        if bi in (8, 15, 22, 29):
                            deq_Td_b((8, 15, 22, 29).index(bi), 2)
                for tt_ in range(HC):
                    i2 = ps * HC + tt_
                    t.wait_ge(v2, sv(f"v.hact.{ps * NIB + NIB - 1}"))
                    if i2 >= 4:
                        t.wait_ge(s2, sv(f"s.dnev.{i2 - 4}"))
                    db = (pA, pB, pC, pE)[tt_ % 4]
                    for hx in range(2):
                        hidx = 2 * i2 + hx
                        t.wait_ge((df0, df1)[hidx % 2], sv(f"sd.df.{hidx}"))
                        so = (hidx % 2) * 2816
                        for icl in range(22):
                            ic = 22 * hx + icl
                            top(t.matmul, db[:], dwb[:, so + 128 * icl:so + 128 * (icl + 1)],
                                hact(ic), start=(ic == 0), stop=(ic == NIB - 1),
                                m=f"t.dnh.{hidx}" if icl == 21 else None)
                    mk(f"t.dn.{i2}", T2)
                    t.wait_ge(s2, sv(f"s.dnev.{i2}"))
                    top(t.matmul, pZ[0:1, :], ones_c[:], sq_t[:, 0:TP],
                        start=(tt_ == 0), stop=(tt_ == HC - 1), m=f"t.rmsstat.{i2}")
                mk(f"t.rmss.{ps}", T2)
                t.wait_ge(s2, sv(f"s.rms.{ps}"))
                top(t.matmul, pS[:], ones_r[:], rden, start=True, stop=True,
                    m=f"t.rmsbc.{ps}")

    return nc, marks


def build2(probe=None):
    _, marks = build({}, probe=probe)
    nc, _ = build(marks, probe=probe)
    return nc, marks


def _stage_inputs(L0_attn_x, L0_attn_Wq, L0_attn_Wk, L0_attn_Wv, L0_attn_Wo,
                  L0_ffn_gate_up_packed, L0_ffn_gate_up_scales,
                  L0_ffn_down_packed, L0_ffn_down_scales):
    import ml_dtypes
    bf = ml_dtypes.bfloat16
    x = np.ascontiguousarray(np.asarray(L0_attn_x, np.float32))
    wqT = np.ascontiguousarray(np.asarray(L0_attn_Wq, np.float32).T)
    wkT = np.ascontiguousarray(np.asarray(L0_attn_Wk, np.float32).T)
    wvT = np.ascontiguousarray(np.asarray(L0_attn_Wv, np.float32).T)
    woT = np.ascontiguousarray(np.asarray(L0_attn_Wo, np.float32).T)
    gupk = np.asarray(L0_ffn_gate_up_packed, np.int32)
    gusT = np.ascontiguousarray(np.asarray(L0_ffn_gate_up_scales, np.float16).T)
    dwpk = np.asarray(L0_ffn_down_packed, np.int32)
    dwsT = np.ascontiguousarray(np.asarray(L0_ffn_down_scales, np.float16).T)
    identb = np.eye(128).astype(bf)
    tril = np.triu(np.ones((128, 128), np.float32))
    tril4 = np.ascontiguousarray(tril).astype(bf)
    constsb = np.tile(np.array([1e-5, 1e-6, -0.5, -1.0, 1.0 / 2048.0, 0.5, 2.0, 1.0],
                               np.float32), (128, 1))
    in_maps = []
    for c in range(8):
        b, q = c // 4, c % 4
        xT = np.ascontiguousarray(x[b, 1024 * q:1024 * (q + 1), :].T)
        pos = 1024 * q + np.arange(1024, dtype=np.float64)
        invf = 10000.0 ** (-np.arange(0, 128, 2, dtype=np.float64) / 128.0)
        fr = np.outer(pos, invf)
        emb = np.concatenate([fr, fr], axis=1)
        cosb = np.cos(emb).astype(np.float32).astype(bf).reshape(8, 128, 128)
        sinb = np.sin(emb).astype(np.float32).astype(bf).reshape(8, 128, 128)
        pm = np.zeros((128, 8), np.float32)
        for r in range(8):
            if r // 4 == b and r % 4 < q:
                pm[:, r] = 1.0
        in_maps.append({
            "xT": xT, "wqT": wqT, "wkT": wkT, "wvT": wvT, "woT": woT,
            "gup": np.ascontiguousarray(gupk[1408 * c:1408 * (c + 1)]),
            "gus": np.ascontiguousarray(gusT[1408 * c:1408 * (c + 1)]),
            "dwp": np.ascontiguousarray(dwpk[256 * c:256 * (c + 1)]),
            "dws": np.ascontiguousarray(dwsT[256 * c:256 * (c + 1)]),
            "cosb": np.ascontiguousarray(cosb), "sinb": np.ascontiguousarray(sinb),
            "identb": identb, "tril4b": tril4, "pmaskb": pm, "constsb": constsb,
        })
    return in_maps


_NC_CACHE = {}


def kernel(**inputs):
    from concourse.bass_utils import run_bass_kernel_spmd
    if "nc" not in _NC_CACHE:
        _, marks = build({})
        nc, _ = build(marks)
        _NC_CACHE["nc"] = nc
    nc = _NC_CACHE["nc"]
    in_maps = _stage_inputs(**inputs)
    res = run_bass_kernel_spmd(nc, in_maps, core_ids=list(range(8)))
    out = np.zeros((2, 4096, 2048), np.float32)
    for c in range(8):
        b, q = c // 4, c % 4
        out[b, 1024 * q:1024 * (q + 1), :] = res.results[c]["y"].T
    return out


# version 16 (final)
# speedup vs baseline: 1.0226x; 1.0226x over previous
import contextlib
import numpy as np
import concourse.bass as bass
import concourse.mybir as mybir

F32 = mybir.dt.float32; BF16 = mybir.dt.bfloat16
I32 = mybir.dt.int32; U16 = mybir.dt.uint16; F16 = mybir.dt.float16
AOT = mybir.AluOpType; ACT = mybir.ActivationFunctionType

DM = 2048; HC = 16
T = 1024; NCH = 8; CL = 128
NH = 16; NKV = 4; HD = 128
IF = 5632; NIB = 44
GUSL = 1408; DWSL = 256
TP = 512
GUP = [4, 4, 3]
EPS_LN = 1e-5; EPS_ATT = 1e-6; EPS_RMS = 1e-6
CONSTS = [EPS_LN, EPS_ATT, -0.5, -1.0, 1.0 / 2048.0, 0.5, 2.0, 1.0]

FFN_ORDER = []
for _k in range(3):
    _lo = sum(GUP[:_k])
    for _r in range(4):
        for _j in range(GUP[_k]):
            FFN_ORDER.append((_k, _r, _lo + _j))

PROJ_SLICES = ([("wkT", c, 0) for c in (0, 256)] + [("wvT", c, 1) for c in (0, 256)]
               + [("wqT", c, 2) for c in range(0, 2048, 256)])


class Ctr:
    def __init__(self):
        self.v = 0


class E:
    def __init__(self, eng, sem, dma=False):
        self.e = eng; self.sem = sem; self.ctr = Ctr()
        self.last = None; self.dma = dma

    def w(self, sem, val):
        if val > 0:
            self.e.wait_ge(sem, val)

    def op(self, fn, *args, inc=None, ser=True, **kw):
        if ser and self.last is not None:
            self.e.wait_ge(*self.last)
        inst = fn(*args, **kw)
        step = 16 if self.dma else 1
        if inc is not None:
            sem, ctr = inc
            ctr.v += step
            inst.then_inc(sem, step)
            self.last = (sem, ctr.v)
        else:
            self.ctr.v += step
            inst.then_inc(self.sem, step)
            self.last = (self.sem, self.ctr.v)
        return inst


def build(sched, probe=None):
    marks = {}

    def mk(name, c):
        marks[name] = c.v

    def sv(name):
        return sched.get(name, 0)

    nc = bass.Bass("TRN2", target_bir_lowering=False)
    dram = lambda n, s, d, k="ExternalInput", **kw: nc.dram_tensor(n, s, d, kind=k, **kw).ap()

    xT = dram("xT", [DM, T], F32)
    w_in = {"wqT": dram("wqT", [DM, DM], F32), "wkT": dram("wkT", [DM, 512], F32),
            "wvT": dram("wvT", [DM, 512], F32)}
    woT = dram("woT", [DM, DM], F32)
    gup = dram("gup", [GUSL, 256], I32)
    dwp = dram("dwp", [DWSL, 704], I32)
    gus = dram("gus", [GUSL, 16], F16)
    dws = dram("dws", [DWSL, 44], F16)
    cosb = dram("cosb", [NCH, CL, HD], BF16)
    sinb = dram("sinb", [NCH, CL, HD], BF16)
    identb = dram("identb", [128, 128], BF16)
    tril4b = dram("tril4b", [128, 128], BF16)
    pmaskb = dram("pmaskb", [128, 8], F32)
    constsb = dram("constsb", [128, 8], F32)

    y = dram("y", [DM, T], F32, k="ExternalOutput")
    prb = dram("prb", [128, 16384], F32, k="ExternalOutput") if probe else None

    RG = [[0, 1, 2, 3, 4, 5, 6, 7]]
    ccs_i = dram("ccs_i", [128, 516], F32, k="Internal")
    ccs_o = dram("ccs_o", [128 * 8, 516], F32, k="Internal", addr_space="Shared")
    ccg_i = [dram(f"ccg_i{k}", [DM, GUP[k] * 128], BF16, k="Internal") for k in range(3)]
    ccg_o = [dram(f"ccg_o{k}", [DM * 8, GUP[k] * 128], BF16, k="Internal", addr_space="Shared")
             for k in range(3)]
    ccd_i2 = [dram(f"ccd_i{h}", [2816, DWSL], BF16, k="Internal") for h in range(2)]
    ccd_o2 = [dram(f"ccd_o{h}", [2816 * 8, DWSL], BF16, k="Internal", addr_space="Shared")
              for h in range(2)]

    es = contextlib.ExitStack()
    with es:
        sb = lambda n, s, d: es.enter_context(nc.sbuf_tensor(n, s, d))
        psum = lambda n, s, d: es.enter_context(nc.psum_tensor(n, s, d))
        sem = lambda n: es.enter_context(nc.semaphore(n))

        x_res = sb("x_res", [128, HC * T], F32)
        lnho = sb("lnho", [128, HC * T], BF16)
        RA = sb("RA", [128, 16384], BF16)
        RB = sb("RB", [128, 12288], BF16)
        RC = sb("RC", [128, 8192], BF16)
        dqw = sb("dqw", [128, 1408], U16)
        dt1 = sb("dt1", [128, 512], U16)
        dt2 = sb("dt2", [128, 512], U16)
        dt3 = sb("dt3", [128, 512], U16)
        dt4 = sb("dt4", [128, 512], U16)
        dvals = sb("dvals", [128, 2816], U16)
        dstg = sb("dstg", [128, 2816], BF16)
        dscal = sb("dscal", [128, 44], F16)
        gsc = sb("gsc", [128, 16], F16)
        dscalf = sb("dscalf", [128, 44], F32)
        gscf = sb("gscf", [128, 16], F32)
        dbuf = sb("dbuf", [128, 2048], BF16)
        dbuf_u16 = None
        rtb = sb("rtb", [128, 4096], BF16)
        sq_t = sb("sq_t", [128, 1024], BF16)
        s_loc = None
        s_eff = sb("s_eff", [128, 1024], BF16)
        z_eff = sb("z_eff", [128, 16], BF16)
        z_loc = sb("z_loc", [128, NCH * 8], BF16)
        ident = sb("ident", [128, 128], BF16)
        trilm = sb("trilm", [128, 128], BF16)
        cos_s = sb("cos_s", [128, NCH * HD], BF16)
        sin_s = sb("sin_s", [128, NCH * HD], BF16)
        pmk = sb("pmk", [128, 8], F32)
        ones_c = sb("ones_c", [128, 1], BF16)
        ones_r = sb("ones_r", [1, 128], BF16)
        consts = sb("consts", [128, 8], F32)
        for _i, _val in enumerate(CONSTS):
            nc.const_aps.register(F32, float(_val), consts[:, _i:_i + 1])

        q_tm = RA
        kv_tm = RB
        s_loc = RB
        SLOC0 = 8192
        wbuf = RC
        dwb = RC

        def ffnw_ap(i):
            s = i % 3
            if s < 2:
                return RC[:, 4096 * s:4096 * (s + 1)]
            return RB[:, 8192:12288]

        dbuf_u16 = dbuf[:].bitcast(U16)

        def hact(gI):
            if gI < 32:
                return RA[:, TP * gI:TP * (gI + 1)]
            return RB[:, TP * (gI - 32):TP * (gI - 31)]

        stin = dstg[:].bitcast(F32)
        lnscr = stin[0:1, 0:512]
        dvbf = dvals[:].bitcast(BF16)
        a4m = dvbf[:, 0:1024]
        qf_fm = dvbf[:, 1024:2048]
        kf_fm = dvbf[:, 2048:2304]
        rdbc = dvbf[:, 2304:2816]
        rt = lambda i, n=1024: rtb[:, 1024 * i:1024 * i + n]
        s_glob = rtb[:].bitcast(F32)[:, 0:512]
        z_glob = rtb[:].bitcast(F32)[:, 512:520]
        s_cum = rtb[:].bitcast(F32)[:, 520:1032]
        z_cum = rtb[:].bitcast(F32)[:, 1032:1040]
        stats = rtb[:].bitcast(F32)
        mu_f = stats[0:1, 0:1024]
        var_f = stats[0:1, 1024:2048]
        rs_s = rtb[0:1, 2048:3072]
        mu_s = rtb[0:1, 3072:4096]
        dent = sq_t[:].bitcast(F32)[0:1, 0:512]
        rden = dstg[0:1, 2048:2560]

        pA = psum("pA", [128, 512], F32)
        pB = psum("pB", [128, 512], F32)
        pS = psum("pS", [128, 512], F32)
        pT2 = psum("pT2", [128, 512], F32)
        pZ = psum("pZ", [128, 512], F32)
        pC = psum("pC", [128, 512], F32)
        pE = psum("pE", [128, 512], F32)
        pT = psum("pT", [128, 512], F32)

        sd = sem("sd"); pk0 = sem("pk0"); pk1 = sem("pk1")
        wb0 = sem("wb0"); wb1 = sem("wb1")
        wos = [sem(f"wo{j}") for j in range(8)]
        gd = sem("gd"); st_0 = sem("st_0"); st_1 = sem("st_1")
        gf0 = sem("gf0"); gf1 = sem("gf1"); gf2 = sem("gf2")
        df0 = sem("df0"); df1 = sem("df1")
        ccsem = sem("ccsem"); vsem = sem("vsem"); ssem = sem("ssem"); tsem = sem("tsem")
        v2 = sem("v2"); s2 = sem("s2"); t2 = sem("t2")
        outs = sem("outs"); css = sem("css"); cst = sem("cst"); prd = sem("prd"); gp2 = sem("gp2")

        OUT = Ctr(); CSS = Ctr()
        block = es.enter_context(nc.Block())

        probe_src = {
            "xln1": (lnho, 16384), "qtm": (q_tm, 16384), "kvtm": (kv_tm, 8192),
            "headout": (lnho, 16384), "xmid": (x_res, 16384), "xln2": (lnho, 16384),
            "hact": (RA, 16384), "yb": (lnho, 16384), "sglob": (s_glob, 512),
            "stot": (s_cum, 512), "sloc": (RB, 12288), "deqg": (dstg, 2816),
        }

        @block.sync
        def _(sync):
            D = E(sync, sd, dma=True)
            D.op(sync.dma_start, out=x_res[:].rearrange("p (c t) -> p c t", c=HC),
                 in_=xT.rearrange("(c p) t -> p c t", p=128), ser=False)
            D.op(sync.dma_start, out=ident[:], in_=identb, ser=False)
            D.op(sync.dma_start, out=trilm[:], in_=tril4b, ser=False)
            D.op(sync.dma_start, out=cos_s[:].rearrange("p (c d) -> p c d", c=NCH),
                 in_=cosb.rearrange("c p d -> p c d"), ser=False)
            D.op(sync.dma_start, out=sin_s[:].rearrange("p (c d) -> p c d", c=NCH),
                 in_=sinb.rearrange("c p d -> p c d"), ser=False)
            D.op(sync.dma_start, out=pmk[:], in_=pmaskb, ser=False)
            D.op(sync.dma_start, out=consts[:], in_=constsb, ser=False)
            mk("sd.consts", D.ctr)
            P = [E(sync, pk0, dma=True), E(sync, pk1, dma=True)]
            S = E(sync, css, dma=True)

            def pk_load(t):
                PP = P[t % 2]
                PP.w(v2, sv(f"v.deqfree.{t}"))

                if t < 11:
                    PP.op(sync.dma_start, out=dqw[:, 0:512],
                          in_=gup[128 * t:128 * (t + 1), :].bitcast(U16), ser=False)
                    PP.op(sync.dma_start, out=gsc[:],
                          in_=gus[128 * t:128 * (t + 1), :], ser=False)
                else:
                    tt = t - 11
                    PP.op(sync.dma_start, out=dqw[:, 0:1408],
                          in_=dwp[128 * tt:128 * (tt + 1), :].bitcast(U16), ser=False)
                    PP.op(sync.dma_start, out=dscal[:],
                          in_=dws[128 * tt:128 * (tt + 1), :], ser=False)
                mk(f"sd.pk.{t}", PP.ctr)

            def cc_store(t):
                S.w(v2, sv(f"v.deqstg.{t}"))
                if t < 11:
                    k = min(t // 4, 2)
                    lc = (t - [0, 4, 8][k]) * 128
                    S.op(sync.dma_start,
                         out=ccg_i[k][:, lc:lc + 128].rearrange("(c p) o -> p c o", p=128),
                         in_=dstg[:, 0:2048].rearrange("p (c o) -> p c o", c=HC),
                         inc=(css, CSS), ser=False)
                    mk(f"sd.ccgst.{t}", CSS)
                else:
                    tt = t - 11
                    ot, hf = tt // 2, tt % 2
                    S.op(sync.dma_start,
                         out=ccd_i2[hf][:, 128 * ot:128 * (ot + 1)]
                             .rearrange("(c p) o -> p c o", p=128),
                         in_=dstg[:, 0:2816].rearrange("p (c o) -> p c o", c=22),
                         inc=(css, CSS), ser=False)
                    mk(f"sd.ccdst.{tt}", CSS)

            ST = E(sync, cst, dma=True)
            pk_load(0)
            pk_load(1)
            for t in range(2, 11):
                cc_store(t - 2)
                if t == 3:
                    ST.w(v2, sv("v.stot"))
                    ST.op(sync.dma_start, out=ccs_i[:, 0:512], in_=s_cum[:], ser=False)
                    ST.op(sync.dma_start, out=ccs_i[:, 512:516], in_=z_cum[:, 0:4], ser=False)
                    mk("sd.ccsst", ST.ctr)
                pk_load(t)
            cc_store(9)
            cc_store(10)
            TT = [E(sync, st_0, dma=True), E(sync, st_1, dma=True)]
            sync.wait_ge(css, sv("sd.ccgst.10"))
            sync.wait_ge(ccsem, 1)
            for r in range(8):
                Q = TT[r % 2]
                Q.w(v2, sv(f"v.stfree.{r}"))
                Q.op(sync.dma_start, out=stin[:, 516 * (r % 2):516 * (r % 2) + 516],
                     in_=ccs_o[128 * r:128 * (r + 1), :], ser=False)
                mk(f"sd.stin.{r}", Q.ctr)
            pk_load(11)
            G = [E(sync, gf0, dma=True), E(sync, gf1, dma=True), E(sync, gf2, dma=True)]
            F = [E(sync, df0, dma=True), E(sync, df1, dma=True)]
            for ps in range(2):
                for bi, (k, r, lb) in enumerate(FFN_ORDER):
                    i = ps * NIB + bi
                    GG = G[i % 3]
                    sync.wait_ge(ccsem, 2 + k)
                    if i == 0:
                        GG.w(t2, sv("t.wbfree.13"))
                    if i == 1:
                        GG.w(t2, sv("t.wo.15.1"))
                    if i == 2:
                        GG.w(t2, sv("t.rbc.31"))
                    if ps == 1 and bi in (1, 2):
                        GG.w(t2, sv("t.dn.15"))
                    GG.w(t2, sv(f"t.gffree.{i}"))
                    lc = (lb - [0, 4, 8][k]) * 128
                    fw = ffnw_ap(i)
                    GG.op(sync.dma_start,
                          out=fw[:, 0:DM].rearrange("p (c o) -> p c o", c=HC),
                          in_=ccg_o[k][DM * r:DM * (r + 1), lc:lc + 128]
                              .rearrange("(c p) o -> p c o", p=128), ser=False)
                    GG.op(sync.dma_start,
                          out=fw[:, DM:2 * DM].rearrange("p (c o) -> p c o", c=HC),
                          in_=ccg_o[k][DM * (4 + r):DM * (5 + r), lc:lc + 128]
                              .rearrange("(c p) o -> p c o", p=128), ser=False)
                    mk(f"sd.gf.{i}", GG.ctr)
                    if ps == 0:
                        if bi == 12:
                            pk_load(12)
                        if bi in (12, 19, 26, 33):
                            cc_store(11 + (12, 19, 26, 33).index(bi))
                sync.wait_ge(ccsem, 5)
                for t in range(HC):
                    i = ps * HC + t
                    r, lc = t // 2, (t % 2) * 128
                    for hx in range(2):
                        hidx = 2 * i + hx
                        FF = F[hidx % 2]
                        if t == 0:
                            FF.w(t2, sv(f"t.gu.{ps * NIB + NIB - 1}"))
                        FF.w(t2, sv(f"t.dnh.{hidx - 2}") if hidx >= 2 else 0)
                        FF.w(ccsem, 5 + hx)
                        so = (hidx % 2) * 2816
                        FF.op(sync.dma_start,
                              out=dwb[:, so:so + 2816].rearrange("p (c o) -> p c o", c=22),
                              in_=ccd_o2[hx][2816 * r:2816 * (r + 1), lc:lc + 128]
                                  .rearrange("(c p) o -> p c o", p=128), ser=False)
                        mk(f"sd.df.{hidx}", FF.ctr)
            O = E(sync, outs, dma=True)
            yv = y.rearrange("(c p) t -> p c t", p=128)
            for ps_ in range(2):
                for hc_ in range(HC):
                    O.w(v2, sv(f"v.res.{ps_}.{hc_}"))
                    O.op(sync.dma_start, out=yv[:, hc_, TP * ps_:TP * (ps_ + 1)],
                         in_=x_res[:, T * hc_ + TP * ps_:T * hc_ + TP * (ps_ + 1)],
                         inc=(outs, OUT), ser=False)
            sync.wait_ge(outs, OUT.v)

        @block.gpsimd
        def _(g):
            GD = E(g, gd, dma=True)
            GD.op(g.dma_start, out=lnho[:].rearrange("p (c t) -> p c t", c=HC),
                  in_=xT.rearrange("(c p) t -> p c t", p=128), ser=False)
            mk("gd.xb", GD.ctr)
            W = [E(g, wb0, dma=True), E(g, wb1, dma=True)]
            GP = E(g, gp2)
            g.wait_ge(sd, sv("sd.consts"))
            dprev = [0, []]

            def wb_load(sl):
                wname, c0, kind = PROJ_SLICES[sl]
                WW = W[sl % 2]
                WW.w(t2, sv(f"t.wbfree.{sl}"))
                so = (sl % 2) * HC * 256
                WW.op(g.dma_start, out=wbuf[:, so:so + HC * 256].rearrange("p (c o) -> p c o", c=HC),
                      in_=w_in[wname][:, c0:c0 + 256].rearrange("(c p) o -> p c o", p=128),
                      ser=False)
                mk(f"gd.wb.{sl}", WW.ctr)

            for sl in range(2):
                wb_load(sl)
            wb_load(2)
            wb_load(3)
            wb_load(4)
            wb_load(5)
            g.wait_ge(cst, sv("sd.ccsst"))
            g.collective_compute("AllGather", AOT.bypass, ins=[ccs_i], outs=[ccs_o],
                                 replica_groups=RG).then_inc(ccsem, 1)
            wb_load(6)
            wb_load(7)
            wb_load(8)
            wb_load(9)
            g.wait_ge(css, sv("sd.ccgst.3"))
            g.collective_compute("AllGather", AOT.bypass, ins=[ccg_i[0]], outs=[ccg_o[0]],
                                 replica_groups=RG).then_inc(ccsem, 1)
            wb_load(10)
            wb_load(11)
            g.wait_ge(css, sv("sd.ccgst.7"))
            g.collective_compute("AllGather", AOT.bypass, ins=[ccg_i[1]], outs=[ccg_o[1]],
                                 replica_groups=RG).then_inc(ccsem, 1)
            g.wait_ge(css, sv("sd.ccgst.10"))
            g.collective_compute("AllGather", AOT.bypass, ins=[ccg_i[2]], outs=[ccg_o[2]],
                                 replica_groups=RG).then_inc(ccsem, 1)
            V = [E(g, s_, dma=True) for s_ in wos]
            p1slot = [RC[:, 4096:6144], RC[:, 6144:8192], RA[:, 0:2048], RA[:, 2048:4096]]
            p1wait = ["t.wbfree.13", "t.wbfree.13", "t.scanT.0.3", "t.scanT.1.3"]
            for pw in range(2):
                for t in range(HC):
                    VV = V[4 * pw + t % 4]
                    if pw == 0:
                        VV.w(t2, sv(p1wait[t]) if t < 4 else sv(f"t.wo.{t - 4}.0"))
                        dst = p1slot[t % 4]
                    else:
                        VV.w(t2, sv(f"t.rbc.{8 * t + 7}") if t < 4 else sv(f"t.wo.{t - 4}.1"))
                        dst = RB[:, 2048 * (t % 4):2048 * (t % 4 + 1)]
                    VV.op(g.dma_start, out=dst.rearrange("p (c o) -> p c o", c=HC),
                          in_=woT[:, 128 * t:128 * (t + 1)].rearrange("(c p) o -> p c o", p=128),
                          ser=False)
                    mk(f"gd.wo{pw}.{t}", VV.ctr)
            g.wait_ge(css, sv("sd.ccdst.2"))
            g.collective_compute("AllGather", AOT.bypass, ins=[ccd_i2[0]], outs=[ccd_o2[0]],
                                 replica_groups=RG).then_inc(ccsem, 1)
            g.wait_ge(css, sv("sd.ccdst.3"))
            g.collective_compute("AllGather", AOT.bypass, ins=[ccd_i2[1]], outs=[ccd_o2[1]],
                                 replica_groups=RG).then_inc(ccsem, 1)

        @block.scalar
        def _(s):
            S = E(s, ssem)
            S2 = Ctr()

            def sop(fn, *a, m=None, **kw):
                S.op(fn, *a, inc=(s2, S2) if m else None, **kw)
                if m:
                    mk(m, S2)

            def sprobe(names):
                if probe in names:
                    s.wait_ge(v2, sv("v.probe"))
                    psrc, ncols = probe_src[probe]
                    if psrc.dtype == F32:
                        s.dma_start(out=prb[:, 0:ncols],
                                    in_=psrc[:, 0:ncols]).then_inc(prd, 16)
                    else:
                        s.dma_start(out=prb[:].bitcast(BF16)[:, 0:ncols],
                                    in_=psrc[:, 0:ncols]).then_inc(prd, 16)
                    s.wait_ge(prd, 16)

            s.wait_ge(sd, sv("sd.consts"))
            s.wait_ge(gd, sv("gd.xb"))
            for hc in range(HC):
                if hc:
                    S.w(t2, sv(f"t.sqc0.{hc - 1}"))
                sop(s.activation, sq_t[:, 0:1024], lnho[:, T * hc:T * (hc + 1)], ACT.Square,
                    m=f"s.sq0.{hc}")
            S.w(v2, sv("v.var1"))
            sop(s.activation, dent, var_f[:, 0:512], ACT.Ln, bias=EPS_LN)
            sop(s.activation, rs_s[:, 0:512], dent, ACT.Exp, scale=-0.5)
            sop(s.activation, dent, var_f[:, 512:1024], ACT.Ln, bias=EPS_LN)
            sop(s.activation, rs_s[:, 512:1024], dent, ACT.Exp, scale=-0.5, m="s.rstd1")
            sprobe(("xln1",))
            def pev(sl):
                wname, c0, kind = PROJ_SLICES[sl]
                for ch in range(NCH):
                    S.w(t2, sv(f"t.proj.{sl}.{ch}"))
                    src = (pA, pB, pC, pE)[ch % 4]
                    if kind == 0:
                        dst = kv_tm[:, 1024 * ch + c0:1024 * ch + c0 + 256]
                    elif kind == 1:
                        dst = kv_tm[:, 1024 * ch + 512 + c0:1024 * ch + 512 + c0 + 256]
                    else:
                        dst = q_tm[:, DM * ch + c0:DM * ch + c0 + 256]
                    sop(s.copy, dst, src[:, 0:256], m=f"s.pev.{sl}.{ch}")

            for sl in range(4):
                pev(sl)
            for ch in range(NCH):
                S.w(v2, sv(f"v.ropek.{ch}"))
                sop(s.activation, rt(1, 512), rt(0, 512), ACT.Exp, m=f"s.expk.{ch}")
            sprobe(("kvtm", "deqg"))
            for sl in range(4, 12):
                pev(sl)
            for ch in range(2):
                for hf in range(2):
                    S.w(v2, sv(f"v.ropeq.{ch}.{hf}"))
                    sop(s.activation, rt(3), rt(2), ACT.Exp, m=f"s.expq.{ch}.{hf}")
            sprobe(("qtm", "stot", "sglob"))
            for ch in range(NCH):
                for kv in range(NKV):
                    S.w(t2, sv(f"t.den.{ch}.{kv}"))
                    sop(s.activation, dent, pE[0:1, :], ACT.Ln, bias=EPS_ATT)
                    sop(s.activation, rden, dent, ACT.Exp, scale=-1.0, m=f"s.rden.{ch}.{kv}")
                    if ch + 2 < NCH and kv in (1, 2):
                        S.w(v2, sv(f"v.ropeq.{ch + 2}.{kv - 1}"))
                        sop(s.activation, rt(3), rt(2), ACT.Exp,
                            m=f"s.expq.{ch + 2}.{kv - 1}")
            sprobe(("headout", "sloc", "xmid"))
            for hc in range(HC):
                S.w(v2, sv(f"v.ln2cast.{hc}"))
                if hc:
                    S.w(t2, sv(f"t.sqc2.{hc - 1}"))
                sop(s.activation, sq_t[:, 0:1024], lnho[:, T * hc:T * (hc + 1)], ACT.Square,
                    m=f"s.sq2.{hc}")
            S.w(v2, sv("v.var2"))
            sop(s.activation, dent, var_f[:, 0:512], ACT.Ln, bias=EPS_LN)
            sop(s.activation, rs_s[:, 0:512], dent, ACT.Exp, scale=-0.5)
            sop(s.activation, dent, var_f[:, 512:1024], ACT.Ln, bias=EPS_LN)
            sop(s.activation, rs_s[:, 512:1024], dent, ACT.Exp, scale=-0.5, m="s.rstd2")
            sprobe(("xln2",))
            for ps in range(2):
                for bi in range(NIB):
                    i = ps * NIB + bi
                    S.w(t2, sv(f"t.gu.{i}"))
                    if i:
                        S.w(v2, sv(f"v.hact.{i - 1}"))
                    gb = (pA, pC)[bi % 2]
                    sop(s.activation, sq_t[:, 0:TP], gb[:], ACT.Silu, m=f"s.silu.{i}")
                if ps == 0:
                    sprobe(("hact",))
                for t in range(HC):
                    i2 = ps * HC + t
                    S.w(t2, sv(f"t.dn.{i2}"))
                    if i2:
                        S.w(t2, sv(f"t.rmsstat.{i2 - 1}"))
                    db = (pA, pC)[t % 2]
                    sop(s.copy, lnho[:, T * t + TP * ps:T * t + TP * (ps + 1)], db[:])
                    sop(s.activation, sq_t[:, 0:TP], db[:], ACT.Square, m=f"s.dnev.{i2}")
                S.w(t2, sv(f"t.rmss.{ps}"))
                sop(s.activation, dent, pZ[0:1, :], ACT.Ln, scale=1.0 / DM, bias=EPS_RMS)
                sop(s.activation, rden, dent, ACT.Exp, scale=-0.5, m=f"s.rms.{ps}")
                S.w(t2, sv(f"t.rmsbc.{ps}"))
                if ps == 1:
                    sprobe(("yb",))

        @block.vector
        def _(v):
            V = E(v, vsem)
            V2 = Ctr()

            def vop(fn, *a, m=None, **kw):
                V.op(fn, *a, inc=(v2, V2) if m else None, **kw)
                if m:
                    mk(m, V2)

            def pmark(name):
                if V.last is not None:
                    v.wait_ge(*V.last)
                v.sem_inc(v2, 1)
                V2.v += 1
                mk(name, V2)
                V.last = (v2, V2.v)
                if name == "v.probe":
                    v.wait_ge(prd, 16)

            vop(v.memset, ones_c[:], 1.0)
            vop(v.memset, ones_r[:], 1.0, m="v.ones")

            v.wait_ge(t2, sv("t.ln1stats"))
            vop(v.tensor_scalar, mu_f[:, 0:512], pC[0:1, :], 1.0 / DM, None, AOT.mult)
            vop(v.tensor_scalar, mu_f[:, 512:1024], pE[0:1, :], 1.0 / DM, None, AOT.mult)
            vop(v.tensor_scalar, var_f[:, 0:512], pZ[0:1, :], 1.0 / DM, None, AOT.mult)
            vop(v.tensor_scalar, var_f[:, 512:1024], pS[0:1, :], 1.0 / DM, None, AOT.mult)
            vop(v.tensor_tensor, z_tot[:].bitcast(F32) if False else lnscr,
                mu_f[:, 0:512], mu_f[:, 0:512], AOT.mult)
            vop(v.tensor_tensor, var_f[:, 0:512], var_f[:, 0:512], lnscr,
                AOT.subtract)
            vop(v.tensor_tensor, lnscr, mu_f[:, 512:1024], mu_f[:, 512:1024],
                AOT.mult)
            vop(v.tensor_tensor, var_f[:, 512:1024], var_f[:, 512:1024], lnscr,
                AOT.subtract, m="v.var1")
            V.w(s2, sv("s.rstd1"))
            vop(v.tensor_tensor, mu_s[:], mu_f[:], rs_s[:], AOT.mult, m="v.murstd1")
            v.wait_ge(t2, sv("t.ln1bc"))
            v.wait_ge(sd, sv("sd.consts"))
            for hc in range(HC):
                for half in range(2):
                    rb = (pC, pE)[half]; mb = (pZ, pS)[half]
                    c0 = T * hc + 512 * half
                    vop(v.tensor_tensor, dstg[:, 0:512], x_res[:, c0:c0 + 512], rb[:], AOT.mult)
                    vop(v.tensor_tensor, lnho[:, c0:c0 + 512], dstg[:, 0:512], mb[:],
                        AOT.subtract)
            pmark("v.ln1done")
            if probe == "xln1":
                pmark("v.probe")

            vqbatch = [0]
            def deq_piece(nu16, src_off, scal_ap, ngrp, grp0, dv_off, dvb=None):
                if dvb is None:
                    dvb = dvals
                for j in range(4):
                    w = dqw[:, src_off:src_off + nu16]
                    vop(v.tensor_scalar, dt1[:, 0:nu16], w, 4 * j, 7,
                        AOT.logical_shift_right, AOT.bitwise_and)
                    if 12 - 4 * j >= 0:
                        vop(v.tensor_scalar, dt2[:, 0:nu16], w, 12 - 4 * j, 0x8000,
                            AOT.logical_shift_left, AOT.bitwise_and)
                    else:
                        vop(v.tensor_scalar, dt2[:, 0:nu16], w, 4 * j - 12, 0x8000,
                            AOT.logical_shift_right, AOT.bitwise_and)
                    vop(v.tensor_scalar, dt3[:, 0:nu16], dt1[:, 0:nu16], 64.0, 16128.0,
                        AOT.mult, AOT.add)
                    vop(v.tensor_scalar, dt4[:, 0:nu16], dt3[:, 0:nu16], 2.0, -16256.0,
                        AOT.mult, AOT.add)
                    vop(v.tensor_tensor, dt3[:, 0:nu16], dt3[:, 0:nu16], dt4[:, 0:nu16],
                        AOT.min)
                    vop(v.tensor_single_scalar, dt4[:, 0:nu16], dt1[:, 0:nu16], 1.0, AOT.min)
                    vop(v.tensor_tensor, dt3[:, 0:nu16], dt3[:, 0:nu16], dt4[:, 0:nu16],
                        AOT.mult)
                    vop(v.tensor_tensor, dvb[:, dv_off + j:dv_off + 4 * nu16:4],
                        dt3[:, 0:nu16], dt2[:, 0:nu16], AOT.add)
                dv = dvb[:, dv_off:dv_off + 4 * nu16].bitcast(BF16)
                for gi in range(ngrp):
                    vop(v.tensor_scalar, dv[:, 128 * gi:128 * (gi + 1)],
                        dv[:, 128 * gi:128 * (gi + 1)],
                        scal_ap[:, grp0 + gi:grp0 + gi + 1], None, AOT.mult)

            def deq_gate_tile(t):
                v.wait_ge((pk0, pk1)[t % 2], sv(f"sd.pk.{t}"))
                if t >= 1:
                    v.wait_ge(t2, sv(f"t.deqT.{t - 1}.1"))
                vop(v.tensor_copy, gscf[:], gsc[:])
                deq_piece(512, 0, gscf, 16, 0, 0)
                pmark(f"v.deqfree.{t + 1}")
                pmark(f"v.deqval.{t}")
                if t >= 1:
                    v.wait_ge(css, sv(f"sd.ccgst.{t - 1}"))
                for hcb in range(2):
                    v.wait_ge(t2, sv(f"t.deqT.{t}.{hcb}"))
                    bank = (pT2, pT)[vqbatch[0] % 2][:].bitcast(BF16)
                    vqbatch[0] += 1
                    vop(v.tensor_copy, dstg[:, 1024 * hcb:1024 * (hcb + 1)], bank[:],
                        m=f"v.deqTev.{t}.{hcb}")
                pmark(f"v.deqstg.{t}")

            def deq_down_map(tt, hf):
                t = 11 + tt
                dd = 2 * tt + hf
                if hf == 0:
                    v.wait_ge((pk0, pk1)[t % 2], sv(f"sd.pk.{t}"))
                if dd >= 1:
                    v.wait_ge(t2, sv(f"t.deqTd.{dd - 1}.2"))
                else:
                    v.wait_ge(t2, sv("t.deqT.10.1"))
                if hf == 0:
                    vop(v.tensor_copy, dscalf[:], dscal[:])
                for sub in range(2):
                    deq_piece(352, 704 * hf + 352 * sub, dscalf, 11, 22 * hf + 11 * sub,
                              1408 * sub)
                if hf == 1:
                    pmark(f"v.deqfree.{t + 1}")
                pmark(f"v.deqvald.{dd}")

            def deq_down_evac_b(dd, hcb):
                if hcb == 0:
                    v.wait_ge(css, sv(f"sd.ccdst.{dd - 1}") if dd >= 1 else sv("sd.ccgst.10"))
                nb = (8, 8, 6)[hcb]
                v.wait_ge(t2, sv(f"t.deqTd.{dd}.{hcb}"))
                bank = (pT2, pT)[vqbatch[0] % 2][:].bitcast(BF16)
                vqbatch[0] += 1
                vop(v.tensor_copy, dstg[:, 1024 * hcb:1024 * hcb + 128 * nb],
                    bank[:, 0:128 * nb], m=f"v.deqTevd.{dd}.{hcb}")
                if hcb == 2:
                    pmark(f"v.deqstg.{11 + dd}")

            deq_gate_tile(0)
            deq_gate_tile(1)

            def rope(dst_ap, n, ch, alt=False):
                hv = dst_ap.rearrange("p (h d) -> p h d", h=n)
                lo = hv[:, :, 0:64]; hi = hv[:, :, 64:128]
                cl = cos_s[:, 128 * ch:128 * ch + 64].rearrange("p d -> p () d").broadcast_to([128, n, 64])
                cn = cos_s[:, 128 * ch + 64:128 * (ch + 1)].rearrange("p d -> p () d").broadcast_to([128, n, 64])
                sl_ = sin_s[:, 128 * ch:128 * ch + 64].rearrange("p d -> p () d").broadcast_to([128, n, 64])
                sn = sin_s[:, 128 * ch + 64:128 * (ch + 1)].rearrange("p d -> p () d").broadcast_to([128, n, 64])
                m = n * 64
                b1 = dbuf[:, 0:m] if alt else rt(0, m)
                b2 = dbuf[:, 1024:1024 + m] if alt else rt(1, m)
                w1 = b1.rearrange("p (h d) -> p h d", h=n)
                w2 = b2.rearrange("p (h d) -> p h d", h=n)
                w3 = rt(2, m).rearrange("p (h d) -> p h d", h=n)
                w4 = rt(3, m).rearrange("p (h d) -> p h d", h=n)
                vop(v.tensor_tensor, w1, lo, cl, AOT.mult)
                vop(v.tensor_tensor, w2, hi, sl_, AOT.mult)
                vop(v.tensor_tensor, w3, hi, cn, AOT.mult)
                vop(v.tensor_tensor, w4, lo, sn, AOT.mult)
                vop(v.tensor_tensor, lo, w1, w2, AOT.subtract)
                vop(v.tensor_tensor, hi, w3, w4, AOT.add)

            for ch in range(NCH):
                V.w(s2, sv(f"s.pev.3.{ch}"))
                kap = kv_tm[:, 1024 * ch:1024 * ch + 512]
                rope(kap, 4, ch)
                vop(v.tensor_single_scalar, rt(0, 512), kap, 0.0, AOT.min, m=f"v.ropek.{ch}")
                V.w(s2, sv(f"s.expk.{ch}"))
                vop(v.scalar_tensor_tensor, kap, kap, 0.0, rt(1, 512), AOT.max, AOT.add,
                    m=f"v.eluk.{ch}")
            if probe == "kvtm":
                pmark("v.probe")

            vop(v.memset, s_cum[:], 0.0)
            vop(v.memset, z_cum[:], 0.0)
            for ch in range(NCH):
                v.wait_ge(t2, sv(f"t.stacc.{ch}"))
                vop(v.tensor_tensor, s_cum[:], s_cum[:], pS[:], AOT.add)
                vop(v.tensor_copy, s_loc[:, SLOC0 + 512 * ch:SLOC0 + 512 * (ch + 1)], s_cum[:])
                vop(v.tensor_tensor, z_cum[:, 0:4], z_cum[:, 0:4], pZ[:, 0:4], AOT.add)
                vop(v.tensor_copy, z_loc[:, 8 * ch:8 * ch + 4], z_cum[:, 0:4],
                    m=f"v.stev.{ch}")
            pmark("v.stot")

            for t in range(2, 11):
                deq_gate_tile(t)
            if probe == "deqg":
                pmark("v.probe")

            def q_rope(ch, alt):
                V.w(s2, sv(f"s.pev.11.{ch}"))
                rope(q_tm[:, DM * ch:DM * (ch + 1)], 16, ch, alt=alt)

            def q_elu(ch, hf):
                qh = q_tm[:, DM * ch + 1024 * hf:DM * ch + 1024 * (hf + 1)]
                vop(v.tensor_single_scalar, rt(2), qh, 0.0, AOT.min,
                    m=f"v.ropeq.{ch}.{hf}")
                V.w(s2, sv(f"s.expq.{ch}.{hf}"))
                vop(v.scalar_tensor_tensor, qh, qh, 0.0, rt(3), AOT.max, AOT.add,
                    m=f"v.eluq.{ch}.{hf}")

            v.wait_ge(cst, 32)
            for ch in range(2):
                q_rope(ch, False)
                q_elu(ch, 0)
                q_elu(ch, 1)
            if probe == "qtm":
                pmark("v.probe")

            vop(v.memset, s_glob[:], 0.0)
            vop(v.memset, z_glob[:], 0.0)
            for r in range(8):
                v.wait_ge((st_0, st_1)[r % 2], sv(f"sd.stin.{r}"))
                so = 516 * (r % 2)
                vop(v.scalar_tensor_tensor, s_glob[:], stin[:, so:so + 512],
                    pmk[:, r:r + 1], s_glob[:], AOT.mult, AOT.add)
                vop(v.scalar_tensor_tensor, z_glob[:, 0:4], stin[:, so + 512:so + 516],
                    pmk[:, r:r + 1], z_glob[:, 0:4], AOT.mult, AOT.add,
                    m=f"v.stfree.{r + 2}")
            pmark("v.sglob")
            if probe == "sglob":
                pmark("v.probe")
            if probe == "stot":
                pmark("v.probe")

            def v_seff(ch):
                se = s_eff[:, 512 * (ch % 2):512 * (ch % 2 + 1)]
                ze = z_eff[:, 8 * (ch % 2):8 * (ch % 2) + 4]
                if ch == 0:
                    vop(v.tensor_scalar, se, s_glob[:], 1.0, None, AOT.mult)
                    vop(v.tensor_scalar, ze, z_glob[:, 0:4], 1.0, None, AOT.mult,
                        m=f"v.seff.{ch}")
                else:
                    vop(v.tensor_tensor, se, s_glob[:],
                        s_loc[:, SLOC0 + 512 * (ch - 1):SLOC0 + 512 * ch], AOT.add)
                    vop(v.tensor_tensor, ze, z_glob[:, 0:4],
                        z_loc[:, 8 * (ch - 1):8 * (ch - 1) + 4], AOT.add, m=f"v.seff.{ch}")

            def v_qfkf_a4m(idx):
                ch, kv = idx // NKV, idx % NKV
                par = idx % 2
                if kv == 0:
                    v_seff(ch)
                pTx = (pT, pT2)[par][:].bitcast(BF16)
                v.wait_ge(t2, sv(f"t.scanT.{ch}.{kv}"))
                for j in range(4):
                    vop(v.tensor_copy,
                        qf_fm[:, 512 * par + 128 * j:512 * par + 128 * (j + 1)],
                        pTx[:, 128 * j:128 * (j + 1)])
                vop(v.tensor_copy, kf_fm[:, 128 * par:128 * (par + 1)], pTx[:, 512:640],
                    m=f"v.qfkf.{ch}.{kv}")
                v.wait_ge(t2, sv(f"t.a4.{ch}.{kv}"))
                vop(v.tensor_tensor,
                    a4m[:, 512 * par:512 * (par + 1)].rearrange("p (h t) -> p h t", h=4),
                    pA[:].rearrange("p (h t) -> p h t", h=4),
                    trilm[:].rearrange("p t -> p () t").broadcast_to([128, 4, 128]),
                    AOT.mult, m=f"v.a4m.{ch}.{kv}")

            def v_odiv(idx):
                ch, kv = idx // NKV, idx % NKV
                v.wait_ge(t2, sv(f"t.rbc.{idx}"))
                vop(v.tensor_copy, rdbc, pC[:])
                vop(v.tensor_tensor,
                    lnho[:].rearrange("p (h t) -> p h t", h=HC)[:, 4 * kv:4 * (kv + 1),
                                                                128 * ch:128 * (ch + 1)],
                    pB[:].rearrange("p (h t) -> p h t", h=4),
                    rdbc.rearrange("p (h t) -> p h t", h=4),
                    AOT.mult, m=f"v.odiv.{ch}.{kv}")

            def woev_h0(ot):
                v.wait_ge(t2, sv(f"t.wo.{ot}.0"))
                src = (pS, pZ)[ot % 2]
                c0 = T * ot
                vop(v.tensor_tensor, x_res[:, c0:c0 + 512], x_res[:, c0:c0 + 512],
                    src[:], AOT.add, m=f"v.woevh0.{ot}")

            v_qfkf_a4m(0)
            for idx in range(NCH * NKV):
                if idx + 1 < NCH * NKV:
                    v_qfkf_a4m(idx + 1)
                v_odiv(idx)
                if idx % 4 == 0 and 2 + idx // 4 < NCH:
                    q_rope(2 + idx // 4, True)
                if idx % 4 == 1 and 2 + idx // 4 < NCH:
                    q_elu(2 + idx // 4, 0)
                if idx % 4 == 2 and 2 + idx // 4 < NCH:
                    q_elu(2 + idx // 4, 1)
                if idx >= 17:
                    woev_h0(idx - 17)
            woev_h0(15)
            pmark("v.scandone")
            if probe == "headout":
                pmark("v.probe")
            if probe == "sloc":
                pmark("v.probe")

            for ot in range(HC):
                v.wait_ge(t2, sv(f"t.wo.{ot}.1"))
                src = (pA, pB)[ot % 2]
                c0 = T * ot + 512
                vop(v.tensor_tensor, x_res[:, c0:c0 + 512], x_res[:, c0:c0 + 512],
                    src[:], AOT.add, m=f"v.woev.{ot}.1")
            if probe == "xmid":
                pmark("v.probe")

            for hc in range(HC):
                vop(v.tensor_copy, lnho[:, T * hc:T * (hc + 1)], x_res[:, T * hc:T * (hc + 1)],
                    m=f"v.ln2cast.{hc}")
            v.wait_ge(t2, sv("t.ln2stats"))
            v.wait_ge(css, sv("sd.ccgst.10"))
            vop(v.tensor_scalar, mu_f[:, 0:512], pC[0:1, :], 1.0 / DM, None, AOT.mult)
            vop(v.tensor_scalar, mu_f[:, 512:1024], pE[0:1, :], 1.0 / DM, None, AOT.mult)
            vop(v.tensor_scalar, var_f[:, 0:512], pZ[0:1, :], 1.0 / DM, None, AOT.mult)
            vop(v.tensor_scalar, var_f[:, 512:1024], pS[0:1, :], 1.0 / DM, None, AOT.mult)
            vop(v.tensor_tensor, lnscr, mu_f[:, 0:512], mu_f[:, 0:512], AOT.mult)
            vop(v.tensor_tensor, var_f[:, 0:512], var_f[:, 0:512], lnscr,
                AOT.subtract)
            vop(v.tensor_tensor, lnscr, mu_f[:, 512:1024], mu_f[:, 512:1024],
                AOT.mult)
            vop(v.tensor_tensor, var_f[:, 512:1024], var_f[:, 512:1024], lnscr,
                AOT.subtract, m="v.var2")
            V.w(s2, sv("s.rstd2"))
            vop(v.tensor_tensor, mu_s[:], mu_f[:], rs_s[:], AOT.mult, m="v.murstd2")
            v.wait_ge(t2, sv("t.ln2bc"))
            for half in range(2):
                for hc in range(HC):
                    rb = (pC, pZ)[half]; mb = (pE, pS)[half]
                    c0 = T * hc + 512 * half
                    vop(v.tensor_tensor, dstg[:, 0:512], x_res[:, c0:c0 + 512], rb[:], AOT.mult)
                    vop(v.tensor_tensor, lnho[:, c0:c0 + 512], dstg[:, 0:512], mb[:],
                        AOT.subtract)
                if half == 0:
                    pmark("v.ln2done0")
            pmark("v.ln2done")
            if probe == "xln2":
                pmark("v.probe")

            for ps in range(2):
                for bi, (k, r, lb) in enumerate(FFN_ORDER):
                    i = ps * NIB + bi
                    gI = 11 * r + lb
                    V.w(s2, sv(f"s.silu.{i}"))
                    ub = (pB, pE)[bi % 2]
                    vop(v.tensor_tensor, hact(gI), sq_t[:, 0:TP], ub[:], AOT.mult,
                        m=f"v.hact.{i}")
                    if ps == 0:
                        if i == 2:
                            deq_down_map(0, 0)
                        if i in (7, 14, 21, 28):
                            dd = (7, 14, 21, 28).index(i)
                            deq_down_evac_b(dd, 0)
                            deq_down_evac_b(dd, 1)
                        if i in (9, 16, 23, 30):
                            dd = (9, 16, 23, 30).index(i)
                            deq_down_evac_b(dd, 2)
                            if dd < 3:
                                deq_down_map((dd + 1) // 2, (dd + 1) % 2)
                if probe == "hact" and ps == 0:
                    pmark("v.probe")
                v.wait_ge(t2, sv(f"t.rmsbc.{ps}"))
                for hc in range(HC):
                    c0 = T * hc + TP * ps
                    vop(v.tensor_tensor, dstg[:, 0:512], lnho[:, c0:c0 + 512], pS[:], AOT.mult)
                    vop(v.tensor_tensor, x_res[:, c0:c0 + 512], x_res[:, c0:c0 + 512],
                        dstg[:, 0:512], AOT.add, m=f"v.res.{ps}.{hc}")
                if probe == "yb" and ps == 1:
                    pmark("v.probe")
            pmark("v.final")

        @block.tensor
        def _(t):
            TE = E(t, tsem)
            T2 = Ctr()

            def top(fn, *a, m=None, **kw):
                TE.op(fn, *a, inc=(t2, T2) if m else None, ser=False, **kw)
                if m:
                    mk(m, T2)

            t.wait_ge(gd, sv("gd.xb"))
            t.wait_ge(sd, sv("sd.consts"))
            t.wait_ge(v2, sv("v.ones"))
            for hc in range(HC):
                t.wait_ge(s2, sv(f"s.sq0.{hc}"))
                st, sp = (hc == 0), (hc == HC - 1)
                top(t.matmul, pC[0:1, :], ones_c[:], lnho[:, T * hc:T * hc + 512],
                    start=st, stop=sp)
                top(t.matmul, pE[0:1, :], ones_c[:], lnho[:, T * hc + 512:T * (hc + 1)],
                    start=st, stop=sp)
                top(t.matmul, pZ[0:1, :], ones_c[:], sq_t[:, 0:512], start=st, stop=sp)
                top(t.matmul, pS[0:1, :], ones_c[:], sq_t[:, 512:1024], start=st, stop=sp,
                    m=f"t.sqc0.{hc}")
            mk("t.ln1stats", T2)
            t.wait_ge(v2, sv("v.murstd1"))
            top(t.matmul, pC[:], ones_r[:], rs_s[:, 0:512], start=True, stop=True)
            top(t.matmul, pE[:], ones_r[:], rs_s[:, 512:1024], start=True, stop=True)
            top(t.matmul, pZ[:], ones_r[:], mu_s[:, 0:512], start=True, stop=True)
            top(t.matmul, pS[:], ones_r[:], mu_s[:, 512:1024], start=True, stop=True,
                m="t.ln1bc")

            dqbatch = [0]
            dqevs = []

            def deq_T(ti):
                t.wait_ge(v2, sv(f"v.deqval.{ti}"))
                tsrc = dvals
                for hcb in range(2):
                    bank = (pT2, pT)[dqbatch[0] % 2][:].bitcast(BF16)
                    if dqbatch[0] >= 2:
                        t.wait_ge(v2, sv(dqevs[dqbatch[0] - 2]))
                    dqbatch[0] += 1
                    for qq in range(8):
                        hcx = 8 * hcb + qq
                        top(t.transpose, bank[:, 128 * qq:128 * (qq + 1)],
                            tsrc[:, 128 * hcx:128 * (hcx + 1)].bitcast(BF16), ident[:],
                            m=f"t.deqT.{ti}.{hcb}" if qq == 7 else None)
                    dqevs.append(f"v.deqTev.{ti}.{hcb}")

            def deq_Td_b(dd, hcb):
                if hcb == 0:
                    t.wait_ge(v2, sv(f"v.deqvald.{dd}"))
                nb = (8, 8, 6)[hcb]
                bank = (pT2, pT)[dqbatch[0] % 2][:].bitcast(BF16)
                if dqbatch[0] >= 2:
                    t.wait_ge(v2, sv(dqevs[dqbatch[0] - 2]))
                dqbatch[0] += 1
                for qq in range(nb):
                    icx = 8 * hcb + qq
                    top(t.transpose, bank[:, 128 * qq:128 * (qq + 1)],
                        dvals[:, 128 * icx:128 * (icx + 1)].bitcast(BF16), ident[:],
                        m=f"t.deqTd.{dd}.{hcb}" if qq == nb - 1 else None)
                dqevs.append(f"v.deqTevd.{dd}.{hcb}")

            def proj_slice(sl):
                t.wait_ge((wb0, wb1)[sl % 2], sv(f"gd.wb.{sl}"))
                t.wait_ge(v2, sv("v.ln1done"))
                so = (sl % 2) * HC * 256
                for ch in range(NCH):
                    idx = sl * NCH + ch
                    if idx >= 4:
                        pr = idx - 4
                        t.wait_ge(s2, sv(f"s.pev.{pr // NCH}.{pr % NCH}"))
                    dstp = (pA, pB, pC, pE)[ch % 4]
                    for hc in range(HC):
                        top(t.matmul, dstp[:, 0:256],
                            lnho[:, T * hc + 128 * ch:T * hc + 128 * (ch + 1)],
                            wbuf[:, so + 256 * hc:so + 256 * (hc + 1)],
                            start=(hc == 0), stop=(hc == HC - 1),
                            m=f"t.proj.{sl}.{ch}" if hc == HC - 1 else None)
                mk(f"t.wbfree.{sl + 2}", T2)

            for sl in range(4):
                proj_slice(sl)
            deq_T(0)
            deq_T(1)
            for ch in range(NCH):
                t.wait_ge(v2, sv(f"v.eluk.{ch}"))
                if ch > 0:
                    t.wait_ge(v2, sv(f"v.stev.{ch - 1}"))
                for kv in range(NKV):
                    st = (kv == 0)
                    sp = (kv == NKV - 1)
                    top(t.matmul, pS[:, 128 * kv:128 * (kv + 1)],
                        kv_tm[:, 1024 * ch + 128 * kv:1024 * ch + 128 * (kv + 1)],
                        kv_tm[:, 1024 * ch + 512 + 128 * kv:1024 * ch + 512 + 128 * (kv + 1)],
                        start=st, stop=(kv == NKV - 1))
                    top(t.matmul, pZ[:, kv:kv + 1],
                        kv_tm[:, 1024 * ch + 128 * kv:1024 * ch + 128 * (kv + 1)],
                        ones_c[:], start=st, stop=sp,
                        m=f"t.stacc.{ch}" if kv == NKV - 1 else None)
            for sl in range(4, 12):
                proj_slice(sl)
                if sl - 2 <= 10:
                    deq_T(sl - 2)
            deq_T(10)

            def scan_T_A4(idx):
                ch, kv = idx // NKV, idx % NKV
                par = idx % 2
                t.wait_ge(v2, sv(f"v.eluq.{ch}.{kv // 2}"))
                if idx >= 2:
                    t.wait_ge(v2, sv(f"v.qfkf.{(idx - 2) // NKV}.{(idx - 2) % NKV}"))
                pTx = (pT, pT2)[par][:].bitcast(BF16)
                for j in range(4):
                    h = 4 * kv + j
                    top(t.transpose, pTx[:, 128 * j:128 * (j + 1)],
                        q_tm[:, DM * ch + 128 * h:DM * ch + 128 * (h + 1)], ident[:])
                top(t.transpose, pTx[:, 512:640],
                    kv_tm[:, 1024 * ch + 128 * kv:1024 * ch + 128 * (kv + 1)], ident[:],
                    m=f"t.scanT.{ch}.{kv}")
                t.wait_ge(v2, sv(f"v.qfkf.{ch}.{kv}"))
                if idx >= 1:
                    t.wait_ge(v2, sv(f"v.a4m.{(idx - 1) // NKV}.{(idx - 1) % NKV}"))
                top(t.matmul, pA[:], kf_fm[:, 128 * par:128 * (par + 1)],
                    qf_fm[:, 512 * par:512 * (par + 1)], start=True, stop=True,
                    m=f"t.a4.{ch}.{kv}")

            def scan_rest(idx):
                ch, kv = idx // NKV, idx % NKV
                par = idx % 2
                t.wait_ge(v2, sv(f"v.a4m.{ch}.{kv}"))
                t.wait_ge(v2, sv(f"v.seff.{ch}"))
                if idx >= 1:
                    t.wait_ge(s2, sv(f"s.rden.{(idx - 1) // NKV}.{(idx - 1) % NKV}"))
                top(t.matmul, pE[0:1, :], ones_c[:], a4m[:, 512 * par:512 * (par + 1)],
                    start=True, stop=False)
                top(t.matmul, pE[0:1, :], z_eff[:, 8 * (ch % 2) + kv:8 * (ch % 2) + kv + 1],
                    qf_fm[:, 512 * par:512 * (par + 1)], start=False, stop=True,
                    m=f"t.den.{ch}.{kv}")
                if idx >= 1:
                    t.wait_ge(v2, sv(f"v.odiv.{(idx - 1) // NKV}.{(idx - 1) % NKV}"))
                top(t.matmul, pB[:],
                    kv_tm[:, 1024 * ch + 512 + 128 * kv:1024 * ch + 512 + 128 * (kv + 1)],
                    a4m[:, 512 * par:512 * (par + 1)], start=True, stop=False)
                top(t.matmul, pB[:], s_eff[:, 512 * (ch % 2) + 128 * kv:512 * (ch % 2) + 128 * (kv + 1)],
                    qf_fm[:, 512 * par:512 * (par + 1)], start=False, stop=True)
                t.wait_ge(s2, sv(f"s.rden.{ch}.{kv}"))
                if idx >= 1:
                    t.wait_ge(v2, sv(f"v.odiv.{(idx - 1) // NKV}.{(idx - 1) % NKV}"))
                top(t.matmul, pC[:], ones_r[:], rden, start=True, stop=True,
                    m=f"t.rbc.{idx}")

            wo_p1slot = [RC[:, 4096:6144], RC[:, 6144:8192], RA[:, 0:2048], RA[:, 2048:4096]]

            def wo_half(ot, half):
                t.wait_ge(wos[4 * half + ot % 4], sv(f"gd.wo{half}.{ot}"))
                if half == 0:
                    if ot == 0:
                        t.wait_ge(v2, sv("v.odiv.3.3"))
                    if ot >= 2:
                        t.wait_ge(v2, sv(f"v.woevh0.{ot - 2}"))
                    dstp = (pS, pZ)[ot % 2]
                    wsl = wo_p1slot[ot % 4]
                else:
                    if ot == 0:
                        t.wait_ge(v2, sv(f"v.odiv.{NCH - 1}.{NKV - 1}"))
                    if ot >= 2:
                        t.wait_ge(v2, sv(f"v.woev.{ot - 2}.1"))
                    dstp = (pA, pB)[ot % 2]
                    wsl = RB[:, 2048 * (ot % 4):2048 * (ot % 4 + 1)]
                for hp in range(HC):
                    top(t.matmul, dstp[:],
                        wsl[:, 128 * hp:128 * (hp + 1)],
                        lnho[:, T * hp + 512 * half:T * hp + 512 * (half + 1)],
                        start=(hp == 0), stop=(hp == HC - 1),
                        m=f"t.wo.{ot}.{half}" if hp == HC - 1 else None)

            scan_T_A4(0)
            for idx in range(NCH * NKV):
                if idx + 1 < NCH * NKV:
                    scan_T_A4(idx + 1)
                scan_rest(idx)
                if idx >= 16:
                    wo_half(idx - 16, 0)

            for ot in range(HC):
                wo_half(ot, 1)
            for hc in range(HC):
                t.wait_ge(s2, sv(f"s.sq2.{hc}"))
                st, sp = (hc == 0), (hc == HC - 1)
                top(t.matmul, pC[0:1, :], ones_c[:], lnho[:, T * hc:T * hc + 512],
                    start=st, stop=sp)
                top(t.matmul, pE[0:1, :], ones_c[:], lnho[:, T * hc + 512:T * (hc + 1)],
                    start=st, stop=sp)
                top(t.matmul, pZ[0:1, :], ones_c[:], sq_t[:, 0:512], start=st, stop=sp)
                top(t.matmul, pS[0:1, :], ones_c[:], sq_t[:, 512:1024], start=st, stop=sp,
                    m=f"t.sqc2.{hc}")
            mk("t.ln2stats", T2)
            t.wait_ge(v2, sv("v.murstd2"))
            top(t.matmul, pC[:], ones_r[:], rs_s[:, 0:512], start=True, stop=True)
            top(t.matmul, pZ[:], ones_r[:], rs_s[:, 512:1024], start=True, stop=True)
            top(t.matmul, pE[:], ones_r[:], mu_s[:, 0:512], start=True, stop=True)
            top(t.matmul, pS[:], ones_r[:], mu_s[:, 512:1024], start=True, stop=True,
                m="t.ln2bc")

            for ps in range(2):
                t.wait_ge(v2, sv("v.ln2done0" if ps == 0 else "v.ln2done"))
                for bi, (k, r, lb) in enumerate(FFN_ORDER):
                    i = ps * NIB + bi
                    t.wait_ge((gf0, gf1, gf2)[i % 3], sv(f"sd.gf.{i}"))
                    if i >= 2:
                        t.wait_ge(v2, sv(f"v.hact.{i - 2}"))
                    gb = (pA, pC)[bi % 2]; ub = (pB, pE)[bi % 2]
                    fw = ffnw_ap(i)
                    for hc in range(HC):
                        st, sp = (hc == 0), (hc == HC - 1)
                        top(t.matmul, gb[:], fw[:, 128 * hc:128 * (hc + 1)],
                            lnho[:, T * hc + TP * ps:T * hc + TP * (ps + 1)],
                            start=st, stop=sp)
                        top(t.matmul, ub[:],
                            fw[:, DM + 128 * hc:DM + 128 * (hc + 1)],
                            lnho[:, T * hc + TP * ps:T * hc + TP * (ps + 1)],
                            start=st, stop=sp, m=f"t.gu.{i}" if sp else None)
                    mk(f"t.gffree.{i + 3}", T2)
                    if ps == 0:
                        if bi in (6, 13, 20, 27):
                            dd = (6, 13, 20, 27).index(bi)
                            deq_Td_b(dd, 0)
                            deq_Td_b(dd, 1)
                        if bi in (8, 15, 22, 29):
                            deq_Td_b((8, 15, 22, 29).index(bi), 2)
                for tt_ in range(HC):
                    i2 = ps * HC + tt_
                    t.wait_ge(v2, sv(f"v.hact.{ps * NIB + NIB - 1}"))
                    if i2 >= 2:
                        t.wait_ge(s2, sv(f"s.dnev.{i2 - 2}"))
                    db = (pA, pC)[tt_ % 2]
                    for hx in range(2):
                        hidx = 2 * i2 + hx
                        t.wait_ge((df0, df1)[hidx % 2], sv(f"sd.df.{hidx}"))
                        so = (hidx % 2) * 2816
                        for icl in range(22):
                            ic = 22 * hx + icl
                            top(t.matmul, db[:], dwb[:, so + 128 * icl:so + 128 * (icl + 1)],
                                hact(ic), start=(ic == 0), stop=(ic == NIB - 1),
                                m=f"t.dnh.{hidx}" if icl == 21 else None)
                    mk(f"t.dn.{i2}", T2)
                    t.wait_ge(s2, sv(f"s.dnev.{i2}"))
                    top(t.matmul, pZ[0:1, :], ones_c[:], sq_t[:, 0:TP],
                        start=(tt_ == 0), stop=(tt_ == HC - 1), m=f"t.rmsstat.{i2}")
                mk(f"t.rmss.{ps}", T2)
                t.wait_ge(s2, sv(f"s.rms.{ps}"))
                top(t.matmul, pS[:], ones_r[:], rden, start=True, stop=True,
                    m=f"t.rmsbc.{ps}")

    return nc, marks


def build2(probe=None):
    _, marks = build({}, probe=probe)
    nc, _ = build(marks, probe=probe)
    return nc, marks


def _stage_inputs(L0_attn_x, L0_attn_Wq, L0_attn_Wk, L0_attn_Wv, L0_attn_Wo,
                  L0_ffn_gate_up_packed, L0_ffn_gate_up_scales,
                  L0_ffn_down_packed, L0_ffn_down_scales):
    import ml_dtypes
    bf = ml_dtypes.bfloat16
    x = np.ascontiguousarray(np.asarray(L0_attn_x, np.float32))
    wqT = np.ascontiguousarray(np.asarray(L0_attn_Wq, np.float32).T)
    wkT = np.ascontiguousarray(np.asarray(L0_attn_Wk, np.float32).T)
    wvT = np.ascontiguousarray(np.asarray(L0_attn_Wv, np.float32).T)
    woT = np.ascontiguousarray(np.asarray(L0_attn_Wo, np.float32).T)
    gupk = np.asarray(L0_ffn_gate_up_packed, np.int32)
    gusT = np.ascontiguousarray(np.asarray(L0_ffn_gate_up_scales, np.float16).T)
    dwpk = np.asarray(L0_ffn_down_packed, np.int32)
    dwsT = np.ascontiguousarray(np.asarray(L0_ffn_down_scales, np.float16).T)
    identb = np.eye(128).astype(bf)
    tril = np.triu(np.ones((128, 128), np.float32))
    tril4 = np.ascontiguousarray(tril).astype(bf)
    constsb = np.tile(np.array([1e-5, 1e-6, -0.5, -1.0, 1.0 / 2048.0, 0.5, 2.0, 1.0],
                               np.float32), (128, 1))
    in_maps = []
    for c in range(8):
        b, q = c // 4, c % 4
        xT = np.ascontiguousarray(x[b, 1024 * q:1024 * (q + 1), :].T)
        pos = 1024 * q + np.arange(1024, dtype=np.float64)
        invf = 10000.0 ** (-np.arange(0, 128, 2, dtype=np.float64) / 128.0)
        fr = np.outer(pos, invf)
        emb = np.concatenate([fr, fr], axis=1)
        cosb = np.cos(emb).astype(np.float32).astype(bf).reshape(8, 128, 128)
        sinb = np.sin(emb).astype(np.float32).astype(bf).reshape(8, 128, 128)
        pm = np.zeros((128, 8), np.float32)
        for r in range(8):
            if r // 4 == b and r % 4 < q:
                pm[:, r] = 1.0
        in_maps.append({
            "xT": xT, "wqT": wqT, "wkT": wkT, "wvT": wvT, "woT": woT,
            "gup": np.ascontiguousarray(gupk[1408 * c:1408 * (c + 1)]),
            "gus": np.ascontiguousarray(gusT[1408 * c:1408 * (c + 1)]),
            "dwp": np.ascontiguousarray(dwpk[256 * c:256 * (c + 1)]),
            "dws": np.ascontiguousarray(dwsT[256 * c:256 * (c + 1)]),
            "cosb": np.ascontiguousarray(cosb), "sinb": np.ascontiguousarray(sinb),
            "identb": identb, "tril4b": tril4, "pmaskb": pm, "constsb": constsb,
        })
    return in_maps


_NC_CACHE = {}


def kernel(**inputs):
    from concourse.bass_utils import run_bass_kernel_spmd
    if "nc" not in _NC_CACHE:
        _, marks = build({})
        nc, _ = build(marks)
        _NC_CACHE["nc"] = nc
    nc = _NC_CACHE["nc"]
    in_maps = _stage_inputs(**inputs)
    res = run_bass_kernel_spmd(nc, in_maps, core_ids=list(range(8)))
    out = np.zeros((2, 4096, 2048), np.float32)
    for c in range(8):
        b, q = c // 4, c % 4
        out[b, 1024 * q:1024 * (q + 1), :] = res.results[c]["y"].T
    return out


# version 17
# speedup vs baseline: 1.0231x; 1.0004x over previous
import contextlib
import numpy as np
import concourse.bass as bass
import concourse.mybir as mybir

F32 = mybir.dt.float32; BF16 = mybir.dt.bfloat16
I32 = mybir.dt.int32; U16 = mybir.dt.uint16; F16 = mybir.dt.float16
AOT = mybir.AluOpType; ACT = mybir.ActivationFunctionType

DM = 2048; HC = 16
T = 1024; NCH = 8; CL = 128
NH = 16; NKV = 4; HD = 128
IF = 5632; NIB = 44
GUSL = 1408; DWSL = 256
TP = 512
GUP = [4, 4, 3]
EPS_LN = 1e-5; EPS_ATT = 1e-6; EPS_RMS = 1e-6
CONSTS = [EPS_LN, EPS_ATT, -0.5, -1.0, 1.0 / 2048.0, 0.5, 2.0, 1.0]

FFN_ORDER = []
for _k in range(3):
    _lo = sum(GUP[:_k])
    for _r in range(4):
        for _j in range(GUP[_k]):
            FFN_ORDER.append((_k, _r, _lo + _j))

PROJ_SLICES = ([("wkT", c, 0) for c in (0, 256)] + [("wvT", c, 1) for c in (0, 256)]
               + [("wqT", c, 2) for c in range(0, 2048, 256)])


class Ctr:
    def __init__(self):
        self.v = 0


class E:
    def __init__(self, eng, sem, dma=False):
        self.e = eng; self.sem = sem; self.ctr = Ctr()
        self.last = None; self.dma = dma

    def w(self, sem, val):
        if val > 0:
            self.e.wait_ge(sem, val)

    def op(self, fn, *args, inc=None, ser=True, **kw):
        if ser and self.last is not None:
            self.e.wait_ge(*self.last)
        inst = fn(*args, **kw)
        step = 16 if self.dma else 1
        if inc is not None:
            sem, ctr = inc
            ctr.v += step
            inst.then_inc(sem, step)
            self.last = (sem, ctr.v)
        else:
            self.ctr.v += step
            inst.then_inc(self.sem, step)
            self.last = (self.sem, self.ctr.v)
        return inst


def build(sched, probe=None):
    marks = {}

    def mk(name, c):
        marks[name] = c.v

    def sv(name):
        return sched.get(name, 0)

    nc = bass.Bass("TRN2", target_bir_lowering=False)
    dram = lambda n, s, d, k="ExternalInput", **kw: nc.dram_tensor(n, s, d, kind=k, **kw).ap()

    xT = dram("xT", [DM, T], F32)
    w_in = {"wqT": dram("wqT", [DM, DM], F32), "wkT": dram("wkT", [DM, 512], F32),
            "wvT": dram("wvT", [DM, 512], F32)}
    woT = dram("woT", [DM, DM], F32)
    gup = dram("gup", [GUSL, 256], I32)
    dwp = dram("dwp", [DWSL, 704], I32)
    gus = dram("gus", [GUSL, 16], F16)
    dws = dram("dws", [DWSL, 44], F16)
    cosb = dram("cosb", [NCH, CL, HD], BF16)
    sinb = dram("sinb", [NCH, CL, HD], BF16)
    identb = dram("identb", [128, 128], BF16)
    tril4b = dram("tril4b", [128, 128], BF16)
    pmaskb = dram("pmaskb", [128, 8], F32)
    constsb = dram("constsb", [128, 8], F32)

    y = dram("y", [DM, T], F32, k="ExternalOutput")
    prb = dram("prb", [128, 16384], F32, k="ExternalOutput") if probe else None

    RG = [[0, 1, 2, 3, 4, 5, 6, 7]]
    ccs_i = dram("ccs_i", [128, 516], F32, k="Internal")
    ccs_o = dram("ccs_o", [128 * 8, 516], F32, k="Internal", addr_space="Shared")
    ccg_i = [dram(f"ccg_i{k}", [DM, GUP[k] * 128], BF16, k="Internal") for k in range(3)]
    ccg_o = [dram(f"ccg_o{k}", [DM * 8, GUP[k] * 128], BF16, k="Internal", addr_space="Shared")
             for k in range(3)]
    ccd_i2 = [dram(f"ccd_i{h}", [2816, DWSL], BF16, k="Internal") for h in range(2)]
    ccd_o2 = [dram(f"ccd_o{h}", [2816 * 8, DWSL], BF16, k="Internal", addr_space="Shared")
              for h in range(2)]

    es = contextlib.ExitStack()
    with es:
        sb = lambda n, s, d: es.enter_context(nc.sbuf_tensor(n, s, d))
        psum = lambda n, s, d: es.enter_context(nc.psum_tensor(n, s, d))
        sem = lambda n: es.enter_context(nc.semaphore(n))

        x_res = sb("x_res", [128, HC * T], F32)
        lnho = sb("lnho", [128, HC * T], BF16)
        RA = sb("RA", [128, 16384], BF16)
        RB = sb("RB", [128, 12288], BF16)
        RC = sb("RC", [128, 8192], BF16)
        dqw = sb("dqw", [128, 1408], U16)
        dt1 = sb("dt1", [128, 512], U16)
        dt2 = sb("dt2", [128, 512], U16)
        dt3 = sb("dt3", [128, 512], U16)
        dt4 = sb("dt4", [128, 512], U16)
        dvals = sb("dvals", [128, 2816], U16)
        dstg = sb("dstg", [128, 2816], BF16)
        dscal = sb("dscal", [128, 44], F16)
        gsc = sb("gsc", [128, 16], F16)
        dscalf = sb("dscalf", [128, 44], F32)
        gscf = sb("gscf", [128, 16], F32)
        dbuf = sb("dbuf", [128, 2048], BF16)
        dbuf_u16 = None
        rtb = sb("rtb", [128, 4096], BF16)
        sq_t = sb("sq_t", [128, 1024], BF16)
        s_loc = None
        s_eff = sb("s_eff", [128, 1024], BF16)
        z_eff = sb("z_eff", [128, 16], BF16)
        z_loc = sb("z_loc", [128, NCH * 8], BF16)
        ident = sb("ident", [128, 128], BF16)
        trilm = sb("trilm", [128, 128], BF16)
        cos_s = sb("cos_s", [128, NCH * HD], BF16)
        sin_s = sb("sin_s", [128, NCH * HD], BF16)
        pmk = sb("pmk", [128, 8], F32)
        ones_c = sb("ones_c", [128, 1], BF16)
        ones_r = sb("ones_r", [1, 128], BF16)
        consts = sb("consts", [128, 8], F32)
        for _i, _val in enumerate(CONSTS):
            nc.const_aps.register(F32, float(_val), consts[:, _i:_i + 1])

        q_tm = RA
        kv_tm = RB
        s_loc = RB
        SLOC0 = 8192
        wbuf = RC
        dwb = RC

        def ffnw_ap(i):
            s = i % 3
            if s < 2:
                return RC[:, 4096 * s:4096 * (s + 1)]
            return RB[:, 8192:12288]

        dbuf_u16 = dbuf[:].bitcast(U16)

        def hact(gI):
            if gI < 32:
                return RA[:, TP * gI:TP * (gI + 1)]
            return RB[:, TP * (gI - 32):TP * (gI - 31)]

        stin = dstg[:].bitcast(F32)
        lnscr = stin[0:1, 0:512]
        dvbf = dvals[:].bitcast(BF16)
        a4m = dvbf[:, 0:1024]
        qf_fm = dvbf[:, 1024:2048]
        kf_fm = dvbf[:, 2048:2304]
        rdbc = dvbf[:, 2304:2816]
        rt = lambda i, n=1024: rtb[:, 1024 * i:1024 * i + n]
        s_glob = rtb[:].bitcast(F32)[:, 0:512]
        z_glob = rtb[:].bitcast(F32)[:, 512:520]
        s_cum = rtb[:].bitcast(F32)[:, 520:1032]
        z_cum = rtb[:].bitcast(F32)[:, 1032:1040]
        stats = rtb[:].bitcast(F32)
        mu_f = stats[0:1, 0:1024]
        var_f = stats[0:1, 1024:2048]
        rs_s = rtb[0:1, 2048:3072]
        mu_s = rtb[0:1, 3072:4096]
        dent = sq_t[:].bitcast(F32)[0:1, 0:512]
        rden = dstg[0:1, 2048:2560]

        pA = psum("pA", [128, 512], F32)
        pB = psum("pB", [128, 512], F32)
        pS = psum("pS", [128, 512], F32)
        pT2 = psum("pT2", [128, 512], F32)
        pZ = psum("pZ", [128, 512], F32)
        pC = psum("pC", [128, 512], F32)
        pE = psum("pE", [128, 512], F32)
        pT = psum("pT", [128, 512], F32)

        sd = sem("sd"); pk0 = sem("pk0"); pk1 = sem("pk1")
        wb0 = sem("wb0"); wb1 = sem("wb1")
        wos = [sem(f"wo{j}") for j in range(8)]
        gd = sem("gd"); st_0 = sem("st_0"); st_1 = sem("st_1")
        gf0 = sem("gf0"); gf1 = sem("gf1"); gf2 = sem("gf2")
        df0 = sem("df0"); df1 = sem("df1")
        ccsem = sem("ccsem"); vsem = sem("vsem"); ssem = sem("ssem"); tsem = sem("tsem")
        v2 = sem("v2"); s2 = sem("s2"); t2 = sem("t2")
        outs = sem("outs"); css = sem("css"); cst = sem("cst"); prd = sem("prd"); gp2 = sem("gp2")

        OUT = Ctr(); CSS = Ctr()
        block = es.enter_context(nc.Block())

        probe_src = {
            "xln1": (lnho, 16384), "qtm": (q_tm, 16384), "kvtm": (kv_tm, 8192),
            "headout": (lnho, 16384), "xmid": (x_res, 16384), "xln2": (lnho, 16384),
            "hact": (RA, 16384), "yb": (lnho, 16384), "sglob": (s_glob, 512),
            "stot": (s_cum, 512), "sloc": (RB, 12288), "deqg": (dstg, 2816),
        }

        @block.sync
        def _(sync):
            D = E(sync, sd, dma=True)
            D.op(sync.dma_start, out=x_res[:].rearrange("p (c t) -> p c t", c=HC),
                 in_=xT.rearrange("(c p) t -> p c t", p=128), ser=False)
            D.op(sync.dma_start, out=ident[:], in_=identb, ser=False)
            D.op(sync.dma_start, out=trilm[:], in_=tril4b, ser=False)
            D.op(sync.dma_start, out=cos_s[:].rearrange("p (c d) -> p c d", c=NCH),
                 in_=cosb.rearrange("c p d -> p c d"), ser=False)
            D.op(sync.dma_start, out=sin_s[:].rearrange("p (c d) -> p c d", c=NCH),
                 in_=sinb.rearrange("c p d -> p c d"), ser=False)
            D.op(sync.dma_start, out=pmk[:], in_=pmaskb, ser=False)
            D.op(sync.dma_start, out=consts[:], in_=constsb, ser=False)
            mk("sd.consts", D.ctr)
            P = [E(sync, pk0, dma=True), E(sync, pk1, dma=True)]
            S = E(sync, css, dma=True)

            def pk_load(t):
                PP = P[t % 2]
                PP.w(v2, sv(f"v.deqfree.{t}"))

                if t < 11:
                    PP.op(sync.dma_start, out=dqw[:, 0:512],
                          in_=gup[128 * t:128 * (t + 1), :].bitcast(U16), ser=False)
                    PP.op(sync.dma_start, out=gsc[:],
                          in_=gus[128 * t:128 * (t + 1), :], ser=False)
                else:
                    tt = t - 11
                    PP.op(sync.dma_start, out=dqw[:, 0:1408],
                          in_=dwp[128 * tt:128 * (tt + 1), :].bitcast(U16), ser=False)
                    PP.op(sync.dma_start, out=dscal[:],
                          in_=dws[128 * tt:128 * (tt + 1), :], ser=False)
                mk(f"sd.pk.{t}", PP.ctr)

            def cc_store(t):
                S.w(v2, sv(f"v.deqstg.{t}"))
                if t < 11:
                    k = min(t // 4, 2)
                    lc = (t - [0, 4, 8][k]) * 128
                    S.op(sync.dma_start,
                         out=ccg_i[k][:, lc:lc + 128].rearrange("(c p) o -> p c o", p=128),
                         in_=dstg[:, 0:2048].rearrange("p (c o) -> p c o", c=HC),
                         inc=(css, CSS), ser=False)
                    mk(f"sd.ccgst.{t}", CSS)
                else:
                    tt = t - 11
                    ot, hf = tt // 2, tt % 2
                    S.op(sync.dma_start,
                         out=ccd_i2[hf][:, 128 * ot:128 * (ot + 1)]
                             .rearrange("(c p) o -> p c o", p=128),
                         in_=dstg[:, 0:2816].rearrange("p (c o) -> p c o", c=22),
                         inc=(css, CSS), ser=False)
                    mk(f"sd.ccdst.{tt}", CSS)

            ST = E(sync, cst, dma=True)
            pk_load(0)
            pk_load(1)
            for t in range(2, 11):
                cc_store(t - 2)
                if t == 3:
                    ST.w(v2, sv("v.stot"))
                    ST.op(sync.dma_start, out=ccs_i[:, 0:512], in_=s_cum[:], ser=False)
                    ST.op(sync.dma_start, out=ccs_i[:, 512:516], in_=z_cum[:, 0:4], ser=False)
                    mk("sd.ccsst", ST.ctr)
                pk_load(t)
            cc_store(9)
            cc_store(10)
            TT = [E(sync, st_0, dma=True), E(sync, st_1, dma=True)]
            sync.wait_ge(css, sv("sd.ccgst.10"))
            sync.wait_ge(ccsem, 1)
            for r in range(8):
                Q = TT[r % 2]
                Q.w(v2, sv(f"v.stfree.{r}"))
                Q.op(sync.dma_start, out=stin[:, 516 * (r % 2):516 * (r % 2) + 516],
                     in_=ccs_o[128 * r:128 * (r + 1), :], ser=False)
                mk(f"sd.stin.{r}", Q.ctr)
            pk_load(11)
            G = [E(sync, gf0, dma=True), E(sync, gf1, dma=True), E(sync, gf2, dma=True)]
            F = [E(sync, df0, dma=True), E(sync, df1, dma=True)]
            for ps in range(2):
                for bi, (k, r, lb) in enumerate(FFN_ORDER):
                    i = ps * NIB + bi
                    GG = G[i % 3]
                    sync.wait_ge(ccsem, 2 + k)
                    if i == 0:
                        GG.w(t2, sv("t.wbfree.13"))
                    if i == 1:
                        GG.w(t2, sv("t.wo.15.1"))
                    if i == 2:
                        GG.w(t2, sv("t.rbc.31"))
                    if ps == 1 and bi in (1, 2):
                        GG.w(t2, sv("t.dn.15"))
                    GG.w(t2, sv(f"t.gffree.{i}"))
                    lc = (lb - [0, 4, 8][k]) * 128
                    fw = ffnw_ap(i)
                    GG.op(sync.dma_start,
                          out=fw[:, 0:DM].rearrange("p (c o) -> p c o", c=HC),
                          in_=ccg_o[k][DM * r:DM * (r + 1), lc:lc + 128]
                              .rearrange("(c p) o -> p c o", p=128), ser=False)
                    GG.op(sync.dma_start,
                          out=fw[:, DM:2 * DM].rearrange("p (c o) -> p c o", c=HC),
                          in_=ccg_o[k][DM * (4 + r):DM * (5 + r), lc:lc + 128]
                              .rearrange("(c p) o -> p c o", p=128), ser=False)
                    mk(f"sd.gf.{i}", GG.ctr)
                    if ps == 0:
                        if bi == 12:
                            pk_load(12)
                        if bi in (12, 19, 26, 33):
                            cc_store(11 + (12, 19, 26, 33).index(bi))
                sync.wait_ge(ccsem, 5)
                for t in range(HC):
                    i = ps * HC + t
                    r, lc = t // 2, (t % 2) * 128
                    for hx in range(2):
                        hidx = 2 * i + hx
                        FF = F[hidx % 2]
                        if t == 0:
                            FF.w(t2, sv(f"t.gu.{ps * NIB + NIB - 1}"))
                        FF.w(t2, sv(f"t.dnh.{hidx - 2}") if hidx >= 2 else 0)
                        FF.w(ccsem, 5 + hx)
                        so = (hidx % 2) * 2816
                        FF.op(sync.dma_start,
                              out=dwb[:, so:so + 2816].rearrange("p (c o) -> p c o", c=22),
                              in_=ccd_o2[hx][2816 * r:2816 * (r + 1), lc:lc + 128]
                                  .rearrange("(c p) o -> p c o", p=128), ser=False)
                        mk(f"sd.df.{hidx}", FF.ctr)
            O = E(sync, outs, dma=True)
            yv = y.rearrange("(c p) t -> p c t", p=128)
            for ps_ in range(2):
                for hc_ in range(HC):
                    O.w(v2, sv(f"v.res.{ps_}.{hc_}"))
                    O.op(sync.dma_start, out=yv[:, hc_, TP * ps_:TP * (ps_ + 1)],
                         in_=x_res[:, T * hc_ + TP * ps_:T * hc_ + TP * (ps_ + 1)],
                         inc=(outs, OUT), ser=False)
            sync.wait_ge(outs, OUT.v)

        @block.gpsimd
        def _(g):
            GD = E(g, gd, dma=True)
            GD.op(g.dma_start, out=lnho[:].rearrange("p (c t) -> p c t", c=HC),
                  in_=xT.rearrange("(c p) t -> p c t", p=128), ser=False)
            mk("gd.xb", GD.ctr)
            W = [E(g, wb0, dma=True), E(g, wb1, dma=True)]
            GP = E(g, gp2)
            g.wait_ge(sd, sv("sd.consts"))
            dprev = [0, []]

            def wb_load(sl):
                wname, c0, kind = PROJ_SLICES[sl]
                WW = W[sl % 2]
                WW.w(t2, sv(f"t.wbfree.{sl}"))
                so = (sl % 2) * HC * 256
                WW.op(g.dma_start, out=wbuf[:, so:so + HC * 256].rearrange("p (c o) -> p c o", c=HC),
                      in_=w_in[wname][:, c0:c0 + 256].rearrange("(c p) o -> p c o", p=128),
                      ser=False)
                mk(f"gd.wb.{sl}", WW.ctr)

            for sl in range(2):
                wb_load(sl)
            wb_load(2)
            wb_load(3)
            wb_load(4)
            wb_load(5)
            g.wait_ge(cst, sv("sd.ccsst"))
            g.collective_compute("AllGather", AOT.bypass, ins=[ccs_i], outs=[ccs_o],
                                 replica_groups=RG).then_inc(ccsem, 1)
            wb_load(6)
            wb_load(7)
            wb_load(8)
            wb_load(9)
            g.wait_ge(css, sv("sd.ccgst.3"))
            g.collective_compute("AllGather", AOT.bypass, ins=[ccg_i[0]], outs=[ccg_o[0]],
                                 replica_groups=RG).then_inc(ccsem, 1)
            wb_load(10)
            wb_load(11)
            g.wait_ge(css, sv("sd.ccgst.7"))
            g.collective_compute("AllGather", AOT.bypass, ins=[ccg_i[1]], outs=[ccg_o[1]],
                                 replica_groups=RG).then_inc(ccsem, 1)
            g.wait_ge(css, sv("sd.ccgst.10"))
            g.collective_compute("AllGather", AOT.bypass, ins=[ccg_i[2]], outs=[ccg_o[2]],
                                 replica_groups=RG).then_inc(ccsem, 1)
            V = [E(g, s_, dma=True) for s_ in wos]
            p1slot = [RC[:, 4096:6144], RC[:, 6144:8192], RA[:, 0:2048], RA[:, 2048:4096]]
            p1wait = ["t.wbfree.13", "t.wbfree.13", "t.scanT.0.3", "t.scanT.1.3"]
            for pw in range(2):
                for t in range(HC):
                    VV = V[4 * pw + t % 4]
                    if pw == 0:
                        VV.w(t2, sv(p1wait[t]) if t < 4 else sv(f"t.wo.{t - 4}.0"))
                        dst = p1slot[t % 4]
                    else:
                        VV.w(t2, sv(f"t.rbc.{8 * t + 7}") if t < 4 else sv(f"t.wo.{t - 4}.1"))
                        dst = RB[:, 2048 * (t % 4):2048 * (t % 4 + 1)]
                    VV.op(g.dma_start, out=dst.rearrange("p (c o) -> p c o", c=HC),
                          in_=woT[:, 128 * t:128 * (t + 1)].rearrange("(c p) o -> p c o", p=128),
                          ser=False)
                    mk(f"gd.wo{pw}.{t}", VV.ctr)
            g.wait_ge(css, sv("sd.ccdst.2"))
            g.collective_compute("AllGather", AOT.bypass, ins=[ccd_i2[0]], outs=[ccd_o2[0]],
                                 replica_groups=RG).then_inc(ccsem, 1)
            g.wait_ge(css, sv("sd.ccdst.3"))
            g.collective_compute("AllGather", AOT.bypass, ins=[ccd_i2[1]], outs=[ccd_o2[1]],
                                 replica_groups=RG).then_inc(ccsem, 1)

        @block.scalar
        def _(s):
            S = E(s, ssem)
            S2 = Ctr()

            def sop(fn, *a, m=None, **kw):
                S.op(fn, *a, inc=(s2, S2) if m else None, **kw)
                if m:
                    mk(m, S2)

            def sprobe(names):
                if probe in names:
                    s.wait_ge(v2, sv("v.probe"))
                    psrc, ncols = probe_src[probe]
                    if psrc.dtype == F32:
                        s.dma_start(out=prb[:, 0:ncols],
                                    in_=psrc[:, 0:ncols]).then_inc(prd, 16)
                    else:
                        s.dma_start(out=prb[:].bitcast(BF16)[:, 0:ncols],
                                    in_=psrc[:, 0:ncols]).then_inc(prd, 16)
                    s.wait_ge(prd, 16)

            s.wait_ge(sd, sv("sd.consts"))
            s.wait_ge(gd, sv("gd.xb"))
            for hc in range(HC):
                if hc:
                    S.w(t2, sv(f"t.sqc0.{hc - 1}"))
                sop(s.activation, sq_t[:, 0:1024], lnho[:, T * hc:T * (hc + 1)], ACT.Square,
                    m=f"s.sq0.{hc}")
            S.w(v2, sv("v.var1"))
            sop(s.activation, dent, var_f[:, 0:512], ACT.Ln, bias=EPS_LN)
            sop(s.activation, rs_s[:, 0:512], dent, ACT.Exp, scale=-0.5)
            sop(s.activation, dent, var_f[:, 512:1024], ACT.Ln, bias=EPS_LN)
            sop(s.activation, rs_s[:, 512:1024], dent, ACT.Exp, scale=-0.5, m="s.rstd1")
            sprobe(("xln1",))
            def pev(sl):
                wname, c0, kind = PROJ_SLICES[sl]
                for ch in range(NCH):
                    S.w(t2, sv(f"t.proj.{sl}.{ch}"))
                    src = (pA, pB, pC, pE)[ch % 4]
                    if kind == 0:
                        dst = kv_tm[:, 1024 * ch + c0:1024 * ch + c0 + 256]
                    elif kind == 1:
                        dst = kv_tm[:, 1024 * ch + 512 + c0:1024 * ch + 512 + c0 + 256]
                    else:
                        dst = q_tm[:, DM * ch + c0:DM * ch + c0 + 256]
                    sop(s.copy, dst, src[:, 0:256], m=f"s.pev.{sl}.{ch}")

            for sl in range(4):
                pev(sl)
            for ch in range(NCH):
                S.w(v2, sv(f"v.ropek.{ch}"))
                sop(s.activation, rt(1, 512), rt(0, 512), ACT.Exp, m=f"s.expk.{ch}")
            sprobe(("kvtm", "deqg"))
            for sl in range(4, 12):
                pev(sl)
            for ch in range(2):
                for hf in range(2):
                    S.w(v2, sv(f"v.ropeq.{ch}.{hf}"))
                    sop(s.activation, rt(3), rt(2), ACT.Exp, m=f"s.expq.{ch}.{hf}")
            sprobe(("qtm", "stot", "sglob"))
            for ch in range(NCH):
                for kv in range(NKV):
                    S.w(t2, sv(f"t.den.{ch}.{kv}"))
                    sop(s.activation, dent, pE[0:1, :], ACT.Ln, bias=EPS_ATT)
                    sop(s.activation, rden, dent, ACT.Exp, scale=-1.0, m=f"s.rden.{ch}.{kv}")
                    if ch + 2 < NCH and kv in (1, 2):
                        S.w(v2, sv(f"v.ropeq.{ch + 2}.{kv - 1}"))
                        sop(s.activation, rt(3), rt(2), ACT.Exp,
                            m=f"s.expq.{ch + 2}.{kv - 1}")
            sprobe(("headout", "sloc", "xmid"))
            for hc in range(HC):
                S.w(v2, sv(f"v.ln2cast.{hc}"))
                if hc:
                    S.w(t2, sv(f"t.sqc2.{hc - 1}"))
                sop(s.activation, sq_t[:, 0:1024], lnho[:, T * hc:T * (hc + 1)], ACT.Square,
                    m=f"s.sq2.{hc}")
            S.w(v2, sv("v.var2"))
            sop(s.activation, dent, var_f[:, 0:512], ACT.Ln, bias=EPS_LN)
            sop(s.activation, rs_s[:, 0:512], dent, ACT.Exp, scale=-0.5)
            sop(s.activation, dent, var_f[:, 512:1024], ACT.Ln, bias=EPS_LN)
            sop(s.activation, rs_s[:, 512:1024], dent, ACT.Exp, scale=-0.5, m="s.rstd2")
            sprobe(("xln2",))
            for ps in range(2):
                for bi in range(NIB):
                    i = ps * NIB + bi
                    S.w(t2, sv(f"t.gu.{i}"))
                    if i:
                        S.w(v2, sv(f"v.hact.{i - 1}"))
                    gb = (pA, pC)[bi % 2]
                    sop(s.activation, sq_t[:, 0:TP], gb[:], ACT.Silu, m=f"s.silu.{i}")
                if ps == 0:
                    sprobe(("hact",))
                for t in range(HC):
                    i2 = ps * HC + t
                    S.w(t2, sv(f"t.dn.{i2}"))
                    if i2:
                        S.w(t2, sv(f"t.rmsstat.{i2 - 1}"))
                    db = (pA, pC)[t % 2]
                    sop(s.copy, lnho[:, T * t + TP * ps:T * t + TP * (ps + 1)], db[:])
                    sop(s.activation, sq_t[:, 0:TP], db[:], ACT.Square, m=f"s.dnev.{i2}")
                S.w(t2, sv(f"t.rmss.{ps}"))
                sop(s.activation, dent, pZ[0:1, :], ACT.Ln, scale=1.0 / DM, bias=EPS_RMS)
                sop(s.activation, rden, dent, ACT.Exp, scale=-0.5, m=f"s.rms.{ps}")
                S.w(t2, sv(f"t.rmsbc.{ps}"))
                if ps == 1:
                    sprobe(("yb",))

        @block.vector
        def _(v):
            V = E(v, vsem)
            V2 = Ctr()

            def vop(fn, *a, m=None, **kw):
                V.op(fn, *a, inc=(v2, V2) if m else None, **kw)
                if m:
                    mk(m, V2)

            def pmark(name):
                if V.last is not None:
                    v.wait_ge(*V.last)
                v.sem_inc(v2, 1)
                V2.v += 1
                mk(name, V2)
                V.last = (v2, V2.v)
                if name == "v.probe":
                    v.wait_ge(prd, 16)

            vop(v.memset, ones_c[:], 1.0)
            vop(v.memset, ones_r[:], 1.0, m="v.ones")

            v.wait_ge(t2, sv("t.ln1stats"))
            vop(v.tensor_scalar, mu_f[:, 0:512], pC[0:1, :], 1.0 / DM, None, AOT.mult)
            vop(v.tensor_scalar, mu_f[:, 512:1024], pE[0:1, :], 1.0 / DM, None, AOT.mult)
            vop(v.tensor_scalar, var_f[:, 0:512], pZ[0:1, :], 1.0 / DM, None, AOT.mult)
            vop(v.tensor_scalar, var_f[:, 512:1024], pS[0:1, :], 1.0 / DM, None, AOT.mult)
            vop(v.tensor_tensor, z_tot[:].bitcast(F32) if False else lnscr,
                mu_f[:, 0:512], mu_f[:, 0:512], AOT.mult)
            vop(v.tensor_tensor, var_f[:, 0:512], var_f[:, 0:512], lnscr,
                AOT.subtract)
            vop(v.tensor_tensor, lnscr, mu_f[:, 512:1024], mu_f[:, 512:1024],
                AOT.mult)
            vop(v.tensor_tensor, var_f[:, 512:1024], var_f[:, 512:1024], lnscr,
                AOT.subtract, m="v.var1")
            V.w(s2, sv("s.rstd1"))
            vop(v.tensor_tensor, mu_s[:], mu_f[:], rs_s[:], AOT.mult, m="v.murstd1")
            v.wait_ge(t2, sv("t.ln1bc"))
            v.wait_ge(sd, sv("sd.consts"))
            for hc in range(HC):
                for half in range(2):
                    rb = (pC, pE)[half]; mb = (pZ, pS)[half]
                    c0 = T * hc + 512 * half
                    vop(v.tensor_tensor, dstg[:, 0:512], x_res[:, c0:c0 + 512], rb[:], AOT.mult)
                    vop(v.tensor_tensor, lnho[:, c0:c0 + 512], dstg[:, 0:512], mb[:],
                        AOT.subtract)
            pmark("v.ln1done")
            if probe == "xln1":
                pmark("v.probe")

            vqbatch = [0]
            def deq_piece(nu16, src_off, scal_ap, ngrp, grp0, dv_off, dvb=None):
                if dvb is None:
                    dvb = dvals
                for j in range(4):
                    w = dqw[:, src_off:src_off + nu16]
                    vop(v.tensor_scalar, dt1[:, 0:nu16], w, 4 * j, 7,
                        AOT.logical_shift_right, AOT.bitwise_and)
                    if 12 - 4 * j >= 0:
                        vop(v.tensor_scalar, dt2[:, 0:nu16], w, 12 - 4 * j, 0x8000,
                            AOT.logical_shift_left, AOT.bitwise_and)
                    else:
                        vop(v.tensor_scalar, dt2[:, 0:nu16], w, 4 * j - 12, 0x8000,
                            AOT.logical_shift_right, AOT.bitwise_and)
                    vop(v.tensor_scalar, dt3[:, 0:nu16], dt1[:, 0:nu16], 64.0, 16128.0,
                        AOT.mult, AOT.add)
                    vop(v.tensor_scalar, dt4[:, 0:nu16], dt3[:, 0:nu16], 2.0, -16256.0,
                        AOT.mult, AOT.add)
                    vop(v.tensor_tensor, dt3[:, 0:nu16], dt3[:, 0:nu16], dt4[:, 0:nu16],
                        AOT.min)
                    vop(v.tensor_single_scalar, dt4[:, 0:nu16], dt1[:, 0:nu16], 1.0, AOT.min)
                    vop(v.tensor_tensor, dt3[:, 0:nu16], dt3[:, 0:nu16], dt4[:, 0:nu16],
                        AOT.mult)
                    vop(v.tensor_tensor, dvb[:, dv_off + j:dv_off + 4 * nu16:4],
                        dt3[:, 0:nu16], dt2[:, 0:nu16], AOT.add)
                dv = dvb[:, dv_off:dv_off + 4 * nu16].bitcast(BF16)
                for gi in range(ngrp):
                    vop(v.tensor_scalar, dv[:, 128 * gi:128 * (gi + 1)],
                        dv[:, 128 * gi:128 * (gi + 1)],
                        scal_ap[:, grp0 + gi:grp0 + gi + 1], None, AOT.mult)

            def deq_gate_tile(t):
                v.wait_ge((pk0, pk1)[t % 2], sv(f"sd.pk.{t}"))
                if t >= 1:
                    v.wait_ge(t2, sv(f"t.deqT.{t - 1}.1"))
                vop(v.tensor_copy, gscf[:], gsc[:])
                deq_piece(512, 0, gscf, 16, 0, 0)
                pmark(f"v.deqfree.{t + 1}")
                pmark(f"v.deqval.{t}")
                if t >= 1:
                    v.wait_ge(css, sv(f"sd.ccgst.{t - 1}"))
                for hcb in range(2):
                    v.wait_ge(t2, sv(f"t.deqT.{t}.{hcb}"))
                    bank = (pT2, pT)[vqbatch[0] % 2][:].bitcast(BF16)
                    vqbatch[0] += 1
                    vop(v.tensor_copy, dstg[:, 1024 * hcb:1024 * (hcb + 1)], bank[:],
                        m=f"v.deqTev.{t}.{hcb}")
                pmark(f"v.deqstg.{t}")

            def deq_down_map(tt, hf):
                t = 11 + tt
                dd = 2 * tt + hf
                if hf == 0:
                    v.wait_ge((pk0, pk1)[t % 2], sv(f"sd.pk.{t}"))
                if dd >= 1:
                    v.wait_ge(t2, sv(f"t.deqTd.{dd - 1}.2"))
                else:
                    v.wait_ge(t2, sv("t.deqT.10.1"))
                if hf == 0:
                    vop(v.tensor_copy, dscalf[:], dscal[:])
                for sub in range(2):
                    deq_piece(352, 704 * hf + 352 * sub, dscalf, 11, 22 * hf + 11 * sub,
                              1408 * sub)
                if hf == 1:
                    pmark(f"v.deqfree.{t + 1}")
                pmark(f"v.deqvald.{dd}")

            def deq_down_evac_b(dd, hcb):
                if hcb == 0:
                    v.wait_ge(css, sv(f"sd.ccdst.{dd - 1}") if dd >= 1 else sv("sd.ccgst.10"))
                nb = (8, 8, 6)[hcb]
                v.wait_ge(t2, sv(f"t.deqTd.{dd}.{hcb}"))
                bank = (pT2, pT)[vqbatch[0] % 2][:].bitcast(BF16)
                vqbatch[0] += 1
                vop(v.tensor_copy, dstg[:, 1024 * hcb:1024 * hcb + 128 * nb],
                    bank[:, 0:128 * nb], m=f"v.deqTevd.{dd}.{hcb}")
                if hcb == 2:
                    pmark(f"v.deqstg.{11 + dd}")

            deq_gate_tile(0)
            deq_gate_tile(1)

            def rope(dst_ap, n, ch, alt=False):
                hv = dst_ap.rearrange("p (h d) -> p h d", h=n)
                lo = hv[:, :, 0:64]; hi = hv[:, :, 64:128]
                cl = cos_s[:, 128 * ch:128 * ch + 64].rearrange("p d -> p () d").broadcast_to([128, n, 64])
                cn = cos_s[:, 128 * ch + 64:128 * (ch + 1)].rearrange("p d -> p () d").broadcast_to([128, n, 64])
                sl_ = sin_s[:, 128 * ch:128 * ch + 64].rearrange("p d -> p () d").broadcast_to([128, n, 64])
                sn = sin_s[:, 128 * ch + 64:128 * (ch + 1)].rearrange("p d -> p () d").broadcast_to([128, n, 64])
                m = n * 64
                b1 = dbuf[:, 0:m] if alt else rt(0, m)
                b2 = dbuf[:, 1024:1024 + m] if alt else rt(1, m)
                w1 = b1.rearrange("p (h d) -> p h d", h=n)
                w2 = b2.rearrange("p (h d) -> p h d", h=n)
                w3 = rt(2, m).rearrange("p (h d) -> p h d", h=n)
                w4 = rt(3, m).rearrange("p (h d) -> p h d", h=n)
                vop(v.tensor_tensor, w1, lo, cl, AOT.mult)
                vop(v.tensor_tensor, w2, hi, sl_, AOT.mult)
                vop(v.tensor_tensor, w3, hi, cn, AOT.mult)
                vop(v.tensor_tensor, w4, lo, sn, AOT.mult)
                vop(v.tensor_tensor, lo, w1, w2, AOT.subtract)
                vop(v.tensor_tensor, hi, w3, w4, AOT.add)

            for ch in range(NCH):
                V.w(s2, sv(f"s.pev.3.{ch}"))
                kap = kv_tm[:, 1024 * ch:1024 * ch + 512]
                rope(kap, 4, ch)
                vop(v.tensor_single_scalar, rt(0, 512), kap, 0.0, AOT.min, m=f"v.ropek.{ch}")
                V.w(s2, sv(f"s.expk.{ch}"))
                vop(v.scalar_tensor_tensor, kap, kap, 0.0, rt(1, 512), AOT.max, AOT.add,
                    m=f"v.eluk.{ch}")
            if probe == "kvtm":
                pmark("v.probe")

            vop(v.memset, s_cum[:], 0.0)
            vop(v.memset, z_cum[:], 0.0)
            for ch in range(NCH):
                v.wait_ge(t2, sv(f"t.stacc.{ch}"))
                vop(v.tensor_tensor, s_cum[:], s_cum[:], pS[:], AOT.add)
                vop(v.tensor_copy, s_loc[:, SLOC0 + 512 * ch:SLOC0 + 512 * (ch + 1)], s_cum[:])
                vop(v.tensor_tensor, z_cum[:, 0:4], z_cum[:, 0:4], pZ[:, 0:4], AOT.add)
                vop(v.tensor_copy, z_loc[:, 8 * ch:8 * ch + 4], z_cum[:, 0:4],
                    m=f"v.stev.{ch}")
            pmark("v.stot")

            for t in range(2, 11):
                deq_gate_tile(t)
            if probe == "deqg":
                pmark("v.probe")

            def q_rope(ch, alt):
                V.w(s2, sv(f"s.pev.11.{ch}"))
                rope(q_tm[:, DM * ch:DM * (ch + 1)], 16, ch, alt=alt)

            def q_elu(ch, hf):
                qh = q_tm[:, DM * ch + 1024 * hf:DM * ch + 1024 * (hf + 1)]
                vop(v.tensor_single_scalar, rt(2), qh, 0.0, AOT.min,
                    m=f"v.ropeq.{ch}.{hf}")
                V.w(s2, sv(f"s.expq.{ch}.{hf}"))
                vop(v.scalar_tensor_tensor, qh, qh, 0.0, rt(3), AOT.max, AOT.add,
                    m=f"v.eluq.{ch}.{hf}")

            v.wait_ge(cst, 32)
            for ch in range(2):
                q_rope(ch, False)
                q_elu(ch, 0)
                q_elu(ch, 1)
            if probe == "qtm":
                pmark("v.probe")

            vop(v.memset, s_glob[:], 0.0)
            vop(v.memset, z_glob[:], 0.0)
            for r in range(8):
                v.wait_ge((st_0, st_1)[r % 2], sv(f"sd.stin.{r}"))
                so = 516 * (r % 2)
                vop(v.scalar_tensor_tensor, s_glob[:], stin[:, so:so + 512],
                    pmk[:, r:r + 1], s_glob[:], AOT.mult, AOT.add)
                vop(v.scalar_tensor_tensor, z_glob[:, 0:4], stin[:, so + 512:so + 516],
                    pmk[:, r:r + 1], z_glob[:, 0:4], AOT.mult, AOT.add,
                    m=f"v.stfree.{r + 2}")
            pmark("v.sglob")
            if probe == "sglob":
                pmark("v.probe")
            if probe == "stot":
                pmark("v.probe")

            def v_seff(ch):
                se = s_eff[:, 512 * (ch % 2):512 * (ch % 2 + 1)]
                ze = z_eff[:, 8 * (ch % 2):8 * (ch % 2) + 4]
                if ch == 0:
                    vop(v.tensor_scalar, se, s_glob[:], 1.0, None, AOT.mult)
                    vop(v.tensor_scalar, ze, z_glob[:, 0:4], 1.0, None, AOT.mult,
                        m=f"v.seff.{ch}")
                else:
                    vop(v.tensor_tensor, se, s_glob[:],
                        s_loc[:, SLOC0 + 512 * (ch - 1):SLOC0 + 512 * ch], AOT.add)
                    vop(v.tensor_tensor, ze, z_glob[:, 0:4],
                        z_loc[:, 8 * (ch - 1):8 * (ch - 1) + 4], AOT.add, m=f"v.seff.{ch}")

            def v_qfkf_a4m(idx):
                ch, kv = idx // NKV, idx % NKV
                par = idx % 2
                if kv == 0:
                    v_seff(ch)
                pTx = (pT, pT2)[par][:].bitcast(BF16)
                v.wait_ge(t2, sv(f"t.scanT.{ch}.{kv}"))
                for j in range(4):
                    vop(v.tensor_copy,
                        qf_fm[:, 512 * par + 128 * j:512 * par + 128 * (j + 1)],
                        pTx[:, 128 * j:128 * (j + 1)])
                vop(v.tensor_copy, kf_fm[:, 128 * par:128 * (par + 1)], pTx[:, 512:640],
                    m=f"v.qfkf.{ch}.{kv}")
                v.wait_ge(t2, sv(f"t.a4.{ch}.{kv}"))
                vop(v.tensor_tensor,
                    a4m[:, 512 * par:512 * (par + 1)].rearrange("p (h t) -> p h t", h=4),
                    pA[:].rearrange("p (h t) -> p h t", h=4),
                    trilm[:].rearrange("p t -> p () t").broadcast_to([128, 4, 128]),
                    AOT.mult, m=f"v.a4m.{ch}.{kv}")

            def v_odiv(idx):
                ch, kv = idx // NKV, idx % NKV
                v.wait_ge(t2, sv(f"t.rbc.{idx}"))
                vop(v.tensor_copy, rdbc, pC[:])
                vop(v.tensor_tensor,
                    lnho[:].rearrange("p (h t) -> p h t", h=HC)[:, 4 * kv:4 * (kv + 1),
                                                                128 * ch:128 * (ch + 1)],
                    pB[:].rearrange("p (h t) -> p h t", h=4),
                    rdbc.rearrange("p (h t) -> p h t", h=4),
                    AOT.mult, m=f"v.odiv.{ch}.{kv}")

            def woev_h0(ot):
                v.wait_ge(t2, sv(f"t.wo.{ot}.0"))
                src = (pS, pZ)[ot % 2]
                c0 = T * ot
                vop(v.tensor_tensor, x_res[:, c0:c0 + 512], x_res[:, c0:c0 + 512],
                    src[:], AOT.add, m=f"v.woevh0.{ot}")

            v_qfkf_a4m(0)
            for idx in range(NCH * NKV):
                if idx + 1 < NCH * NKV:
                    v_qfkf_a4m(idx + 1)
                v_odiv(idx)
                if idx % 4 == 0 and 2 + idx // 4 < NCH:
                    q_rope(2 + idx // 4, True)
                if idx % 4 == 1 and 2 + idx // 4 < NCH:
                    q_elu(2 + idx // 4, 0)
                if idx % 4 == 2 and 2 + idx // 4 < NCH:
                    q_elu(2 + idx // 4, 1)
                if idx >= 17:
                    woev_h0(idx - 17)
            woev_h0(15)
            pmark("v.scandone")
            if probe == "headout":
                pmark("v.probe")
            if probe == "sloc":
                pmark("v.probe")

            for ot in range(HC):
                v.wait_ge(t2, sv(f"t.wo.{ot}.1"))
                src = (pA, pB)[ot % 2]
                c0 = T * ot + 512
                vop(v.tensor_tensor, x_res[:, c0:c0 + 512], x_res[:, c0:c0 + 512],
                    src[:], AOT.add, m=f"v.woev.{ot}.1")
            if probe == "xmid":
                pmark("v.probe")

            for hc in range(HC):
                vop(v.tensor_copy, lnho[:, T * hc:T * (hc + 1)], x_res[:, T * hc:T * (hc + 1)],
                    m=f"v.ln2cast.{hc}")
            v.wait_ge(t2, sv("t.ln2stats"))
            v.wait_ge(css, sv("sd.ccgst.10"))
            vop(v.tensor_scalar, mu_f[:, 0:512], pC[0:1, :], 1.0 / DM, None, AOT.mult)
            vop(v.tensor_scalar, mu_f[:, 512:1024], pE[0:1, :], 1.0 / DM, None, AOT.mult)
            vop(v.tensor_scalar, var_f[:, 0:512], pZ[0:1, :], 1.0 / DM, None, AOT.mult)
            vop(v.tensor_scalar, var_f[:, 512:1024], pS[0:1, :], 1.0 / DM, None, AOT.mult)
            vop(v.tensor_tensor, lnscr, mu_f[:, 0:512], mu_f[:, 0:512], AOT.mult)
            vop(v.tensor_tensor, var_f[:, 0:512], var_f[:, 0:512], lnscr,
                AOT.subtract)
            vop(v.tensor_tensor, lnscr, mu_f[:, 512:1024], mu_f[:, 512:1024],
                AOT.mult)
            vop(v.tensor_tensor, var_f[:, 512:1024], var_f[:, 512:1024], lnscr,
                AOT.subtract, m="v.var2")
            V.w(s2, sv("s.rstd2"))
            vop(v.tensor_tensor, mu_s[:], mu_f[:], rs_s[:], AOT.mult, m="v.murstd2")
            v.wait_ge(t2, sv("t.ln2bc"))
            for half in range(2):
                for hc in range(HC):
                    rb = (pC, pZ)[half]; mb = (pE, pS)[half]
                    c0 = T * hc + 512 * half
                    vop(v.tensor_tensor, dstg[:, 0:512], x_res[:, c0:c0 + 512], rb[:], AOT.mult)
                    vop(v.tensor_tensor, lnho[:, c0:c0 + 512], dstg[:, 0:512], mb[:],
                        AOT.subtract, m=f"v.ln2a.{hc}" if half == 0 else None)
                if half == 0:
                    pmark("v.ln2done0")
            pmark("v.ln2done")
            if probe == "xln2":
                pmark("v.probe")

            for ps in range(2):
                for bi, (k, r, lb) in enumerate(FFN_ORDER):
                    i = ps * NIB + bi
                    gI = 11 * r + lb
                    V.w(s2, sv(f"s.silu.{i}"))
                    ub = (pB, pE)[bi % 2]
                    vop(v.tensor_tensor, hact(gI), sq_t[:, 0:TP], ub[:], AOT.mult,
                        m=f"v.hact.{i}")
                    if ps == 0:
                        if i == 2:
                            deq_down_map(0, 0)
                        if i in (7, 14, 21, 28):
                            dd = (7, 14, 21, 28).index(i)
                            deq_down_evac_b(dd, 0)
                            deq_down_evac_b(dd, 1)
                        if i in (9, 16, 23, 30):
                            dd = (9, 16, 23, 30).index(i)
                            deq_down_evac_b(dd, 2)
                            if dd < 3:
                                deq_down_map((dd + 1) // 2, (dd + 1) % 2)
                if probe == "hact" and ps == 0:
                    pmark("v.probe")
                v.wait_ge(t2, sv(f"t.rmsbc.{ps}"))
                for hc in range(HC):
                    c0 = T * hc + TP * ps
                    vop(v.tensor_tensor, dstg[:, 0:512], lnho[:, c0:c0 + 512], pS[:], AOT.mult)
                    vop(v.tensor_tensor, x_res[:, c0:c0 + 512], x_res[:, c0:c0 + 512],
                        dstg[:, 0:512], AOT.add, m=f"v.res.{ps}.{hc}")
                if probe == "yb" and ps == 1:
                    pmark("v.probe")
            pmark("v.final")

        @block.tensor
        def _(t):
            TE = E(t, tsem)
            T2 = Ctr()

            def top(fn, *a, m=None, **kw):
                TE.op(fn, *a, inc=(t2, T2) if m else None, ser=False, **kw)
                if m:
                    mk(m, T2)

            t.wait_ge(gd, sv("gd.xb"))
            t.wait_ge(sd, sv("sd.consts"))
            t.wait_ge(v2, sv("v.ones"))
            for hc in range(HC):
                t.wait_ge(s2, sv(f"s.sq0.{hc}"))
                st, sp = (hc == 0), (hc == HC - 1)
                top(t.matmul, pC[0:1, :], ones_c[:], lnho[:, T * hc:T * hc + 512],
                    start=st, stop=sp)
                top(t.matmul, pE[0:1, :], ones_c[:], lnho[:, T * hc + 512:T * (hc + 1)],
                    start=st, stop=sp)
                top(t.matmul, pZ[0:1, :], ones_c[:], sq_t[:, 0:512], start=st, stop=sp)
                top(t.matmul, pS[0:1, :], ones_c[:], sq_t[:, 512:1024], start=st, stop=sp,
                    m=f"t.sqc0.{hc}")
            mk("t.ln1stats", T2)
            t.wait_ge(v2, sv("v.murstd1"))
            top(t.matmul, pC[:], ones_r[:], rs_s[:, 0:512], start=True, stop=True)
            top(t.matmul, pE[:], ones_r[:], rs_s[:, 512:1024], start=True, stop=True)
            top(t.matmul, pZ[:], ones_r[:], mu_s[:, 0:512], start=True, stop=True)
            top(t.matmul, pS[:], ones_r[:], mu_s[:, 512:1024], start=True, stop=True,
                m="t.ln1bc")

            dqbatch = [0]
            dqevs = []

            def deq_T(ti):
                t.wait_ge(v2, sv(f"v.deqval.{ti}"))
                tsrc = dvals
                for hcb in range(2):
                    bank = (pT2, pT)[dqbatch[0] % 2][:].bitcast(BF16)
                    if dqbatch[0] >= 2:
                        t.wait_ge(v2, sv(dqevs[dqbatch[0] - 2]))
                    dqbatch[0] += 1
                    for qq in range(8):
                        hcx = 8 * hcb + qq
                        top(t.transpose, bank[:, 128 * qq:128 * (qq + 1)],
                            tsrc[:, 128 * hcx:128 * (hcx + 1)].bitcast(BF16), ident[:],
                            m=f"t.deqT.{ti}.{hcb}" if qq == 7 else None)
                    dqevs.append(f"v.deqTev.{ti}.{hcb}")

            def deq_Td_b(dd, hcb):
                if hcb == 0:
                    t.wait_ge(v2, sv(f"v.deqvald.{dd}"))
                nb = (8, 8, 6)[hcb]
                bank = (pT2, pT)[dqbatch[0] % 2][:].bitcast(BF16)
                if dqbatch[0] >= 2:
                    t.wait_ge(v2, sv(dqevs[dqbatch[0] - 2]))
                dqbatch[0] += 1
                for qq in range(nb):
                    icx = 8 * hcb + qq
                    top(t.transpose, bank[:, 128 * qq:128 * (qq + 1)],
                        dvals[:, 128 * icx:128 * (icx + 1)].bitcast(BF16), ident[:],
                        m=f"t.deqTd.{dd}.{hcb}" if qq == nb - 1 else None)
                dqevs.append(f"v.deqTevd.{dd}.{hcb}")

            def proj_slice(sl):
                t.wait_ge((wb0, wb1)[sl % 2], sv(f"gd.wb.{sl}"))
                t.wait_ge(v2, sv("v.ln1done"))
                so = (sl % 2) * HC * 256
                for ch in range(NCH):
                    idx = sl * NCH + ch
                    if idx >= 4:
                        pr = idx - 4
                        t.wait_ge(s2, sv(f"s.pev.{pr // NCH}.{pr % NCH}"))
                    dstp = (pA, pB, pC, pE)[ch % 4]
                    for hc in range(HC):
                        top(t.matmul, dstp[:, 0:256],
                            lnho[:, T * hc + 128 * ch:T * hc + 128 * (ch + 1)],
                            wbuf[:, so + 256 * hc:so + 256 * (hc + 1)],
                            start=(hc == 0), stop=(hc == HC - 1),
                            m=f"t.proj.{sl}.{ch}" if hc == HC - 1 else None)
                mk(f"t.wbfree.{sl + 2}", T2)

            for sl in range(4):
                proj_slice(sl)
            deq_T(0)
            deq_T(1)
            for ch in range(NCH):
                t.wait_ge(v2, sv(f"v.eluk.{ch}"))
                if ch > 0:
                    t.wait_ge(v2, sv(f"v.stev.{ch - 1}"))
                for kv in range(NKV):
                    st = (kv == 0)
                    sp = (kv == NKV - 1)
                    top(t.matmul, pS[:, 128 * kv:128 * (kv + 1)],
                        kv_tm[:, 1024 * ch + 128 * kv:1024 * ch + 128 * (kv + 1)],
                        kv_tm[:, 1024 * ch + 512 + 128 * kv:1024 * ch + 512 + 128 * (kv + 1)],
                        start=st, stop=(kv == NKV - 1))
                    top(t.matmul, pZ[:, kv:kv + 1],
                        kv_tm[:, 1024 * ch + 128 * kv:1024 * ch + 128 * (kv + 1)],
                        ones_c[:], start=st, stop=sp,
                        m=f"t.stacc.{ch}" if kv == NKV - 1 else None)
            for sl in range(4, 12):
                proj_slice(sl)
                if sl - 2 <= 10:
                    deq_T(sl - 2)
            deq_T(10)

            def scan_T_A4(idx):
                ch, kv = idx // NKV, idx % NKV
                par = idx % 2
                t.wait_ge(v2, sv(f"v.eluq.{ch}.{kv // 2}"))
                if idx >= 2:
                    t.wait_ge(v2, sv(f"v.qfkf.{(idx - 2) // NKV}.{(idx - 2) % NKV}"))
                pTx = (pT, pT2)[par][:].bitcast(BF16)
                for j in range(4):
                    h = 4 * kv + j
                    top(t.transpose, pTx[:, 128 * j:128 * (j + 1)],
                        q_tm[:, DM * ch + 128 * h:DM * ch + 128 * (h + 1)], ident[:])
                top(t.transpose, pTx[:, 512:640],
                    kv_tm[:, 1024 * ch + 128 * kv:1024 * ch + 128 * (kv + 1)], ident[:],
                    m=f"t.scanT.{ch}.{kv}")
                t.wait_ge(v2, sv(f"v.qfkf.{ch}.{kv}"))
                if idx >= 1:
                    t.wait_ge(v2, sv(f"v.a4m.{(idx - 1) // NKV}.{(idx - 1) % NKV}"))
                top(t.matmul, pA[:], kf_fm[:, 128 * par:128 * (par + 1)],
                    qf_fm[:, 512 * par:512 * (par + 1)], start=True, stop=True,
                    m=f"t.a4.{ch}.{kv}")

            def scan_rest(idx):
                ch, kv = idx // NKV, idx % NKV
                par = idx % 2
                t.wait_ge(v2, sv(f"v.a4m.{ch}.{kv}"))
                t.wait_ge(v2, sv(f"v.seff.{ch}"))
                if idx >= 1:
                    t.wait_ge(s2, sv(f"s.rden.{(idx - 1) // NKV}.{(idx - 1) % NKV}"))
                top(t.matmul, pE[0:1, :], ones_c[:], a4m[:, 512 * par:512 * (par + 1)],
                    start=True, stop=False)
                top(t.matmul, pE[0:1, :], z_eff[:, 8 * (ch % 2) + kv:8 * (ch % 2) + kv + 1],
                    qf_fm[:, 512 * par:512 * (par + 1)], start=False, stop=True,
                    m=f"t.den.{ch}.{kv}")
                if idx >= 1:
                    t.wait_ge(v2, sv(f"v.odiv.{(idx - 1) // NKV}.{(idx - 1) % NKV}"))
                top(t.matmul, pB[:],
                    kv_tm[:, 1024 * ch + 512 + 128 * kv:1024 * ch + 512 + 128 * (kv + 1)],
                    a4m[:, 512 * par:512 * (par + 1)], start=True, stop=False)
                top(t.matmul, pB[:], s_eff[:, 512 * (ch % 2) + 128 * kv:512 * (ch % 2) + 128 * (kv + 1)],
                    qf_fm[:, 512 * par:512 * (par + 1)], start=False, stop=True)
                t.wait_ge(s2, sv(f"s.rden.{ch}.{kv}"))
                if idx >= 1:
                    t.wait_ge(v2, sv(f"v.odiv.{(idx - 1) // NKV}.{(idx - 1) % NKV}"))
                top(t.matmul, pC[:], ones_r[:], rden, start=True, stop=True,
                    m=f"t.rbc.{idx}")

            wo_p1slot = [RC[:, 4096:6144], RC[:, 6144:8192], RA[:, 0:2048], RA[:, 2048:4096]]

            def wo_half(ot, half):
                t.wait_ge(wos[4 * half + ot % 4], sv(f"gd.wo{half}.{ot}"))
                if half == 0:
                    if ot == 0:
                        t.wait_ge(v2, sv("v.odiv.3.3"))
                    if ot >= 2:
                        t.wait_ge(v2, sv(f"v.woevh0.{ot - 2}"))
                    dstp = (pS, pZ)[ot % 2]
                    wsl = wo_p1slot[ot % 4]
                else:
                    if ot == 0:
                        t.wait_ge(v2, sv(f"v.odiv.{NCH - 1}.{NKV - 1}"))
                    if ot >= 2:
                        t.wait_ge(v2, sv(f"v.woev.{ot - 2}.1"))
                    dstp = (pA, pB)[ot % 2]
                    wsl = RB[:, 2048 * (ot % 4):2048 * (ot % 4 + 1)]
                for hp in range(HC):
                    top(t.matmul, dstp[:],
                        wsl[:, 128 * hp:128 * (hp + 1)],
                        lnho[:, T * hp + 512 * half:T * hp + 512 * (half + 1)],
                        start=(hp == 0), stop=(hp == HC - 1),
                        m=f"t.wo.{ot}.{half}" if hp == HC - 1 else None)

            scan_T_A4(0)
            for idx in range(NCH * NKV):
                if idx + 1 < NCH * NKV:
                    scan_T_A4(idx + 1)
                scan_rest(idx)
                if idx >= 16:
                    wo_half(idx - 16, 0)

            for ot in range(HC):
                wo_half(ot, 1)
            for hc in range(HC):
                t.wait_ge(s2, sv(f"s.sq2.{hc}"))
                st, sp = (hc == 0), (hc == HC - 1)
                top(t.matmul, pC[0:1, :], ones_c[:], lnho[:, T * hc:T * hc + 512],
                    start=st, stop=sp)
                top(t.matmul, pE[0:1, :], ones_c[:], lnho[:, T * hc + 512:T * (hc + 1)],
                    start=st, stop=sp)
                top(t.matmul, pZ[0:1, :], ones_c[:], sq_t[:, 0:512], start=st, stop=sp)
                top(t.matmul, pS[0:1, :], ones_c[:], sq_t[:, 512:1024], start=st, stop=sp,
                    m=f"t.sqc2.{hc}")
            mk("t.ln2stats", T2)
            t.wait_ge(v2, sv("v.murstd2"))
            top(t.matmul, pC[:], ones_r[:], rs_s[:, 0:512], start=True, stop=True)
            top(t.matmul, pZ[:], ones_r[:], rs_s[:, 512:1024], start=True, stop=True)
            top(t.matmul, pE[:], ones_r[:], mu_s[:, 0:512], start=True, stop=True)
            top(t.matmul, pS[:], ones_r[:], mu_s[:, 512:1024], start=True, stop=True,
                m="t.ln2bc")

            for ps in range(2):
                if ps == 1:
                    t.wait_ge(v2, sv("v.ln2done"))
                for bi, (k, r, lb) in enumerate(FFN_ORDER):
                    i = ps * NIB + bi
                    t.wait_ge((gf0, gf1, gf2)[i % 3], sv(f"sd.gf.{i}"))
                    if i >= 2:
                        t.wait_ge(v2, sv(f"v.hact.{i - 2}"))
                    gb = (pA, pC)[bi % 2]; ub = (pB, pE)[bi % 2]
                    if ps == 0 and bi == 1:
                        t.wait_ge(v2, sv("v.ln2done0"))
                    fw = ffnw_ap(i)
                    for hc in range(HC):
                        st, sp = (hc == 0), (hc == HC - 1)
                        if ps == 0 and bi == 0:
                            t.wait_ge(v2, sv(f"v.ln2a.{hc}"))
                        top(t.matmul, gb[:], fw[:, 128 * hc:128 * (hc + 1)],
                            lnho[:, T * hc + TP * ps:T * hc + TP * (ps + 1)],
                            start=st, stop=sp)
                        top(t.matmul, ub[:],
                            fw[:, DM + 128 * hc:DM + 128 * (hc + 1)],
                            lnho[:, T * hc + TP * ps:T * hc + TP * (ps + 1)],
                            start=st, stop=sp, m=f"t.gu.{i}" if sp else None)
                    mk(f"t.gffree.{i + 3}", T2)
                    if ps == 0:
                        if bi in (6, 13, 20, 27):
                            dd = (6, 13, 20, 27).index(bi)
                            deq_Td_b(dd, 0)
                            deq_Td_b(dd, 1)
                        if bi in (8, 15, 22, 29):
                            deq_Td_b((8, 15, 22, 29).index(bi), 2)
                for tt_ in range(HC):
                    i2 = ps * HC + tt_
                    t.wait_ge(v2, sv(f"v.hact.{ps * NIB + NIB - 1}"))
                    if i2 >= 2:
                        t.wait_ge(s2, sv(f"s.dnev.{i2 - 2}"))
                    db = (pA, pC)[tt_ % 2]
                    for hx in range(2):
                        hidx = 2 * i2 + hx
                        t.wait_ge((df0, df1)[hidx % 2], sv(f"sd.df.{hidx}"))
                        so = (hidx % 2) * 2816
                        for icl in range(22):
                            ic = 22 * hx + icl
                            top(t.matmul, db[:], dwb[:, so + 128 * icl:so + 128 * (icl + 1)],
                                hact(ic), start=(ic == 0), stop=(ic == NIB - 1),
                                m=f"t.dnh.{hidx}" if icl == 21 else None)
                    mk(f"t.dn.{i2}", T2)
                    t.wait_ge(s2, sv(f"s.dnev.{i2}"))
                    top(t.matmul, pZ[0:1, :], ones_c[:], sq_t[:, 0:TP],
                        start=(tt_ == 0), stop=(tt_ == HC - 1), m=f"t.rmsstat.{i2}")
                mk(f"t.rmss.{ps}", T2)
                t.wait_ge(s2, sv(f"s.rms.{ps}"))
                top(t.matmul, pS[:], ones_r[:], rden, start=True, stop=True,
                    m=f"t.rmsbc.{ps}")

    return nc, marks


def build2(probe=None):
    _, marks = build({}, probe=probe)
    nc, _ = build(marks, probe=probe)
    return nc, marks


def _stage_inputs(L0_attn_x, L0_attn_Wq, L0_attn_Wk, L0_attn_Wv, L0_attn_Wo,
                  L0_ffn_gate_up_packed, L0_ffn_gate_up_scales,
                  L0_ffn_down_packed, L0_ffn_down_scales):
    import ml_dtypes
    bf = ml_dtypes.bfloat16
    x = np.ascontiguousarray(np.asarray(L0_attn_x, np.float32))
    wqT = np.ascontiguousarray(np.asarray(L0_attn_Wq, np.float32).T)
    wkT = np.ascontiguousarray(np.asarray(L0_attn_Wk, np.float32).T)
    wvT = np.ascontiguousarray(np.asarray(L0_attn_Wv, np.float32).T)
    woT = np.ascontiguousarray(np.asarray(L0_attn_Wo, np.float32).T)
    gupk = np.asarray(L0_ffn_gate_up_packed, np.int32)
    gusT = np.ascontiguousarray(np.asarray(L0_ffn_gate_up_scales, np.float16).T)
    dwpk = np.asarray(L0_ffn_down_packed, np.int32)
    dwsT = np.ascontiguousarray(np.asarray(L0_ffn_down_scales, np.float16).T)
    identb = np.eye(128).astype(bf)
    tril = np.triu(np.ones((128, 128), np.float32))
    tril4 = np.ascontiguousarray(tril).astype(bf)
    constsb = np.tile(np.array([1e-5, 1e-6, -0.5, -1.0, 1.0 / 2048.0, 0.5, 2.0, 1.0],
                               np.float32), (128, 1))
    in_maps = []
    for c in range(8):
        b, q = c // 4, c % 4
        xT = np.ascontiguousarray(x[b, 1024 * q:1024 * (q + 1), :].T)
        pos = 1024 * q + np.arange(1024, dtype=np.float64)
        invf = 10000.0 ** (-np.arange(0, 128, 2, dtype=np.float64) / 128.0)
        fr = np.outer(pos, invf)
        emb = np.concatenate([fr, fr], axis=1)
        cosb = np.cos(emb).astype(np.float32).astype(bf).reshape(8, 128, 128)
        sinb = np.sin(emb).astype(np.float32).astype(bf).reshape(8, 128, 128)
        pm = np.zeros((128, 8), np.float32)
        for r in range(8):
            if r // 4 == b and r % 4 < q:
                pm[:, r] = 1.0
        in_maps.append({
            "xT": xT, "wqT": wqT, "wkT": wkT, "wvT": wvT, "woT": woT,
            "gup": np.ascontiguousarray(gupk[1408 * c:1408 * (c + 1)]),
            "gus": np.ascontiguousarray(gusT[1408 * c:1408 * (c + 1)]),
            "dwp": np.ascontiguousarray(dwpk[256 * c:256 * (c + 1)]),
            "dws": np.ascontiguousarray(dwsT[256 * c:256 * (c + 1)]),
            "cosb": np.ascontiguousarray(cosb), "sinb": np.ascontiguousarray(sinb),
            "identb": identb, "tril4b": tril4, "pmaskb": pm, "constsb": constsb,
        })
    return in_maps


_NC_CACHE = {}


def kernel(**inputs):
    from concourse.bass_utils import run_bass_kernel_spmd
    if "nc" not in _NC_CACHE:
        _, marks = build({})
        nc, _ = build(marks)
        _NC_CACHE["nc"] = nc
    nc = _NC_CACHE["nc"]
    in_maps = _stage_inputs(**inputs)
    res = run_bass_kernel_spmd(nc, in_maps, core_ids=list(range(8)))
    out = np.zeros((2, 4096, 2048), np.float32)
    for c in range(8):
        b, q = c // 4, c % 4
        out[b, 1024 * q:1024 * (q + 1), :] = res.results[c]["y"].T
    return out


# version 18
# speedup vs baseline: 1.0248x; 1.0017x over previous
import contextlib
import numpy as np
import concourse.bass as bass
import concourse.mybir as mybir

F32 = mybir.dt.float32; BF16 = mybir.dt.bfloat16
I32 = mybir.dt.int32; U16 = mybir.dt.uint16; F16 = mybir.dt.float16
AOT = mybir.AluOpType; ACT = mybir.ActivationFunctionType

DM = 2048; HC = 16
T = 1024; NCH = 8; CL = 128
NH = 16; NKV = 4; HD = 128
IF = 5632; NIB = 44
GUSL = 1408; DWSL = 256
TP = 512
GUP = [4, 4, 3]
EPS_LN = 1e-5; EPS_ATT = 1e-6; EPS_RMS = 1e-6
CONSTS = [EPS_LN, EPS_ATT, -0.5, -1.0, 1.0 / 2048.0, 0.5, 2.0, 1.0]

FFN_ORDER = []
for _k in range(3):
    _lo = sum(GUP[:_k])
    for _r in range(4):
        for _j in range(GUP[_k]):
            FFN_ORDER.append((_k, _r, _lo + _j))

PROJ_SLICES = ([("wkT", c, 0) for c in (0, 256)] + [("wvT", c, 1) for c in (0, 256)]
               + [("wqT", c, 2) for c in range(0, 2048, 256)])


class Ctr:
    def __init__(self):
        self.v = 0


class E:
    def __init__(self, eng, sem, dma=False):
        self.e = eng; self.sem = sem; self.ctr = Ctr()
        self.last = None; self.dma = dma

    def w(self, sem, val):
        if val > 0:
            self.e.wait_ge(sem, val)

    def op(self, fn, *args, inc=None, ser=True, **kw):
        if ser and self.last is not None:
            self.e.wait_ge(*self.last)
        inst = fn(*args, **kw)
        step = 16 if self.dma else 1
        if inc is not None:
            sem, ctr = inc
            ctr.v += step
            inst.then_inc(sem, step)
            self.last = (sem, ctr.v)
        else:
            self.ctr.v += step
            inst.then_inc(self.sem, step)
            self.last = (self.sem, self.ctr.v)
        return inst


def build(sched, probe=None):
    marks = {}

    def mk(name, c):
        marks[name] = c.v

    def sv(name):
        return sched.get(name, 0)

    nc = bass.Bass("TRN2", target_bir_lowering=False)
    dram = lambda n, s, d, k="ExternalInput", **kw: nc.dram_tensor(n, s, d, kind=k, **kw).ap()

    xT = dram("xT", [DM, T], F32)
    w_in = {"wqT": dram("wqT", [DM, DM], F32), "wkT": dram("wkT", [DM, 512], F32),
            "wvT": dram("wvT", [DM, 512], F32)}
    woT = dram("woT", [DM, DM], F32)
    gup = dram("gup", [GUSL, 256], I32)
    dwp = dram("dwp", [DWSL, 704], I32)
    gus = dram("gus", [GUSL, 16], F16)
    dws = dram("dws", [DWSL, 44], F16)
    cosb = dram("cosb", [NCH, CL, HD], BF16)
    sinb = dram("sinb", [NCH, CL, HD], BF16)
    identb = dram("identb", [128, 128], BF16)
    tril4b = dram("tril4b", [128, 128], BF16)
    pmaskb = dram("pmaskb", [128, 8], F32)
    constsb = dram("constsb", [128, 8], F32)

    y = dram("y", [DM, T], F32, k="ExternalOutput")
    prb = dram("prb", [128, 16384], F32, k="ExternalOutput") if probe else None

    RG = [[0, 1, 2, 3, 4, 5, 6, 7]]
    ccs_i = dram("ccs_i", [128, 516], F32, k="Internal")
    ccs_o = dram("ccs_o", [128 * 8, 516], F32, k="Internal", addr_space="Shared")
    ccg_i = [dram(f"ccg_i{k}", [DM, GUP[k] * 128], BF16, k="Internal") for k in range(3)]
    ccg_o = [dram(f"ccg_o{k}", [DM * 8, GUP[k] * 128], BF16, k="Internal", addr_space="Shared")
             for k in range(3)]
    ccd_i2 = [dram(f"ccd_i{h}", [2816, DWSL], BF16, k="Internal") for h in range(2)]
    ccd_o2 = [dram(f"ccd_o{h}", [2816 * 8, DWSL], BF16, k="Internal", addr_space="Shared")
              for h in range(2)]

    es = contextlib.ExitStack()
    with es:
        sb = lambda n, s, d: es.enter_context(nc.sbuf_tensor(n, s, d))
        psum = lambda n, s, d: es.enter_context(nc.psum_tensor(n, s, d))
        sem = lambda n: es.enter_context(nc.semaphore(n))

        x_res = sb("x_res", [128, HC * T], F32)
        lnho = sb("lnho", [128, HC * T], BF16)
        RA = sb("RA", [128, 16384], BF16)
        RB = sb("RB", [128, 12288], BF16)
        RC = sb("RC", [128, 8192], BF16)
        dqw = sb("dqw", [128, 1408], U16)
        dt1 = sb("dt1", [128, 512], U16)
        dt2 = sb("dt2", [128, 512], U16)
        dt3 = sb("dt3", [128, 512], U16)
        dt4 = sb("dt4", [128, 512], U16)
        dvals = sb("dvals", [128, 2816], U16)
        dstg = sb("dstg", [128, 2816], BF16)
        dscal = sb("dscal", [128, 44], F16)
        gsc = sb("gsc", [128, 16], F16)
        dscalf = sb("dscalf", [128, 44], F32)
        gscf = sb("gscf", [128, 16], F32)
        dbuf = sb("dbuf", [128, 2048], BF16)
        dbuf_u16 = None
        rtb = sb("rtb", [128, 4096], BF16)
        sq_t = sb("sq_t", [128, 1024], BF16)
        s_loc = None
        s_eff = sb("s_eff", [128, 1024], BF16)
        z_eff = sb("z_eff", [128, 16], BF16)
        z_loc = sb("z_loc", [128, NCH * 8], BF16)
        ident = sb("ident", [128, 128], BF16)
        trilm = sb("trilm", [128, 128], BF16)
        cos_s = sb("cos_s", [128, NCH * HD], BF16)
        sin_s = sb("sin_s", [128, NCH * HD], BF16)
        pmk = sb("pmk", [128, 8], F32)
        ones_c = sb("ones_c", [128, 1], BF16)
        ones_r = sb("ones_r", [1, 128], BF16)
        consts = sb("consts", [128, 8], F32)
        for _i, _val in enumerate(CONSTS):
            nc.const_aps.register(F32, float(_val), consts[:, _i:_i + 1])

        q_tm = RA
        kv_tm = RB
        s_loc = RB
        SLOC0 = 8192
        wbuf = RC
        dwb = RC

        def ffnw_ap(i):
            s = i % 3
            if s < 2:
                return RC[:, 4096 * s:4096 * (s + 1)]
            return RB[:, 8192:12288]

        dbuf_u16 = dbuf[:].bitcast(U16)

        def hact(gI):
            if gI < 32:
                return RA[:, TP * gI:TP * (gI + 1)]
            return RB[:, TP * (gI - 32):TP * (gI - 31)]

        stin = dstg[:].bitcast(F32)
        lnscr = stin[0:1, 0:512]
        dvbf = dvals[:].bitcast(BF16)
        a4m = dvbf[:, 0:1024]
        qf_fm = dvbf[:, 1024:2048]
        kf_fm = dvbf[:, 2048:2304]
        rdbc = dvbf[:, 2304:2816]
        rt = lambda i, n=1024: rtb[:, 1024 * i:1024 * i + n]
        s_glob = rtb[:].bitcast(F32)[:, 0:512]
        z_glob = rtb[:].bitcast(F32)[:, 512:520]
        s_cum = rtb[:].bitcast(F32)[:, 520:1032]
        z_cum = rtb[:].bitcast(F32)[:, 1032:1040]
        stats = rtb[:].bitcast(F32)
        mu_f = stats[0:1, 0:1024]
        var_f = stats[0:1, 1024:2048]
        rs_s = rtb[0:1, 2048:3072]
        mu_s = rtb[0:1, 3072:4096]
        dent = sq_t[:].bitcast(F32)[0:1, 0:512]
        rden = dstg[0:1, 2048:2560]

        pA = psum("pA", [128, 512], F32)
        pB = psum("pB", [128, 512], F32)
        pS = psum("pS", [128, 512], F32)
        pT2 = psum("pT2", [128, 512], F32)
        pZ = psum("pZ", [128, 512], F32)
        pC = psum("pC", [128, 512], F32)
        pE = psum("pE", [128, 512], F32)
        pT = psum("pT", [128, 512], F32)

        sd = sem("sd"); pk0 = sem("pk0"); pk1 = sem("pk1")
        wb0 = sem("wb0"); wb1 = sem("wb1")
        wos = [sem(f"wo{j}") for j in range(8)]
        gd = sem("gd"); st_0 = sem("st_0"); st_1 = sem("st_1")
        gf0 = sem("gf0"); gf1 = sem("gf1"); gf2 = sem("gf2")
        df0 = sem("df0"); df1 = sem("df1")
        ccsem = sem("ccsem"); vsem = sem("vsem"); ssem = sem("ssem"); tsem = sem("tsem")
        v2 = sem("v2"); s2 = sem("s2"); t2 = sem("t2")
        outs = sem("outs"); css = sem("css"); cst = sem("cst"); prd = sem("prd"); gp2 = sem("gp2")

        OUT = Ctr(); CSS = Ctr()
        block = es.enter_context(nc.Block())

        probe_src = {
            "xln1": (lnho, 16384), "qtm": (q_tm, 16384), "kvtm": (kv_tm, 8192),
            "headout": (lnho, 16384), "xmid": (x_res, 16384), "xln2": (lnho, 16384),
            "hact": (RA, 16384), "yb": (lnho, 16384), "sglob": (s_glob, 512),
            "stot": (s_cum, 512), "sloc": (RB, 12288), "deqg": (dstg, 2816),
        }

        @block.sync
        def _(sync):
            D = E(sync, sd, dma=True)
            D.op(sync.dma_start, out=x_res[:].rearrange("p (c t) -> p c t", c=HC),
                 in_=xT.rearrange("(c p) t -> p c t", p=128), ser=False)
            D.op(sync.dma_start, out=ident[:], in_=identb, ser=False)
            D.op(sync.dma_start, out=trilm[:], in_=tril4b, ser=False)
            D.op(sync.dma_start, out=cos_s[:].rearrange("p (c d) -> p c d", c=NCH),
                 in_=cosb.rearrange("c p d -> p c d"), ser=False)
            D.op(sync.dma_start, out=sin_s[:].rearrange("p (c d) -> p c d", c=NCH),
                 in_=sinb.rearrange("c p d -> p c d"), ser=False)
            D.op(sync.dma_start, out=pmk[:], in_=pmaskb, ser=False)
            D.op(sync.dma_start, out=consts[:], in_=constsb, ser=False)
            mk("sd.consts", D.ctr)
            P = [E(sync, pk0, dma=True), E(sync, pk1, dma=True)]
            S = E(sync, css, dma=True)

            def pk_load(t):
                PP = P[t % 2]
                PP.w(v2, sv(f"v.deqfree.{t}"))

                if t < 11:
                    PP.op(sync.dma_start, out=dqw[:, 0:512],
                          in_=gup[128 * t:128 * (t + 1), :].bitcast(U16), ser=False)
                    PP.op(sync.dma_start, out=gsc[:],
                          in_=gus[128 * t:128 * (t + 1), :], ser=False)
                else:
                    tt = t - 11
                    PP.op(sync.dma_start, out=dqw[:, 0:1408],
                          in_=dwp[128 * tt:128 * (tt + 1), :].bitcast(U16), ser=False)
                    PP.op(sync.dma_start, out=dscal[:],
                          in_=dws[128 * tt:128 * (tt + 1), :], ser=False)
                mk(f"sd.pk.{t}", PP.ctr)

            def cc_store(t):
                S.w(v2, sv(f"v.deqstg.{t}"))
                if t < 11:
                    k = min(t // 4, 2)
                    lc = (t - [0, 4, 8][k]) * 128
                    S.op(sync.dma_start,
                         out=ccg_i[k][:, lc:lc + 128].rearrange("(c p) o -> p c o", p=128),
                         in_=dstg[:, 0:2048].rearrange("p (c o) -> p c o", c=HC),
                         inc=(css, CSS), ser=False)
                    mk(f"sd.ccgst.{t}", CSS)
                else:
                    tt = t - 11
                    ot, hf = tt // 2, tt % 2
                    S.op(sync.dma_start,
                         out=ccd_i2[hf][:, 128 * ot:128 * (ot + 1)]
                             .rearrange("(c p) o -> p c o", p=128),
                         in_=dstg[:, 0:2816].rearrange("p (c o) -> p c o", c=22),
                         inc=(css, CSS), ser=False)
                    mk(f"sd.ccdst.{tt}", CSS)

            ST = E(sync, cst, dma=True)
            pk_load(0)
            pk_load(1)
            for t in range(2, 11):
                cc_store(t - 2)
                if t == 3:
                    ST.w(v2, sv("v.stot"))
                    ST.op(sync.dma_start, out=ccs_i[:, 0:512], in_=s_cum[:], ser=False)
                    ST.op(sync.dma_start, out=ccs_i[:, 512:516], in_=z_cum[:, 0:4], ser=False)
                    mk("sd.ccsst", ST.ctr)
                pk_load(t)
            cc_store(9)
            cc_store(10)
            TT = [E(sync, st_0, dma=True), E(sync, st_1, dma=True)]
            sync.wait_ge(css, sv("sd.ccgst.10"))
            sync.wait_ge(ccsem, 1)
            for r in range(8):
                Q = TT[r % 2]
                Q.w(v2, sv(f"v.stfree.{r}"))
                Q.op(sync.dma_start, out=stin[:, 516 * (r % 2):516 * (r % 2) + 516],
                     in_=ccs_o[128 * r:128 * (r + 1), :], ser=False)
                mk(f"sd.stin.{r}", Q.ctr)
            pk_load(11)
            G = [E(sync, gf0, dma=True), E(sync, gf1, dma=True), E(sync, gf2, dma=True)]
            F = [E(sync, df0, dma=True), E(sync, df1, dma=True)]
            for ps in range(2):
                for bi, (k, r, lb) in enumerate(FFN_ORDER):
                    i = ps * NIB + bi
                    GG = G[i % 3]
                    sync.wait_ge(ccsem, 2 + k)
                    if i == 0:
                        GG.w(t2, sv("t.wbfree.13"))
                    if i == 1:
                        GG.w(t2, sv("t.wo.15.1"))
                    if i == 2:
                        GG.w(t2, sv("t.rbc.31"))
                    if ps == 1 and bi in (1, 2):
                        GG.w(t2, sv("t.dn.15"))
                    GG.w(t2, sv(f"t.gffree.{i}"))
                    lc = (lb - [0, 4, 8][k]) * 128
                    fw = ffnw_ap(i)
                    GG.op(sync.dma_start,
                          out=fw[:, 0:DM].rearrange("p (c o) -> p c o", c=HC),
                          in_=ccg_o[k][DM * r:DM * (r + 1), lc:lc + 128]
                              .rearrange("(c p) o -> p c o", p=128), ser=False)
                    GG.op(sync.dma_start,
                          out=fw[:, DM:2 * DM].rearrange("p (c o) -> p c o", c=HC),
                          in_=ccg_o[k][DM * (4 + r):DM * (5 + r), lc:lc + 128]
                              .rearrange("(c p) o -> p c o", p=128), ser=False)
                    mk(f"sd.gf.{i}", GG.ctr)
                    if ps == 0:
                        if bi == 12:
                            pk_load(12)
                        if bi in (12, 19, 26, 33):
                            cc_store(11 + (12, 19, 26, 33).index(bi))
                sync.wait_ge(ccsem, 5)
                for t in range(HC):
                    i = ps * HC + t
                    r, lc = t // 2, (t % 2) * 128
                    for hx in range(2):
                        hidx = 2 * i + hx
                        FF = F[hidx % 2]
                        if t == 0:
                            FF.w(t2, sv(f"t.gu.{ps * NIB + NIB - 1}"))
                        FF.w(t2, sv(f"t.dnh.{hidx - 2}") if hidx >= 2 else 0)
                        FF.w(ccsem, 5 + hx)
                        so = (hidx % 2) * 2816
                        FF.op(sync.dma_start,
                              out=dwb[:, so:so + 2816].rearrange("p (c o) -> p c o", c=22),
                              in_=ccd_o2[hx][2816 * r:2816 * (r + 1), lc:lc + 128]
                                  .rearrange("(c p) o -> p c o", p=128), ser=False)
                        mk(f"sd.df.{hidx}", FF.ctr)
            O = E(sync, outs, dma=True)
            yv = y.rearrange("(c p) t -> p c t", p=128)
            for ps_ in range(2):
                for hc_ in range(HC):
                    O.w(v2, sv(f"v.res.{ps_}.{hc_}"))
                    O.op(sync.dma_start, out=yv[:, hc_, TP * ps_:TP * (ps_ + 1)],
                         in_=x_res[:, T * hc_ + TP * ps_:T * hc_ + TP * (ps_ + 1)],
                         inc=(outs, OUT), ser=False)
            sync.wait_ge(outs, OUT.v)

        @block.gpsimd
        def _(g):
            GD = E(g, gd, dma=True)
            GD.op(g.dma_start, out=lnho[:].rearrange("p (c t) -> p c t", c=HC),
                  in_=xT.rearrange("(c p) t -> p c t", p=128), ser=False)
            mk("gd.xb", GD.ctr)
            W = [E(g, wb0, dma=True), E(g, wb1, dma=True)]
            GP = E(g, gp2)
            g.wait_ge(sd, sv("sd.consts"))
            dprev = [0, []]

            def wb_load(sl):
                wname, c0, kind = PROJ_SLICES[sl]
                WW = W[sl % 2]
                WW.w(t2, sv(f"t.wbfree.{sl}"))
                so = (sl % 2) * HC * 256
                WW.op(g.dma_start, out=wbuf[:, so:so + HC * 256].rearrange("p (c o) -> p c o", c=HC),
                      in_=w_in[wname][:, c0:c0 + 256].rearrange("(c p) o -> p c o", p=128),
                      ser=False)
                mk(f"gd.wb.{sl}", WW.ctr)

            for sl in range(2):
                wb_load(sl)
            wb_load(2)
            wb_load(3)
            wb_load(4)
            wb_load(5)
            g.wait_ge(cst, sv("sd.ccsst"))
            g.collective_compute("AllGather", AOT.bypass, ins=[ccs_i], outs=[ccs_o],
                                 replica_groups=RG).then_inc(ccsem, 1)
            wb_load(6)
            wb_load(7)
            wb_load(8)
            wb_load(9)
            g.wait_ge(css, sv("sd.ccgst.3"))
            g.collective_compute("AllGather", AOT.bypass, ins=[ccg_i[0]], outs=[ccg_o[0]],
                                 replica_groups=RG).then_inc(ccsem, 1)
            wb_load(10)
            wb_load(11)
            g.wait_ge(css, sv("sd.ccgst.7"))
            g.collective_compute("AllGather", AOT.bypass, ins=[ccg_i[1]], outs=[ccg_o[1]],
                                 replica_groups=RG).then_inc(ccsem, 1)
            g.wait_ge(css, sv("sd.ccgst.10"))
            g.collective_compute("AllGather", AOT.bypass, ins=[ccg_i[2]], outs=[ccg_o[2]],
                                 replica_groups=RG).then_inc(ccsem, 1)
            V = [E(g, s_, dma=True) for s_ in wos]
            p1slot = [RC[:, 4096:6144], RC[:, 6144:8192], RA[:, 0:2048], RA[:, 2048:4096]]
            p1wait = ["t.wbfree.13", "t.wbfree.13", "t.scanT.0.3", "t.scanT.1.3"]
            for pw in range(2):
                for t in range(HC):
                    VV = V[4 * pw + t % 4]
                    if pw == 0:
                        VV.w(t2, sv(p1wait[t]) if t < 4 else sv(f"t.wo.{t - 4}.0"))
                        dst = p1slot[t % 4]
                    else:
                        VV.w(t2, sv(f"t.rbc.{8 * t + 7}") if t < 4 else sv(f"t.wo.{t - 4}.1"))
                        dst = RB[:, 2048 * (t % 4):2048 * (t % 4 + 1)]
                    VV.op(g.dma_start, out=dst.rearrange("p (c o) -> p c o", c=HC),
                          in_=woT[:, 128 * t:128 * (t + 1)].rearrange("(c p) o -> p c o", p=128),
                          ser=False)
                    mk(f"gd.wo{pw}.{t}", VV.ctr)
            g.wait_ge(css, sv("sd.ccdst.2"))
            g.collective_compute("AllGather", AOT.bypass, ins=[ccd_i2[0]], outs=[ccd_o2[0]],
                                 replica_groups=RG).then_inc(ccsem, 1)
            g.wait_ge(css, sv("sd.ccdst.3"))
            g.collective_compute("AllGather", AOT.bypass, ins=[ccd_i2[1]], outs=[ccd_o2[1]],
                                 replica_groups=RG).then_inc(ccsem, 1)

        @block.scalar
        def _(s):
            S = E(s, ssem)
            S2 = Ctr()

            def sop(fn, *a, m=None, **kw):
                S.op(fn, *a, inc=(s2, S2) if m else None, **kw)
                if m:
                    mk(m, S2)

            def sprobe(names):
                if probe in names:
                    s.wait_ge(v2, sv("v.probe"))
                    psrc, ncols = probe_src[probe]
                    if psrc.dtype == F32:
                        s.dma_start(out=prb[:, 0:ncols],
                                    in_=psrc[:, 0:ncols]).then_inc(prd, 16)
                    else:
                        s.dma_start(out=prb[:].bitcast(BF16)[:, 0:ncols],
                                    in_=psrc[:, 0:ncols]).then_inc(prd, 16)
                    s.wait_ge(prd, 16)

            s.wait_ge(sd, sv("sd.consts"))
            s.wait_ge(gd, sv("gd.xb"))
            for hc in range(HC):
                if hc:
                    S.w(t2, sv(f"t.sqc0.{hc - 1}"))
                sop(s.activation, sq_t[:, 0:1024], lnho[:, T * hc:T * (hc + 1)], ACT.Square,
                    m=f"s.sq0.{hc}")
            S.w(v2, sv("v.var1"))
            sop(s.activation, dent, var_f[:, 0:512], ACT.Ln, bias=EPS_LN)
            sop(s.activation, rs_s[:, 0:512], dent, ACT.Exp, scale=-0.5)
            sop(s.activation, dent, var_f[:, 512:1024], ACT.Ln, bias=EPS_LN)
            sop(s.activation, rs_s[:, 512:1024], dent, ACT.Exp, scale=-0.5, m="s.rstd1")
            sprobe(("xln1",))
            def pev(sl):
                wname, c0, kind = PROJ_SLICES[sl]
                for ch in range(NCH):
                    S.w(t2, sv(f"t.proj.{sl}.{ch}"))
                    src = (pA, pB, pC, pE)[ch % 4]
                    if kind == 0:
                        dst = kv_tm[:, 1024 * ch + c0:1024 * ch + c0 + 256]
                    elif kind == 1:
                        dst = kv_tm[:, 1024 * ch + 512 + c0:1024 * ch + 512 + c0 + 256]
                    else:
                        dst = q_tm[:, DM * ch + c0:DM * ch + c0 + 256]
                    sop(s.copy, dst, src[:, 0:256], m=f"s.pev.{sl}.{ch}")

            for sl in range(4):
                pev(sl)
            for ch in range(NCH):
                S.w(v2, sv(f"v.ropek.{ch}"))
                sop(s.activation, rt(1, 512), rt(0, 512), ACT.Exp, m=f"s.expk.{ch}")
            sprobe(("kvtm", "deqg"))
            for sl in range(4, 12):
                pev(sl)
            for ch in range(2):
                for hf in range(2):
                    S.w(v2, sv(f"v.ropeq.{ch}.{hf}"))
                    sop(s.activation, rt(3), rt(2), ACT.Exp, m=f"s.expq.{ch}.{hf}")
            sprobe(("qtm", "stot", "sglob"))
            for ch in range(NCH):
                for kv in range(NKV):
                    S.w(t2, sv(f"t.den.{ch}.{kv}"))
                    sop(s.activation, dent, pE[0:1, :], ACT.Ln, bias=EPS_ATT)
                    sop(s.activation, rden, dent, ACT.Exp, scale=-1.0, m=f"s.rden.{ch}.{kv}")
                    if ch + 2 < NCH and kv in (1, 2):
                        S.w(v2, sv(f"v.ropeq.{ch + 2}.{kv - 1}"))
                        sop(s.activation, rt(3), rt(2), ACT.Exp,
                            m=f"s.expq.{ch + 2}.{kv - 1}")
            sprobe(("headout", "sloc", "xmid"))
            for hc in range(HC):
                S.w(v2, sv(f"v.ln2cast.{hc}"))
                if hc:
                    S.w(t2, sv(f"t.sqc2.{hc - 1}"))
                sop(s.activation, sq_t[:, 0:1024], lnho[:, T * hc:T * (hc + 1)], ACT.Square,
                    m=f"s.sq2.{hc}")
            S.w(v2, sv("v.var2"))
            sop(s.activation, dent, var_f[:, 0:512], ACT.Ln, bias=EPS_LN)
            sop(s.activation, rs_s[:, 0:512], dent, ACT.Exp, scale=-0.5)
            sop(s.activation, dent, var_f[:, 512:1024], ACT.Ln, bias=EPS_LN)
            sop(s.activation, rs_s[:, 512:1024], dent, ACT.Exp, scale=-0.5, m="s.rstd2")
            sprobe(("xln2",))
            for ps in range(2):
                for bi in range(NIB):
                    i = ps * NIB + bi
                    S.w(t2, sv(f"t.gu.{i}"))
                    if i:
                        S.w(v2, sv(f"v.hact.{i - 1}"))
                    gb = (pA, pC, pS)[bi % 3]
                    sop(s.activation, sq_t[:, 0:TP], gb[:], ACT.Silu, m=f"s.silu.{i}")
                if ps == 0:
                    sprobe(("hact",))
                for t in range(HC):
                    i2 = ps * HC + t
                    S.w(t2, sv(f"t.dn.{i2}"))
                    if i2:
                        S.w(t2, sv(f"t.rmsstat.{i2 - 1}"))
                    db = (pA, pC)[t % 2]
                    sop(s.copy, lnho[:, T * t + TP * ps:T * t + TP * (ps + 1)], db[:])
                    sop(s.activation, sq_t[:, 0:TP], db[:], ACT.Square, m=f"s.dnev.{i2}")
                S.w(t2, sv(f"t.rmss.{ps}"))
                sop(s.activation, dent, pZ[0:1, :], ACT.Ln, scale=1.0 / DM, bias=EPS_RMS)
                sop(s.activation, rden, dent, ACT.Exp, scale=-0.5, m=f"s.rms.{ps}")
                S.w(t2, sv(f"t.rmsbc.{ps}"))
                if ps == 1:
                    sprobe(("yb",))

        @block.vector
        def _(v):
            V = E(v, vsem)
            V2 = Ctr()

            def vop(fn, *a, m=None, **kw):
                V.op(fn, *a, inc=(v2, V2) if m else None, **kw)
                if m:
                    mk(m, V2)

            def pmark(name):
                if V.last is not None:
                    v.wait_ge(*V.last)
                v.sem_inc(v2, 1)
                V2.v += 1
                mk(name, V2)
                V.last = (v2, V2.v)
                if name == "v.probe":
                    v.wait_ge(prd, 16)

            vop(v.memset, ones_c[:], 1.0)
            vop(v.memset, ones_r[:], 1.0, m="v.ones")

            v.wait_ge(t2, sv("t.ln1stats"))
            vop(v.tensor_scalar, mu_f[:, 0:512], pC[0:1, :], 1.0 / DM, None, AOT.mult)
            vop(v.tensor_scalar, mu_f[:, 512:1024], pE[0:1, :], 1.0 / DM, None, AOT.mult)
            vop(v.tensor_scalar, var_f[:, 0:512], pZ[0:1, :], 1.0 / DM, None, AOT.mult)
            vop(v.tensor_scalar, var_f[:, 512:1024], pS[0:1, :], 1.0 / DM, None, AOT.mult)
            vop(v.tensor_tensor, z_tot[:].bitcast(F32) if False else lnscr,
                mu_f[:, 0:512], mu_f[:, 0:512], AOT.mult)
            vop(v.tensor_tensor, var_f[:, 0:512], var_f[:, 0:512], lnscr,
                AOT.subtract)
            vop(v.tensor_tensor, lnscr, mu_f[:, 512:1024], mu_f[:, 512:1024],
                AOT.mult)
            vop(v.tensor_tensor, var_f[:, 512:1024], var_f[:, 512:1024], lnscr,
                AOT.subtract, m="v.var1")
            V.w(s2, sv("s.rstd1"))
            vop(v.tensor_tensor, mu_s[:], mu_f[:], rs_s[:], AOT.mult, m="v.murstd1")
            v.wait_ge(t2, sv("t.ln1bc"))
            v.wait_ge(sd, sv("sd.consts"))
            for hc in range(HC):
                for half in range(2):
                    rb = (pC, pE)[half]; mb = (pZ, pS)[half]
                    c0 = T * hc + 512 * half
                    vop(v.tensor_tensor, dstg[:, 0:512], x_res[:, c0:c0 + 512], rb[:], AOT.mult)
                    vop(v.tensor_tensor, lnho[:, c0:c0 + 512], dstg[:, 0:512], mb[:],
                        AOT.subtract)
            pmark("v.ln1done")
            if probe == "xln1":
                pmark("v.probe")

            vqbatch = [0]
            def deq_piece(nu16, src_off, scal_ap, ngrp, grp0, dv_off, dvb=None):
                if dvb is None:
                    dvb = dvals
                for j in range(4):
                    w = dqw[:, src_off:src_off + nu16]
                    vop(v.tensor_scalar, dt1[:, 0:nu16], w, 4 * j, 7,
                        AOT.logical_shift_right, AOT.bitwise_and)
                    if 12 - 4 * j >= 0:
                        vop(v.tensor_scalar, dt2[:, 0:nu16], w, 12 - 4 * j, 0x8000,
                            AOT.logical_shift_left, AOT.bitwise_and)
                    else:
                        vop(v.tensor_scalar, dt2[:, 0:nu16], w, 4 * j - 12, 0x8000,
                            AOT.logical_shift_right, AOT.bitwise_and)
                    vop(v.tensor_scalar, dt3[:, 0:nu16], dt1[:, 0:nu16], 64.0, 16128.0,
                        AOT.mult, AOT.add)
                    vop(v.tensor_scalar, dt4[:, 0:nu16], dt3[:, 0:nu16], 2.0, -16256.0,
                        AOT.mult, AOT.add)
                    vop(v.tensor_tensor, dt3[:, 0:nu16], dt3[:, 0:nu16], dt4[:, 0:nu16],
                        AOT.min)
                    vop(v.tensor_single_scalar, dt4[:, 0:nu16], dt1[:, 0:nu16], 1.0, AOT.min)
                    vop(v.tensor_tensor, dt3[:, 0:nu16], dt3[:, 0:nu16], dt4[:, 0:nu16],
                        AOT.mult)
                    vop(v.tensor_tensor, dvb[:, dv_off + j:dv_off + 4 * nu16:4],
                        dt3[:, 0:nu16], dt2[:, 0:nu16], AOT.add)
                dv = dvb[:, dv_off:dv_off + 4 * nu16].bitcast(BF16)
                for gi in range(ngrp):
                    vop(v.tensor_scalar, dv[:, 128 * gi:128 * (gi + 1)],
                        dv[:, 128 * gi:128 * (gi + 1)],
                        scal_ap[:, grp0 + gi:grp0 + gi + 1], None, AOT.mult)

            def deq_gate_tile(t):
                v.wait_ge((pk0, pk1)[t % 2], sv(f"sd.pk.{t}"))
                if t >= 1:
                    v.wait_ge(t2, sv(f"t.deqT.{t - 1}.1"))
                vop(v.tensor_copy, gscf[:], gsc[:])
                deq_piece(512, 0, gscf, 16, 0, 0)
                pmark(f"v.deqfree.{t + 1}")
                pmark(f"v.deqval.{t}")
                if t >= 1:
                    v.wait_ge(css, sv(f"sd.ccgst.{t - 1}"))
                for hcb in range(2):
                    v.wait_ge(t2, sv(f"t.deqT.{t}.{hcb}"))
                    bank = (pT2, pT)[vqbatch[0] % 2][:].bitcast(BF16)
                    vqbatch[0] += 1
                    vop(v.tensor_copy, dstg[:, 1024 * hcb:1024 * (hcb + 1)], bank[:],
                        m=f"v.deqTev.{t}.{hcb}")
                pmark(f"v.deqstg.{t}")

            def deq_down_map(tt, hf):
                t = 11 + tt
                dd = 2 * tt + hf
                if hf == 0:
                    v.wait_ge((pk0, pk1)[t % 2], sv(f"sd.pk.{t}"))
                if dd >= 1:
                    v.wait_ge(t2, sv(f"t.deqTd.{dd - 1}.2"))
                else:
                    v.wait_ge(t2, sv("t.deqT.10.1"))
                if hf == 0:
                    vop(v.tensor_copy, dscalf[:], dscal[:])
                for sub in range(2):
                    deq_piece(352, 704 * hf + 352 * sub, dscalf, 11, 22 * hf + 11 * sub,
                              1408 * sub)
                if hf == 1:
                    pmark(f"v.deqfree.{t + 1}")
                pmark(f"v.deqvald.{dd}")

            def deq_down_evac_b(dd, hcb):
                if hcb == 0:
                    v.wait_ge(css, sv(f"sd.ccdst.{dd - 1}") if dd >= 1 else sv("sd.ccgst.10"))
                nb = (8, 8, 6)[hcb]
                v.wait_ge(t2, sv(f"t.deqTd.{dd}.{hcb}"))
                bank = (pT2, pT)[vqbatch[0] % 2][:].bitcast(BF16)
                vqbatch[0] += 1
                vop(v.tensor_copy, dstg[:, 1024 * hcb:1024 * hcb + 128 * nb],
                    bank[:, 0:128 * nb], m=f"v.deqTevd.{dd}.{hcb}")
                if hcb == 2:
                    pmark(f"v.deqstg.{11 + dd}")

            deq_gate_tile(0)
            deq_gate_tile(1)

            def rope(dst_ap, n, ch, alt=False):
                hv = dst_ap.rearrange("p (h d) -> p h d", h=n)
                lo = hv[:, :, 0:64]; hi = hv[:, :, 64:128]
                cl = cos_s[:, 128 * ch:128 * ch + 64].rearrange("p d -> p () d").broadcast_to([128, n, 64])
                cn = cos_s[:, 128 * ch + 64:128 * (ch + 1)].rearrange("p d -> p () d").broadcast_to([128, n, 64])
                sl_ = sin_s[:, 128 * ch:128 * ch + 64].rearrange("p d -> p () d").broadcast_to([128, n, 64])
                sn = sin_s[:, 128 * ch + 64:128 * (ch + 1)].rearrange("p d -> p () d").broadcast_to([128, n, 64])
                m = n * 64
                b1 = dbuf[:, 0:m] if alt else rt(0, m)
                b2 = dbuf[:, 1024:1024 + m] if alt else rt(1, m)
                w1 = b1.rearrange("p (h d) -> p h d", h=n)
                w2 = b2.rearrange("p (h d) -> p h d", h=n)
                w3 = rt(2, m).rearrange("p (h d) -> p h d", h=n)
                w4 = rt(3, m).rearrange("p (h d) -> p h d", h=n)
                vop(v.tensor_tensor, w1, lo, cl, AOT.mult)
                vop(v.tensor_tensor, w2, hi, sl_, AOT.mult)
                vop(v.tensor_tensor, w3, hi, cn, AOT.mult)
                vop(v.tensor_tensor, w4, lo, sn, AOT.mult)
                vop(v.tensor_tensor, lo, w1, w2, AOT.subtract)
                vop(v.tensor_tensor, hi, w3, w4, AOT.add)

            for ch in range(NCH):
                V.w(s2, sv(f"s.pev.3.{ch}"))
                kap = kv_tm[:, 1024 * ch:1024 * ch + 512]
                rope(kap, 4, ch)
                vop(v.tensor_single_scalar, rt(0, 512), kap, 0.0, AOT.min, m=f"v.ropek.{ch}")
                V.w(s2, sv(f"s.expk.{ch}"))
                vop(v.scalar_tensor_tensor, kap, kap, 0.0, rt(1, 512), AOT.max, AOT.add,
                    m=f"v.eluk.{ch}")
            if probe == "kvtm":
                pmark("v.probe")

            vop(v.memset, s_cum[:], 0.0)
            vop(v.memset, z_cum[:], 0.0)
            for ch in range(NCH):
                v.wait_ge(t2, sv(f"t.stacc.{ch}"))
                vop(v.tensor_tensor, s_cum[:], s_cum[:], pS[:], AOT.add)
                vop(v.tensor_copy, s_loc[:, SLOC0 + 512 * ch:SLOC0 + 512 * (ch + 1)], s_cum[:])
                vop(v.tensor_tensor, z_cum[:, 0:4], z_cum[:, 0:4], pZ[:, 0:4], AOT.add)
                vop(v.tensor_copy, z_loc[:, 8 * ch:8 * ch + 4], z_cum[:, 0:4],
                    m=f"v.stev.{ch}")
            pmark("v.stot")

            for t in range(2, 11):
                deq_gate_tile(t)
            if probe == "deqg":
                pmark("v.probe")

            def q_rope(ch, alt):
                V.w(s2, sv(f"s.pev.11.{ch}"))
                rope(q_tm[:, DM * ch:DM * (ch + 1)], 16, ch, alt=alt)

            def q_elu(ch, hf):
                qh = q_tm[:, DM * ch + 1024 * hf:DM * ch + 1024 * (hf + 1)]
                vop(v.tensor_single_scalar, rt(2), qh, 0.0, AOT.min,
                    m=f"v.ropeq.{ch}.{hf}")
                V.w(s2, sv(f"s.expq.{ch}.{hf}"))
                vop(v.scalar_tensor_tensor, qh, qh, 0.0, rt(3), AOT.max, AOT.add,
                    m=f"v.eluq.{ch}.{hf}")

            v.wait_ge(cst, 32)
            for ch in range(2):
                q_rope(ch, False)
                q_elu(ch, 0)
                q_elu(ch, 1)
            if probe == "qtm":
                pmark("v.probe")

            vop(v.memset, s_glob[:], 0.0)
            vop(v.memset, z_glob[:], 0.0)
            for r in range(8):
                v.wait_ge((st_0, st_1)[r % 2], sv(f"sd.stin.{r}"))
                so = 516 * (r % 2)
                vop(v.scalar_tensor_tensor, s_glob[:], stin[:, so:so + 512],
                    pmk[:, r:r + 1], s_glob[:], AOT.mult, AOT.add)
                vop(v.scalar_tensor_tensor, z_glob[:, 0:4], stin[:, so + 512:so + 516],
                    pmk[:, r:r + 1], z_glob[:, 0:4], AOT.mult, AOT.add,
                    m=f"v.stfree.{r + 2}")
            pmark("v.sglob")
            if probe == "sglob":
                pmark("v.probe")
            if probe == "stot":
                pmark("v.probe")

            def v_seff(ch):
                se = s_eff[:, 512 * (ch % 2):512 * (ch % 2 + 1)]
                ze = z_eff[:, 8 * (ch % 2):8 * (ch % 2) + 4]
                if ch == 0:
                    vop(v.tensor_scalar, se, s_glob[:], 1.0, None, AOT.mult)
                    vop(v.tensor_scalar, ze, z_glob[:, 0:4], 1.0, None, AOT.mult,
                        m=f"v.seff.{ch}")
                else:
                    vop(v.tensor_tensor, se, s_glob[:],
                        s_loc[:, SLOC0 + 512 * (ch - 1):SLOC0 + 512 * ch], AOT.add)
                    vop(v.tensor_tensor, ze, z_glob[:, 0:4],
                        z_loc[:, 8 * (ch - 1):8 * (ch - 1) + 4], AOT.add, m=f"v.seff.{ch}")

            def v_qfkf_a4m(idx):
                ch, kv = idx // NKV, idx % NKV
                par = idx % 2
                if kv == 0:
                    v_seff(ch)
                pTx = (pT, pT2)[par][:].bitcast(BF16)
                v.wait_ge(t2, sv(f"t.scanT.{ch}.{kv}"))
                for j in range(4):
                    vop(v.tensor_copy,
                        qf_fm[:, 512 * par + 128 * j:512 * par + 128 * (j + 1)],
                        pTx[:, 128 * j:128 * (j + 1)])
                vop(v.tensor_copy, kf_fm[:, 128 * par:128 * (par + 1)], pTx[:, 512:640],
                    m=f"v.qfkf.{ch}.{kv}")
                v.wait_ge(t2, sv(f"t.a4.{ch}.{kv}"))
                vop(v.tensor_tensor,
                    a4m[:, 512 * par:512 * (par + 1)].rearrange("p (h t) -> p h t", h=4),
                    pA[:].rearrange("p (h t) -> p h t", h=4),
                    trilm[:].rearrange("p t -> p () t").broadcast_to([128, 4, 128]),
                    AOT.mult, m=f"v.a4m.{ch}.{kv}")

            def v_odiv(idx):
                ch, kv = idx // NKV, idx % NKV
                v.wait_ge(t2, sv(f"t.rbc.{idx}"))
                vop(v.tensor_copy, rdbc, pC[:])
                vop(v.tensor_tensor,
                    lnho[:].rearrange("p (h t) -> p h t", h=HC)[:, 4 * kv:4 * (kv + 1),
                                                                128 * ch:128 * (ch + 1)],
                    pB[:].rearrange("p (h t) -> p h t", h=4),
                    rdbc.rearrange("p (h t) -> p h t", h=4),
                    AOT.mult, m=f"v.odiv.{ch}.{kv}")

            def woev_h0(ot):
                v.wait_ge(t2, sv(f"t.wo.{ot}.0"))
                src = (pS, pZ)[ot % 2]
                c0 = T * ot
                vop(v.tensor_tensor, x_res[:, c0:c0 + 512], x_res[:, c0:c0 + 512],
                    src[:], AOT.add, m=f"v.woevh0.{ot}")

            v_qfkf_a4m(0)
            for idx in range(NCH * NKV):
                if idx + 1 < NCH * NKV:
                    v_qfkf_a4m(idx + 1)
                v_odiv(idx)
                if idx % 4 == 0 and 2 + idx // 4 < NCH:
                    q_rope(2 + idx // 4, True)
                if idx % 4 == 1 and 2 + idx // 4 < NCH:
                    q_elu(2 + idx // 4, 0)
                if idx % 4 == 2 and 2 + idx // 4 < NCH:
                    q_elu(2 + idx // 4, 1)
                if idx >= 17:
                    woev_h0(idx - 17)
            woev_h0(15)
            pmark("v.scandone")
            if probe == "headout":
                pmark("v.probe")
            if probe == "sloc":
                pmark("v.probe")

            for ot in range(HC):
                v.wait_ge(t2, sv(f"t.wo.{ot}.1"))
                src = (pA, pB)[ot % 2]
                c0 = T * ot + 512
                vop(v.tensor_tensor, x_res[:, c0:c0 + 512], x_res[:, c0:c0 + 512],
                    src[:], AOT.add, m=f"v.woev.{ot}.1")
            if probe == "xmid":
                pmark("v.probe")

            for hc in range(HC):
                vop(v.tensor_copy, lnho[:, T * hc:T * (hc + 1)], x_res[:, T * hc:T * (hc + 1)],
                    m=f"v.ln2cast.{hc}")
            v.wait_ge(t2, sv("t.ln2stats"))
            v.wait_ge(css, sv("sd.ccgst.10"))
            vop(v.tensor_scalar, mu_f[:, 0:512], pC[0:1, :], 1.0 / DM, None, AOT.mult)
            vop(v.tensor_scalar, mu_f[:, 512:1024], pE[0:1, :], 1.0 / DM, None, AOT.mult)
            vop(v.tensor_scalar, var_f[:, 0:512], pZ[0:1, :], 1.0 / DM, None, AOT.mult)
            vop(v.tensor_scalar, var_f[:, 512:1024], pS[0:1, :], 1.0 / DM, None, AOT.mult)
            vop(v.tensor_tensor, lnscr, mu_f[:, 0:512], mu_f[:, 0:512], AOT.mult)
            vop(v.tensor_tensor, var_f[:, 0:512], var_f[:, 0:512], lnscr,
                AOT.subtract)
            vop(v.tensor_tensor, lnscr, mu_f[:, 512:1024], mu_f[:, 512:1024],
                AOT.mult)
            vop(v.tensor_tensor, var_f[:, 512:1024], var_f[:, 512:1024], lnscr,
                AOT.subtract, m="v.var2")
            V.w(s2, sv("s.rstd2"))
            vop(v.tensor_tensor, mu_s[:], mu_f[:], rs_s[:], AOT.mult, m="v.murstd2")
            v.wait_ge(t2, sv("t.ln2bc"))
            for half in range(2):
                for hc in range(HC):
                    rb = (pC, pZ)[half]; mb = (pE, pS)[half]
                    c0 = T * hc + 512 * half
                    vop(v.tensor_tensor, dstg[:, 0:512], x_res[:, c0:c0 + 512], rb[:], AOT.mult)
                    vop(v.tensor_tensor, lnho[:, c0:c0 + 512], dstg[:, 0:512], mb[:],
                        AOT.subtract, m=f"v.ln2a.{hc}" if half == 0 else None)
                if half == 0:
                    pmark("v.ln2done0")
            pmark("v.ln2done")
            if probe == "xln2":
                pmark("v.probe")

            for ps in range(2):
                for bi, (k, r, lb) in enumerate(FFN_ORDER):
                    i = ps * NIB + bi
                    gI = 11 * r + lb
                    V.w(s2, sv(f"s.silu.{i}"))
                    ub = (pB, pE, pZ)[bi % 3]
                    vop(v.tensor_tensor, hact(gI), sq_t[:, 0:TP], ub[:], AOT.mult,
                        m=f"v.hact.{i}")
                    if ps == 0:
                        if i == 2:
                            deq_down_map(0, 0)
                        if i in (7, 14, 21, 28):
                            dd = (7, 14, 21, 28).index(i)
                            deq_down_evac_b(dd, 0)
                            deq_down_evac_b(dd, 1)
                        if i in (9, 16, 23, 30):
                            dd = (9, 16, 23, 30).index(i)
                            deq_down_evac_b(dd, 2)
                            if dd < 3:
                                deq_down_map((dd + 1) // 2, (dd + 1) % 2)
                if probe == "hact" and ps == 0:
                    pmark("v.probe")
                v.wait_ge(t2, sv(f"t.rmsbc.{ps}"))
                for hc in range(HC):
                    c0 = T * hc + TP * ps
                    vop(v.tensor_tensor, dstg[:, 0:512], lnho[:, c0:c0 + 512], pS[:], AOT.mult)
                    vop(v.tensor_tensor, x_res[:, c0:c0 + 512], x_res[:, c0:c0 + 512],
                        dstg[:, 0:512], AOT.add, m=f"v.res.{ps}.{hc}")
                if probe == "yb" and ps == 1:
                    pmark("v.probe")
            pmark("v.final")

        @block.tensor
        def _(t):
            TE = E(t, tsem)
            T2 = Ctr()

            def top(fn, *a, m=None, **kw):
                TE.op(fn, *a, inc=(t2, T2) if m else None, ser=False, **kw)
                if m:
                    mk(m, T2)

            t.wait_ge(gd, sv("gd.xb"))
            t.wait_ge(sd, sv("sd.consts"))
            t.wait_ge(v2, sv("v.ones"))
            for hc in range(HC):
                t.wait_ge(s2, sv(f"s.sq0.{hc}"))
                st, sp = (hc == 0), (hc == HC - 1)
                top(t.matmul, pC[0:1, :], ones_c[:], lnho[:, T * hc:T * hc + 512],
                    start=st, stop=sp)
                top(t.matmul, pE[0:1, :], ones_c[:], lnho[:, T * hc + 512:T * (hc + 1)],
                    start=st, stop=sp)
                top(t.matmul, pZ[0:1, :], ones_c[:], sq_t[:, 0:512], start=st, stop=sp)
                top(t.matmul, pS[0:1, :], ones_c[:], sq_t[:, 512:1024], start=st, stop=sp,
                    m=f"t.sqc0.{hc}")
            mk("t.ln1stats", T2)
            t.wait_ge(v2, sv("v.murstd1"))
            top(t.matmul, pC[:], ones_r[:], rs_s[:, 0:512], start=True, stop=True)
            top(t.matmul, pE[:], ones_r[:], rs_s[:, 512:1024], start=True, stop=True)
            top(t.matmul, pZ[:], ones_r[:], mu_s[:, 0:512], start=True, stop=True)
            top(t.matmul, pS[:], ones_r[:], mu_s[:, 512:1024], start=True, stop=True,
                m="t.ln1bc")

            dqbatch = [0]
            dqevs = []

            def deq_T(ti):
                t.wait_ge(v2, sv(f"v.deqval.{ti}"))
                tsrc = dvals
                for hcb in range(2):
                    bank = (pT2, pT)[dqbatch[0] % 2][:].bitcast(BF16)
                    if dqbatch[0] >= 2:
                        t.wait_ge(v2, sv(dqevs[dqbatch[0] - 2]))
                    dqbatch[0] += 1
                    for qq in range(8):
                        hcx = 8 * hcb + qq
                        top(t.transpose, bank[:, 128 * qq:128 * (qq + 1)],
                            tsrc[:, 128 * hcx:128 * (hcx + 1)].bitcast(BF16), ident[:],
                            m=f"t.deqT.{ti}.{hcb}" if qq == 7 else None)
                    dqevs.append(f"v.deqTev.{ti}.{hcb}")

            def deq_Td_b(dd, hcb):
                if hcb == 0:
                    t.wait_ge(v2, sv(f"v.deqvald.{dd}"))
                nb = (8, 8, 6)[hcb]
                bank = (pT2, pT)[dqbatch[0] % 2][:].bitcast(BF16)
                if dqbatch[0] >= 2:
                    t.wait_ge(v2, sv(dqevs[dqbatch[0] - 2]))
                dqbatch[0] += 1
                for qq in range(nb):
                    icx = 8 * hcb + qq
                    top(t.transpose, bank[:, 128 * qq:128 * (qq + 1)],
                        dvals[:, 128 * icx:128 * (icx + 1)].bitcast(BF16), ident[:],
                        m=f"t.deqTd.{dd}.{hcb}" if qq == nb - 1 else None)
                dqevs.append(f"v.deqTevd.{dd}.{hcb}")

            def proj_slice(sl):
                t.wait_ge((wb0, wb1)[sl % 2], sv(f"gd.wb.{sl}"))
                t.wait_ge(v2, sv("v.ln1done"))
                so = (sl % 2) * HC * 256
                for ch in range(NCH):
                    idx = sl * NCH + ch
                    if idx >= 4:
                        pr = idx - 4
                        t.wait_ge(s2, sv(f"s.pev.{pr // NCH}.{pr % NCH}"))
                    dstp = (pA, pB, pC, pE)[ch % 4]
                    for hc in range(HC):
                        top(t.matmul, dstp[:, 0:256],
                            lnho[:, T * hc + 128 * ch:T * hc + 128 * (ch + 1)],
                            wbuf[:, so + 256 * hc:so + 256 * (hc + 1)],
                            start=(hc == 0), stop=(hc == HC - 1),
                            m=f"t.proj.{sl}.{ch}" if hc == HC - 1 else None)
                mk(f"t.wbfree.{sl + 2}", T2)

            for sl in range(4):
                proj_slice(sl)
            deq_T(0)
            deq_T(1)
            for ch in range(NCH):
                t.wait_ge(v2, sv(f"v.eluk.{ch}"))
                if ch > 0:
                    t.wait_ge(v2, sv(f"v.stev.{ch - 1}"))
                for kv in range(NKV):
                    st = (kv == 0)
                    sp = (kv == NKV - 1)
                    top(t.matmul, pS[:, 128 * kv:128 * (kv + 1)],
                        kv_tm[:, 1024 * ch + 128 * kv:1024 * ch + 128 * (kv + 1)],
                        kv_tm[:, 1024 * ch + 512 + 128 * kv:1024 * ch + 512 + 128 * (kv + 1)],
                        start=st, stop=(kv == NKV - 1))
                    top(t.matmul, pZ[:, kv:kv + 1],
                        kv_tm[:, 1024 * ch + 128 * kv:1024 * ch + 128 * (kv + 1)],
                        ones_c[:], start=st, stop=sp,
                        m=f"t.stacc.{ch}" if kv == NKV - 1 else None)
            for sl in range(4, 12):
                proj_slice(sl)
                if sl - 2 <= 10:
                    deq_T(sl - 2)
            deq_T(10)

            def scan_T_A4(idx):
                ch, kv = idx // NKV, idx % NKV
                par = idx % 2
                t.wait_ge(v2, sv(f"v.eluq.{ch}.{kv // 2}"))
                if idx >= 2:
                    t.wait_ge(v2, sv(f"v.qfkf.{(idx - 2) // NKV}.{(idx - 2) % NKV}"))
                pTx = (pT, pT2)[par][:].bitcast(BF16)
                for j in range(4):
                    h = 4 * kv + j
                    top(t.transpose, pTx[:, 128 * j:128 * (j + 1)],
                        q_tm[:, DM * ch + 128 * h:DM * ch + 128 * (h + 1)], ident[:])
                top(t.transpose, pTx[:, 512:640],
                    kv_tm[:, 1024 * ch + 128 * kv:1024 * ch + 128 * (kv + 1)], ident[:],
                    m=f"t.scanT.{ch}.{kv}")
                t.wait_ge(v2, sv(f"v.qfkf.{ch}.{kv}"))
                if idx >= 1:
                    t.wait_ge(v2, sv(f"v.a4m.{(idx - 1) // NKV}.{(idx - 1) % NKV}"))
                top(t.matmul, pA[:], kf_fm[:, 128 * par:128 * (par + 1)],
                    qf_fm[:, 512 * par:512 * (par + 1)], start=True, stop=True,
                    m=f"t.a4.{ch}.{kv}")

            def scan_rest(idx):
                ch, kv = idx // NKV, idx % NKV
                par = idx % 2
                t.wait_ge(v2, sv(f"v.a4m.{ch}.{kv}"))
                t.wait_ge(v2, sv(f"v.seff.{ch}"))
                if idx >= 1:
                    t.wait_ge(s2, sv(f"s.rden.{(idx - 1) // NKV}.{(idx - 1) % NKV}"))
                top(t.matmul, pE[0:1, :], ones_c[:], a4m[:, 512 * par:512 * (par + 1)],
                    start=True, stop=False)
                top(t.matmul, pE[0:1, :], z_eff[:, 8 * (ch % 2) + kv:8 * (ch % 2) + kv + 1],
                    qf_fm[:, 512 * par:512 * (par + 1)], start=False, stop=True,
                    m=f"t.den.{ch}.{kv}")
                if idx >= 1:
                    t.wait_ge(v2, sv(f"v.odiv.{(idx - 1) // NKV}.{(idx - 1) % NKV}"))
                top(t.matmul, pB[:],
                    kv_tm[:, 1024 * ch + 512 + 128 * kv:1024 * ch + 512 + 128 * (kv + 1)],
                    a4m[:, 512 * par:512 * (par + 1)], start=True, stop=False)
                top(t.matmul, pB[:], s_eff[:, 512 * (ch % 2) + 128 * kv:512 * (ch % 2) + 128 * (kv + 1)],
                    qf_fm[:, 512 * par:512 * (par + 1)], start=False, stop=True)
                t.wait_ge(s2, sv(f"s.rden.{ch}.{kv}"))
                if idx >= 1:
                    t.wait_ge(v2, sv(f"v.odiv.{(idx - 1) // NKV}.{(idx - 1) % NKV}"))
                top(t.matmul, pC[:], ones_r[:], rden, start=True, stop=True,
                    m=f"t.rbc.{idx}")

            wo_p1slot = [RC[:, 4096:6144], RC[:, 6144:8192], RA[:, 0:2048], RA[:, 2048:4096]]

            def wo_half(ot, half):
                t.wait_ge(wos[4 * half + ot % 4], sv(f"gd.wo{half}.{ot}"))
                if half == 0:
                    if ot == 0:
                        t.wait_ge(v2, sv("v.odiv.3.3"))
                    if ot >= 2:
                        t.wait_ge(v2, sv(f"v.woevh0.{ot - 2}"))
                    dstp = (pS, pZ)[ot % 2]
                    wsl = wo_p1slot[ot % 4]
                else:
                    if ot == 0:
                        t.wait_ge(v2, sv(f"v.odiv.{NCH - 1}.{NKV - 1}"))
                    if ot >= 2:
                        t.wait_ge(v2, sv(f"v.woev.{ot - 2}.1"))
                    dstp = (pA, pB)[ot % 2]
                    wsl = RB[:, 2048 * (ot % 4):2048 * (ot % 4 + 1)]
                for hp in range(HC):
                    top(t.matmul, dstp[:],
                        wsl[:, 128 * hp:128 * (hp + 1)],
                        lnho[:, T * hp + 512 * half:T * hp + 512 * (half + 1)],
                        start=(hp == 0), stop=(hp == HC - 1),
                        m=f"t.wo.{ot}.{half}" if hp == HC - 1 else None)

            scan_T_A4(0)
            for idx in range(NCH * NKV):
                if idx + 1 < NCH * NKV:
                    scan_T_A4(idx + 1)
                scan_rest(idx)
                if idx >= 16:
                    wo_half(idx - 16, 0)

            for ot in range(HC):
                wo_half(ot, 1)
            for hc in range(HC):
                t.wait_ge(s2, sv(f"s.sq2.{hc}"))
                st, sp = (hc == 0), (hc == HC - 1)
                top(t.matmul, pC[0:1, :], ones_c[:], lnho[:, T * hc:T * hc + 512],
                    start=st, stop=sp)
                top(t.matmul, pE[0:1, :], ones_c[:], lnho[:, T * hc + 512:T * (hc + 1)],
                    start=st, stop=sp)
                top(t.matmul, pZ[0:1, :], ones_c[:], sq_t[:, 0:512], start=st, stop=sp)
                top(t.matmul, pS[0:1, :], ones_c[:], sq_t[:, 512:1024], start=st, stop=sp,
                    m=f"t.sqc2.{hc}")
            mk("t.ln2stats", T2)
            t.wait_ge(v2, sv("v.murstd2"))
            top(t.matmul, pC[:], ones_r[:], rs_s[:, 0:512], start=True, stop=True)
            top(t.matmul, pZ[:], ones_r[:], rs_s[:, 512:1024], start=True, stop=True)
            top(t.matmul, pE[:], ones_r[:], mu_s[:, 0:512], start=True, stop=True)
            top(t.matmul, pS[:], ones_r[:], mu_s[:, 512:1024], start=True, stop=True,
                m="t.ln2bc")

            for ps in range(2):
                if ps == 1:
                    t.wait_ge(v2, sv("v.ln2done"))
                for bi, (k, r, lb) in enumerate(FFN_ORDER):
                    i = ps * NIB + bi
                    t.wait_ge((gf0, gf1, gf2)[i % 3], sv(f"sd.gf.{i}"))
                    if i >= 3:
                        t.wait_ge(v2, sv(f"v.hact.{i - 3}"))
                    if ps == 0 and bi == 2:
                        t.wait_ge(v2, sv("v.ln2done"))
                    if ps == 1 and bi == 2:
                        t.wait_ge(v2, sv("v.res.0.15"))
                    gb = (pA, pC, pS)[bi % 3]; ub = (pB, pE, pZ)[bi % 3]
                    if ps == 0 and bi == 1:
                        t.wait_ge(v2, sv("v.ln2done0"))
                    fw = ffnw_ap(i)
                    for hc in range(HC):
                        st, sp = (hc == 0), (hc == HC - 1)
                        if ps == 0 and bi == 0:
                            t.wait_ge(v2, sv(f"v.ln2a.{hc}"))
                        top(t.matmul, gb[:], fw[:, 128 * hc:128 * (hc + 1)],
                            lnho[:, T * hc + TP * ps:T * hc + TP * (ps + 1)],
                            start=st, stop=sp)
                        top(t.matmul, ub[:],
                            fw[:, DM + 128 * hc:DM + 128 * (hc + 1)],
                            lnho[:, T * hc + TP * ps:T * hc + TP * (ps + 1)],
                            start=st, stop=sp, m=f"t.gu.{i}" if sp else None)
                    mk(f"t.gffree.{i + 3}", T2)
                    if ps == 0:
                        if bi in (6, 13, 20, 27):
                            dd = (6, 13, 20, 27).index(bi)
                            deq_Td_b(dd, 0)
                            deq_Td_b(dd, 1)
                        if bi in (8, 15, 22, 29):
                            deq_Td_b((8, 15, 22, 29).index(bi), 2)
                for tt_ in range(HC):
                    i2 = ps * HC + tt_
                    t.wait_ge(v2, sv(f"v.hact.{ps * NIB + NIB - 1}"))
                    if i2 >= 2:
                        t.wait_ge(s2, sv(f"s.dnev.{i2 - 2}"))
                    db = (pA, pC)[tt_ % 2]
                    for hx in range(2):
                        hidx = 2 * i2 + hx
                        t.wait_ge((df0, df1)[hidx % 2], sv(f"sd.df.{hidx}"))
                        so = (hidx % 2) * 2816
                        for icl in range(22):
                            ic = 22 * hx + icl
                            top(t.matmul, db[:], dwb[:, so + 128 * icl:so + 128 * (icl + 1)],
                                hact(ic), start=(ic == 0), stop=(ic == NIB - 1),
                                m=f"t.dnh.{hidx}" if icl == 21 else None)
                    mk(f"t.dn.{i2}", T2)
                    t.wait_ge(s2, sv(f"s.dnev.{i2}"))
                    top(t.matmul, pZ[0:1, :], ones_c[:], sq_t[:, 0:TP],
                        start=(tt_ == 0), stop=(tt_ == HC - 1), m=f"t.rmsstat.{i2}")
                mk(f"t.rmss.{ps}", T2)
                t.wait_ge(s2, sv(f"s.rms.{ps}"))
                top(t.matmul, pS[:], ones_r[:], rden, start=True, stop=True,
                    m=f"t.rmsbc.{ps}")

    return nc, marks


def build2(probe=None):
    _, marks = build({}, probe=probe)
    nc, _ = build(marks, probe=probe)
    return nc, marks


def _stage_inputs(L0_attn_x, L0_attn_Wq, L0_attn_Wk, L0_attn_Wv, L0_attn_Wo,
                  L0_ffn_gate_up_packed, L0_ffn_gate_up_scales,
                  L0_ffn_down_packed, L0_ffn_down_scales):
    import ml_dtypes
    bf = ml_dtypes.bfloat16
    x = np.ascontiguousarray(np.asarray(L0_attn_x, np.float32))
    wqT = np.ascontiguousarray(np.asarray(L0_attn_Wq, np.float32).T)
    wkT = np.ascontiguousarray(np.asarray(L0_attn_Wk, np.float32).T)
    wvT = np.ascontiguousarray(np.asarray(L0_attn_Wv, np.float32).T)
    woT = np.ascontiguousarray(np.asarray(L0_attn_Wo, np.float32).T)
    gupk = np.asarray(L0_ffn_gate_up_packed, np.int32)
    gusT = np.ascontiguousarray(np.asarray(L0_ffn_gate_up_scales, np.float16).T)
    dwpk = np.asarray(L0_ffn_down_packed, np.int32)
    dwsT = np.ascontiguousarray(np.asarray(L0_ffn_down_scales, np.float16).T)
    identb = np.eye(128).astype(bf)
    tril = np.triu(np.ones((128, 128), np.float32))
    tril4 = np.ascontiguousarray(tril).astype(bf)
    constsb = np.tile(np.array([1e-5, 1e-6, -0.5, -1.0, 1.0 / 2048.0, 0.5, 2.0, 1.0],
                               np.float32), (128, 1))
    in_maps = []
    for c in range(8):
        b, q = c // 4, c % 4
        xT = np.ascontiguousarray(x[b, 1024 * q:1024 * (q + 1), :].T)
        pos = 1024 * q + np.arange(1024, dtype=np.float64)
        invf = 10000.0 ** (-np.arange(0, 128, 2, dtype=np.float64) / 128.0)
        fr = np.outer(pos, invf)
        emb = np.concatenate([fr, fr], axis=1)
        cosb = np.cos(emb).astype(np.float32).astype(bf).reshape(8, 128, 128)
        sinb = np.sin(emb).astype(np.float32).astype(bf).reshape(8, 128, 128)
        pm = np.zeros((128, 8), np.float32)
        for r in range(8):
            if r // 4 == b and r % 4 < q:
                pm[:, r] = 1.0
        in_maps.append({
            "xT": xT, "wqT": wqT, "wkT": wkT, "wvT": wvT, "woT": woT,
            "gup": np.ascontiguousarray(gupk[1408 * c:1408 * (c + 1)]),
            "gus": np.ascontiguousarray(gusT[1408 * c:1408 * (c + 1)]),
            "dwp": np.ascontiguousarray(dwpk[256 * c:256 * (c + 1)]),
            "dws": np.ascontiguousarray(dwsT[256 * c:256 * (c + 1)]),
            "cosb": np.ascontiguousarray(cosb), "sinb": np.ascontiguousarray(sinb),
            "identb": identb, "tril4b": tril4, "pmaskb": pm, "constsb": constsb,
        })
    return in_maps


_NC_CACHE = {}


def kernel(**inputs):
    from concourse.bass_utils import run_bass_kernel_spmd
    if "nc" not in _NC_CACHE:
        _, marks = build({})
        nc, _ = build(marks)
        _NC_CACHE["nc"] = nc
    nc = _NC_CACHE["nc"]
    in_maps = _stage_inputs(**inputs)
    res = run_bass_kernel_spmd(nc, in_maps, core_ids=list(range(8)))
    out = np.zeros((2, 4096, 2048), np.float32)
    for c in range(8):
        b, q = c // 4, c % 4
        out[b, 1024 * q:1024 * (q + 1), :] = res.results[c]["y"].T
    return out
